# version 03
# baseline (speedup 1.0000x reference)
import contextlib
import numpy as np
import ml_dtypes
import concourse.bass as bass
import concourse.mybir as mybir
from concourse.bass import AP
from concourse.bass_utils import run_bass_kernel_spmd

F32 = mybir.dt.float32
BF16 = mybir.dt.bfloat16
AF = mybir.ActivationFunctionType
ALU = mybir.AluOpType

B, S, D = 4, 4096, 1024
N = 4096
HN = 2048
PE_W = 6656
BDW = 2304

bf16 = ml_dtypes.bfloat16


def _bf(a):
    return np.ascontiguousarray(np.asarray(a, dtype=np.float32).astype(bf16))


def _f32(a):
    return np.ascontiguousarray(np.asarray(a), dtype=np.float32)


def _cols(v):
    return np.ascontiguousarray(np.asarray(v, np.float32).reshape(-1, 128).T)


def prep_in_maps(inp):
    sh = {}
    sh["l0_WqT"] = _bf(np.asarray(inp["L0_attn_Wq"]).T)
    sh["l0_WkT"] = _bf(np.asarray(inp["L0_attn_Wk"]).T)
    sh["l0_WvT"] = _bf(np.asarray(inp["L0_attn_Wv"]).T)
    sh["l0_WoT"] = _bf(np.asarray(inp["L0_attn_Wo"]).T)
    sh["l0_bq"] = _cols(inp["L0_attn_bq"])
    sh["l0_bk"] = _cols(inp["L0_attn_bk"])
    sh["l0_bv"] = _cols(inp["L0_attn_bv"])
    sh["l0_bo"] = _cols(inp["L0_attn_bo"])
    F = 1408
    wg = np.asarray(inp["L0_ffn_w_gate"]); wu = np.asarray(inp["L0_ffn_w_up"])
    wd = np.asarray(inp["L0_ffn_w_down"])
    swg = np.asarray(inp["L0_ffn_sw_gate"]); swu = np.asarray(inp["L0_ffn_sw_up"])
    swd = np.asarray(inp["L0_ffn_sw_down"])
    wg_all = np.zeros((10, D, F), np.float32)
    wu_all = np.zeros((10, D, F), np.float32)
    wd_all = np.zeros((10, F, D), np.float32)
    wg_all[:8] = wg; wu_all[:8] = wu; wd_all[:8] = wd
    wg_all[8] = swg[:, :F]; wu_all[8] = swu[:, :F]; wd_all[8] = swd[:F]
    wg_all[9] = swg[:, F:]; wu_all[9] = swu[:, F:]; wd_all[9] = swd[F:]
    sh["l0_wg"] = _bf(wg_all); sh["l0_wu"] = _bf(wu_all); sh["l0_wd"] = _bf(wd_all)
    sh["l0_gwT"] = _f32(np.asarray(inp["L0_ffn_gate_w"]).T)
    cw0 = np.zeros((10, N), np.float32); cw0[8:] = 1.0
    sh["cw0_init"] = cw0
    sh["l1a_WqT"] = _bf(np.asarray(inp["L1_attn_Wq"]).T)
    sh["l1a_WkT"] = _bf(np.asarray(inp["L1_attn_Wk"]).T)
    sh["l1a_WvT"] = _bf(np.asarray(inp["L1_attn_Wv"]).T)
    sh["l1a_WoT"] = _bf(np.asarray(inp["L1_attn_Wo"]).T)
    sh["l1f_Wqkv"] = _bf(inp["L1_ffn_Wqkv"])
    sh["l1f_Wo"] = _bf(inp["L1_ffn_Wo"])
    sh["l1f_Wr"] = _bf(inp["L1_ffn_Wr"])
    sh["l1f_ubT"] = _f32(np.asarray(inp["L1_ffn_u_bias"]).T)
    sh["l1f_vbT"] = _f32(np.asarray(inp["L1_ffn_v_bias"]).T)
    gb = np.zeros((128, 48), np.float32)
    for i, nm in enumerate(["ln1_g", "ln1_b", "ln2_g", "ln2_b", "ln3_g", "ln3_b"]):
        gb[:, i * 8:(i + 1) * 8] = _cols(inp["L1_ffn_" + nm])
    sh["l1f_gb"] = gb
    sh["l1f_Wff1"] = _bf(inp["L1_ffn_Wff1"])
    sh["l1f_Wff2"] = _bf(inp["L1_ffn_Wff2"])
    sh["l1f_bff1"] = _cols(inp["L1_ffn_bff1"])
    sh["l1f_bff2"] = _cols(inp["L1_ffn_bff2"])
    sh["l1f_Wg"] = _f32(inp["L1_ffn_Wg"])
    sh["l1f_We1"] = _bf(inp["L1_ffn_We1"])
    sh["l1f_We2"] = _bf(inp["L1_ffn_We2"])
    be1 = np.asarray(inp["L1_ffn_be1"])
    sh["l1f_be1"] = np.ascontiguousarray(
        be1.reshape(8, 32, 128).transpose(2, 0, 1).reshape(128, 256))
    sh["l1f_be2"] = _f32(inp["L1_ffn_be2"])
    sh["l1f_Wout"] = _bf(inp["L1_ffn_Wout"])
    invf = 1.0 / (10000.0 ** (np.arange(0, D, 2, dtype=np.float64) / D))
    dist = 4223 - np.arange(PE_W, dtype=np.float64)
    valid = (dist >= 0) & (dist <= S - 1)
    ang = dist[None, :] * invf[:, None]
    peT = np.concatenate([np.sin(ang), np.cos(ang)], axis=0)
    peT[:, ~valid] = 0.0
    sh["peT"] = np.ascontiguousarray(peT.astype(bf16))
    sh["ident_b"] = np.eye(128, dtype=np.float32).astype(bf16)
    sh["ident_f"] = np.eye(128, dtype=np.float32)
    sh["onescol"] = np.ones((128, 1), np.float32)
    sh["onescolb"] = np.ones((128, 1), np.float32).astype(bf16)
    sh["onesrow"] = np.ones((1, 512), np.float32)
    sh["triu"] = np.triu(np.ones((128, 128), np.float32))
    dm = np.zeros((128, 4 * 512), np.float32)
    for a in range(4):
        w = 128 * (a + 1)
        p = np.arange(128)[:, None]; f = np.arange(512)[None, :]
        dm[:, a * 512:(a + 1) * 512] = np.where(f > p + (w - 128), -1e30, 0.0)
    sh["diagmask"] = dm
    x = np.asarray(inp["L0_attn_x"])
    maps = []
    for c in range(8):
        b, h = c // 2, c % 2
        m = dict(sh)
        xT = np.asarray(x[b], np.float32).T
        if h == 0:
            m["x"] = _f32(xT)
        else:
            m["x"] = _f32(np.concatenate([xT[:, HN:], xT[:, :HN]], axis=1))
        lg = (np.arange(32) + (16 if h else 0)) % 32
        m["M32"] = np.ascontiguousarray((lg[:, None] < lg[None, :]).astype(np.float32))
        mr = np.zeros((1, HN), np.float32)
        if h == 0:
            mr[:] = -1e30
        m["kmaskrow"] = mr
        maps.append(m)
    return maps


ENGS = ("tensor", "vector", "scalar", "sync", "gpsimd")


class Ctx:
    def __init__(self, nc, stack):
        self.nc, self.stack = nc, stack
        self.sems = {e: stack.enter_context(nc.semaphore("S_" + e)) for e in ENGS}
        self.items = []
        self.counts = {e: 0 for e in ENGS}
        self.last_seen = {e: {o: 0 for o in ENGS} for e in ENGS}

    def sbuf(self, name, shape, dtype, region=None):
        st = region if region is not None else self.stack
        return st.enter_context(self.nc.sbuf_tensor(name, shape, dtype))

    def _add(self, eng, fn, n_inc):
        waits = []
        for o in ENGS:
            if o == eng and eng == "tensor":
                continue
            if self.counts[o] > self.last_seen[eng][o]:
                waits.append((o, self.counts[o]))
                self.last_seen[eng][o] = self.counts[o]
        self.items.append((eng, fn, n_inc, waits))
        self.counts[eng] += n_inc

    def comp(self, eng, fn):
        self._add(eng, fn, 1)

    def dma(self, eng, fn, n):
        self._add(eng, fn, 16 * n)

    def fence(self, eng):
        self._add(eng, None, 0)

    def emit(self, block):
        sems = self.sems
        for e in ENGS:
            items = [(fn, n, w) for (eng, fn, n, w) in self.items if eng == e]

            def section(h, e=e, items=items):
                for fn, n_inc, waits in items:
                    for (o, v) in waits:
                        h.wait_ge(sems[o], v)
                    if n_inc == 0:
                        continue
                    if n_inc == 1:
                        fn(h).then_inc(sems[e], 1)
                    else:
                        instrs = fn(h)
                        assert len(instrs) * 16 == n_inc, (e, len(instrs), n_inc)
                        for i in instrs:
                            i.then_inc(sems[e], 16)
            getattr(block, e)(section)


DUMPS = []


def build_nc():
    nc = bass.Bass("TRN2", target_bir_lowering=False, detect_race_conditions=False)
    stack = contextlib.ExitStack()
    dt = {}
    dsh = {}

    def din(name, shape, dtype):
        dt[name] = nc.dram_tensor(name, list(shape), dtype, kind="ExternalInput")
        dsh[name] = (shape, dtype)

    def dint(name, shape, dtype):
        dt[name] = nc.dram_tensor(name, list(shape), dtype, kind="Internal")
        dsh[name] = (shape, dtype)

    din("x", [D, N], F32)
    y_d = nc.dram_tensor("y", [D, HN], F32, kind="ExternalOutput")
    dt["y"] = y_d
    for nm in ("l0_WqT", "l0_WkT", "l0_WvT", "l0_WoT"):
        din(nm, [D, D], BF16)
    for nm in ("l0_bq", "l0_bk", "l0_bv", "l0_bo"):
        din(nm, [128, 8], F32)
    din("l0_wg", [10, D, 1408], BF16)
    din("l0_wu", [10, D, 1408], BF16)
    din("l0_wd", [10, 1408, D], BF16)
    din("l0_gwT", [D, 8], F32)
    din("cw0_init", [10, N], F32)
    din("l1a_WqT", [D, 256], BF16); din("l1a_WkT", [D, 256], BF16)
    din("l1a_WvT", [D, 256], BF16); din("l1a_WoT", [256, D], BF16)
    din("l1f_Wqkv", [D, 3072], BF16); din("l1f_Wo", [D, D], BF16)
    din("l1f_Wr", [D, D], BF16)
    din("l1f_ubT", [64, 16], F32); din("l1f_vbT", [64, 16], F32)
    din("l1f_gb", [128, 48], F32)
    din("l1f_Wff1", [D, 4096], BF16); din("l1f_Wff2", [4096, D], BF16)
    din("l1f_bff1", [128, 32], F32); din("l1f_bff2", [128, 8], F32)
    din("l1f_Wg", [D, 8], F32)
    din("l1f_We1", [8, D, 4096], BF16); din("l1f_We2", [8, 4096, D], BF16)
    din("l1f_be1", [128, 256], F32); din("l1f_be2", [8, D], F32)
    din("l1f_Wout", [D, D], BF16)
    din("peT", [D, PE_W], BF16)
    din("ident_b", [128, 128], BF16); din("ident_f", [128, 128], F32)
    din("onescol", [128, 1], F32); din("onescolb", [128, 1], BF16)
    din("onesrow", [1, 512], F32)
    din("triu", [128, 128], F32)
    din("diagmask", [128, 2048], F32)
    din("M32", [32, 32], F32)
    din("kmaskrow", [1, HN], F32)

    dint("ln0b", [D, N], BF16)
    dint("q0b", [D, N], BF16); dint("k0b", [D, N], BF16); dint("v0b", [D, N], BF16)
    dint("k0tok", [128, 32 * 1024], BF16); dint("v0tok", [128, 32 * 1024], BF16)
    dint("kv0b", [D, D], BF16); dint("ksum0", [128, 8], F32)
    dint("c0b", [D, N], BF16)
    dint("res1", [D, N], F32)
    dint("ln1b", [D, N], BF16); dint("ln1f", [D, N], F32)
    dint("cw0", [10, N], F32)
    dint("res2", [D, N], F32)
    dint("ln2b", [D, N], BF16)
    dint("q1n", [256, N], F32); dint("k1n", [256, N], F32); dint("v1T", [256, N], F32)
    dint("dkv1", [64, 4 * 2048], F32)
    dint("dk1T", [128, 64], F32)
    dint("st1", [64, 4 * 2048], F32)
    dint("kstT", [64, 128], F32)
    dint("y1b", [256, N], BF16)
    dint("res3", [D, N], F32)
    dint("ln3b", [D, N], BF16); dint("ln3f", [D, N], F32)
    dint("qT1f", [D, HN], BF16)
    dint("kT1f", [D, N], BF16); dint("vT1f", [D, N], BF16)
    dint("rTs", [D, PE_W], BF16)
    dint("vtok1f", [128, 32 * 1024], BF16)
    dint("BDa", [128, BDW], BF16)
    dint("BDb", [128, BDW], BF16)
    dint("ctxtok", [128, 16 * 1024], BF16)
    dint("ctxT", [D, HN], BF16)
    dint("h0", [D, HN], F32)
    dint("h1b", [D, HN], BF16); dint("h1f", [D, HN], F32)
    dint("ffh", [4096, HN], BF16)
    dint("h2pre", [D, HN], F32)
    dint("h2b", [D, HN], BF16); dint("h2f", [D, HN], F32)
    dint("cw1", [8, HN], F32)
    dint("h3pre", [D, HN], F32)
    dint("h3b", [D, HN], BF16)

    block = stack.enter_context(nc.Block())
    ctx = Ctx(nc, stack)
    pb = [stack.enter_context(nc.psum_tensor("pb%d" % i, [128, 512], F32))
          for i in range(8)]
    pbb = [p.ap().bitcast(BF16) for p in pb]

    onescol = ctx.sbuf("c_onescol", [128, 1], F32)
    onescolb = ctx.sbuf("c_onescolb", [128, 1], BF16)
    onesrow = ctx.sbuf("c_onesrow", [1, 512], F32)
    identb = ctx.sbuf("c_identb", [128, 128], BF16)
    identf = ctx.sbuf("c_identf", [128, 128], F32)
    triu = ctx.sbuf("c_triu", [128, 128], F32)
    dmask = ctx.sbuf("c_dmask", [128, 2048], F32)
    kmrow = ctx.sbuf("c_kmrow", [1, HN], F32)
    ubT = ctx.sbuf("c_ubT", [64, 16], F32)
    vbT = ctx.sbuf("c_vbT", [64, 16], F32)
    gbcols = ctx.sbuf("c_gb", [128, 48], F32)
    m32sb = ctx.sbuf("c_m32", [32, 32], F32)

    xs = ctx.sbuf("g_xs", [128, 8 * 512], F32)
    sc = ctx.sbuf("g_sc", [128, 4 * 512], F32)
    rows = ctx.sbuf("g_rows", [1, 3 * 512], F32)
    bst = ctx.sbuf("g_bst", [128, 512], BF16)
    tstg = ctx.sbuf("g_tstg", [128, 512], BF16)

    def TT(out, a, b, op):
        ctx.comp("vector", lambda v, out=out, a=a, b=b, op=op:
                 v.tensor_tensor(out=out, in0=a, in1=b, op=op))

    def TS(out, a, s1, op):
        ctx.comp("vector", lambda v, out=out, a=a, s1=s1, op=op:
                 v.tensor_scalar(out=out, in0=a, scalar1=s1, scalar2=None, op0=op))

    def CP(out, a):
        ctx.comp("vector", lambda v, out=out, a=a: v.tensor_copy(out=out, in_=a))

    def RMAX(out, a):
        ctx.comp("vector", lambda v, out=out, a=a:
                 v.reduce_max(out=out, in_=a, axis=mybir.AxisListType.X))

    def RSUM(out, a):
        ctx.comp("vector", lambda v, out=out, a=a:
                 v.reduce_sum(out=out, in_=a, axis=mybir.AxisListType.X))

    def RECIP(out, a):
        ctx.comp("vector", lambda v, out=out, a=a: v.reciprocal(out=out, in_=a))

    def ACT(out, a, func, bias=None, scale=1.0):
        if bias is None:
            ctx.comp("scalar", lambda s, out=out, a=a, func=func, scale=scale:
                     s.activation(out=out, in_=a, func=func, scale=scale))
        else:
            ctx.comp("scalar", lambda s, out=out, a=a, func=func, bias=bias,
                     scale=scale:
                     s.activation(out=out, in_=a, func=func, bias=bias, scale=scale))

    def SMUL(out, a, c):
        ctx.comp("scalar", lambda s, out=out, a=a, c=c: s.mul(out, a, c))

    def SCP(out, a):
        ctx.comp("scalar", lambda s, out=out, a=a: s.copy(out=out, in_=a))

    def MM(fn):
        ctx.comp("tensor", fn)

    def LD1(out, in_):
        ctx.dma("sync", lambda s, out=out, in_=in_:
                [s.dma_start(out=out, in_=in_)], 1)

    def LDL(pairs):
        ctx.dma("sync", lambda s, pairs=pairs:
                [s.dma_start(out=o, in_=i) for (o, i) in pairs], len(pairs))

    def GDL(pairs):
        ctx.dma("gpsimd", lambda g, pairs=pairs:
                [g.dma_start(out=o, in_=i) for (o, i) in pairs], len(pairs))

    def GACC(out, in_):
        ctx.dma("gpsimd", lambda g, out=out, in_=in_:
                [g.dma_start(out=out, in_=in_, accum_op=ALU.add)], 1)

    def bcast_row(bank, row_ap, width, np_=128):
        MM(lambda t, bank=bank, row_ap=row_ap, width=width, np_=np_:
           t.matmul(bank[0:np_, 0:width], onesrow[0:1, 0:np_], row_ap,
                    start=True, stop=True))

    LDL([(onescol[:], dt["onescol"].ap()),
         (onescolb[:], dt["onescolb"].ap()),
         (onesrow[:], dt["onesrow"].ap()),
         (identb[:], dt["ident_b"].ap()),
         (identf[:], dt["ident_f"].ap()),
         (triu[:], dt["triu"].ap()),
         (dmask[:], dt["diagmask"].ap()),
         (kmrow[:], dt["kmaskrow"].ap()),
         (ubT[:], dt["l1f_ubT"].ap()),
         (vbT[:], dt["l1f_vbT"].ap()),
         (gbcols[:], dt["l1f_gb"].ap()),
         (m32sb[:], dt["M32"].ap()),
         (dt["cw0"].ap(), dt["cw0_init"].ap())])


    def emit_ln(src, Nc, out_bf, out_f32=None, gb_idx=None):
        src_a, out_a = dt[src].ap(), dt[out_bf].ap()
        of_a = dt[out_f32].ap() if out_f32 else None
        for w in range(Nc // 512):
            w0 = w * 512
            LD1(xs.ap().rearrange("p (c n) -> p c n", n=512),
                src_a[:, w0:w0 + 512].rearrange("(c p) n -> p c n", p=128))
            MM(lambda t, _=None: [
                t.matmul(pb[0][0:1, :], onescol[:], xs[:, c * 512:(c + 1) * 512],
                         start=(c == 0), stop=(c == 7)) for c in range(8)][-1])
            for half in range(2):
                cs = [0, 1, 2, 3] if half == 0 else [4, 5, 6, 7]

                def sqw(s, cs=cs):
                    i = None
                    for j, c in enumerate(cs):
                        i = s.activation(out=sc[:, j * 512:(j + 1) * 512],
                                         in_=xs[:, c * 512:(c + 1) * 512],
                                         func=AF.Square)
                    return i
                ctx.comp("scalar", sqw)

                def mmq(t, cs=cs, half=half):
                    i = None
                    for j in range(4):
                        i = t.matmul(pb[1][0:1, :], onescol[:],
                                     sc[:, j * 512:(j + 1) * 512],
                                     start=(half == 0 and j == 0),
                                     stop=(half == 1 and j == 3))
                    return i
                ctx.comp("tensor", mmq)
            SMUL(rows[0:1, 0:512], pb[0][0:1, :], 1.0 / D)
            SMUL(rows[0:1, 512:1024], pb[1][0:1, :], 1.0 / D)
            TT(rows[0:1, 1024:1536], rows[0:1, 0:512], rows[0:1, 0:512], ALU.mult)
            TT(rows[0:1, 512:1024], rows[0:1, 512:1024], rows[0:1, 1024:1536],
               ALU.subtract)
            TS(rows[0:1, 512:1024], rows[0:1, 512:1024], 1e-5, ALU.add)
            ACT(rows[0:1, 1024:1536], rows[0:1, 512:1024], AF.Ln)
            ACT(rows[0:1, 512:1024], rows[0:1, 1024:1536], AF.Exp, scale=-0.5)
            TT(rows[0:1, 1024:1536], rows[0:1, 0:512], rows[0:1, 512:1024], ALU.mult)
            TS(rows[0:1, 1024:1536], rows[0:1, 1024:1536], -1.0, ALU.mult)
            bcast_row(pb[2], rows[0:1, 512:1024], 512)
            bcast_row(pb[3], rows[0:1, 1024:1536], 512)
            for c in range(8):
                TT(sc[:, 0:512], xs[:, c * 512:(c + 1) * 512], pb[2][:, :], ALU.mult)
                TT(sc[:, 512:1024], sc[:, 0:512], pb[3][:, :], ALU.add)
                if gb_idx is not None:
                    g0 = gb_idx * 16
                    TS(sc[:, 512:1024], sc[:, 512:1024],
                       gbcols[:, g0 + c:g0 + c + 1], ALU.mult)
                    TS(sc[:, 512:1024], sc[:, 512:1024],
                       gbcols[:, g0 + 8 + c:g0 + 8 + c + 1], ALU.add)
                CP(bst[:, 0:512], sc[:, 512:1024])
                LD1(out_a[c * 128:(c + 1) * 128, w0:w0 + 512], bst[:, 0:512])
                if of_a is not None:
                    LD1(of_a[c * 128:(c + 1) * 128, w0:w0 + 512], sc[:, 512:1024])

    def load_w(wreg, w_name, K, M, cols=None):
        a = dt[w_name].ap()
        if cols is not None:
            a = a[:, cols[0]:cols[1]]
            M = cols[1] - cols[0]
        pairs = []
        for c in range(K // 128):
            pairs.append((wreg.ap()[:, c * M:(c + 1) * M],
                          a[c * 128:(c + 1) * 128, :]))
        LDL(pairs)

    def emit_proj(src, K, w_name, M, Nc, wreg, out_bf=None, wcols=None,
                  bias=None, act=None, res=None, res_out=None, srcoff=0,
                  bias_sb=None):
        load_w(wreg, w_name, K, M, cols=wcols)
        src_a = dt[src].ap()
        nkc = K // 128
        regp = contextlib.ExitStack()
        stg = ctx.sbuf("stg_%s" % (out_bf or res_out), [128, nkc * 512], BF16,
                       regp)
        for w in range(Nc // 512):
            w0 = srcoff + w * 512
            LD1(stg.ap().rearrange("p (c n) -> p c n", n=512),
                src_a[:, w0:w0 + 512].rearrange("(c p) n -> p c n", p=128))
            for mc in range(M // 128):
                bank = pb[mc % 2]

                def mmp(t, mc=mc, bank=bank, nkc=nkc):
                    i = None
                    for kc in range(nkc):
                        i = t.matmul(bank[:, :],
                                     wreg[:, kc * M + mc * 128:kc * M + (mc + 1) * 128],
                                     stg[:, kc * 512:(kc + 1) * 512],
                                     start=(kc == 0), stop=(kc == nkc - 1))
                    return i
                ctx.comp("tensor", mmp)
                bcol = None
                if bias is not None:
                    bsb = bias_sb if bias_sb is not None else biascols[bias]
                    bcol = bsb[:, mc:mc + 1]
                if act == "elu1":
                    if bcol is not None:
                        TS(sc[:, 0:512], bank[:, :], bcol, ALU.add)
                    else:
                        CP(sc[:, 0:512], bank[:, :])
                    TS(sc[:, 512:1024], sc[:, 0:512], 0.0, ALU.min)
                    ACT(sc[:, 1024:1536], sc[:, 512:1024], AF.Exp)
                    ACT(sc[:, 512:1024], sc[:, 0:512], AF.Relu)
                    TT(sc[:, 0:512], sc[:, 512:1024], sc[:, 1024:1536], ALU.add)
                elif act == "relu":
                    ACT(sc[:, 0:512], bank[:, :], AF.Relu, bias=bcol)
                else:
                    if bcol is not None:
                        TS(sc[:, 0:512], bank[:, :], bcol, ALU.add)
                    else:
                        CP(sc[:, 0:512], bank[:, :])
                if res is not None:
                    LD1(sc[:, 512:1024],
                        dt[res].ap()[mc * 128:(mc + 1) * 128, w * 512:w * 512 + 512])
                    TT(sc[:, 0:512], sc[:, 0:512], sc[:, 512:1024], ALU.add)
                    LD1(dt[res_out].ap()[mc * 128:(mc + 1) * 128, w * 512:w * 512 + 512],
                        sc[:, 0:512])
                    if out_bf is not None:
                        CP(bst[:, 0:512], sc[:, 0:512])
                        LD1(dt[out_bf].ap()[mc * 128:(mc + 1) * 128,
                                            w * 512:w * 512 + 512], bst[:, 0:512])
                else:
                    CP(bst[:, 0:512], sc[:, 0:512])
                    LD1(dt[out_bf].ap()[mc * 128:(mc + 1) * 128,
                                        w * 512:w * 512 + 512], bst[:, 0:512])
        regp.close()

    def emit_tok_transpose(src, R, Nc, dst):
        src_a, dst_a = dt[src].ap(), dt[dst].ap()
        for t4 in range(Nc // 512):
            for rc in range(R // 128):
                LD1(tstg[:, 0:512],
                    src_a[rc * 128:(rc + 1) * 128, t4 * 512:(t4 + 1) * 512])

                def tp(t, _=None):
                    i = None
                    for j in range(4):
                        i = t.transpose(pbb[4][:, j * 128:(j + 1) * 128],
                                        tstg[:, j * 128:(j + 1) * 128], identb[:])
                    return i
                ctx.comp("tensor", tp)
                CP(bst[:, 0:512], pbb[4][:, 0:512])
                LD1(dst_a.rearrange("p (t r) -> p t r", r=R)
                    [:, t4 * 4:t4 * 4 + 4, rc * 128:(rc + 1) * 128],
                    bst[:, 0:512].rearrange("p (t r) -> p t r", r=128))

    biascols = {}
    for nm in ("l0_bq", "l0_bk", "l0_bv", "l0_bo", "l1f_bff1", "l1f_bff2",
               "l1f_be1"):
        bw = {"l0_bq": 8, "l0_bk": 8, "l0_bv": 8, "l0_bo": 8,
              "l1f_bff1": 32, "l1f_bff2": 8, "l1f_be1": 256}[nm]
        biascols[nm] = ctx.sbuf("bc_" + nm, [128, bw], F32)
        LD1(biascols[nm][:], dt[nm].ap())

    emit_ln("x", N, "ln0b")

    reg = contextlib.ExitStack()
    wreg2 = ctx.sbuf("wreg2", [128, 8 * 1024], BF16, reg)
    emit_proj("ln0b", D, "l0_WqT", D, N, wreg2, out_bf="q0b", bias="l0_bq",
              act="elu1")
    emit_proj("ln0b", D, "l0_WkT", D, N, wreg2, out_bf="k0b", bias="l0_bk",
              act="elu1")
    emit_proj("ln0b", D, "l0_WvT", D, N, wreg2, out_bf="v0b", bias="l0_bv")
    reg.close()

    emit_tok_transpose("k0b", D, N, "k0tok")
    emit_tok_transpose("v0b", D, N, "v0tok")

    reg = contextlib.ExitStack()
    kvkb = ctx.sbuf("kvkb", [128, 512], BF16, reg)
    kvvb = ctx.sbuf("kvvb", [128, 512], BF16, reg)
    for eh in range(2):
        for dg in range(2):
            for tc in range(32):
                LDL([(kvkb[:], dt["k0tok"].ap()[:, tc * 1024 + dg * 512:
                                                tc * 1024 + dg * 512 + 512]),
                     (kvvb[:], dt["v0tok"].ap()[:, tc * 1024 + eh * 512:
                                                tc * 1024 + eh * 512 + 512])])

                def mkv(t, eh=eh, dg=dg, tc=tc):
                    i = None
                    for dc in range(4):
                        i = t.matmul(pb[dc][:, :], kvkb[:, dc * 128:(dc + 1) * 128],
                                     kvvb[:], start=(tc == 0), stop=(tc == 31))
                    if eh == 0:
                        for dc in range(4):
                            i = t.matmul(pb[4][:, dg * 4 + dc:dg * 4 + dc + 1],
                                         kvkb[:, dc * 128:(dc + 1) * 128],
                                         onescolb[:], start=(tc == 0),
                                         stop=(tc == 31))
                    return i
                ctx.comp("tensor", mkv)
            for dc in range(4):
                dci = dg * 4 + dc
                CP(bst[:, 0:512], pb[dc][:, :])
                LD1(dt["kv0b"].ap()[dci * 128:(dci + 1) * 128,
                                    eh * 512:eh * 512 + 512], bst[:, 0:512])
        if eh == 0:
            CP(sc[:, 0:8], pb[4][:, 0:8])
            TS(sc[:, 0:8], sc[:, 0:8], 1e-6, ALU.add)
            LD1(dt["ksum0"].ap(), sc[:, 0:8])
    reg.close()

    reg = contextlib.ExitStack()
    kvsb = ctx.sbuf("kvsb", [128, 8 * 1024], BF16, reg)
    ksp = ctx.sbuf("ksp", [128, 8], F32, reg)
    qstg = ctx.sbuf("qstg", [128, 8 * 512], BF16, reg)
    LDL([(kvsb.ap()[:, c * 1024:(c + 1) * 1024],
          dt["kv0b"].ap()[c * 128:(c + 1) * 128, :]) for c in range(8)])
    LD1(ksp[:], dt["ksum0"].ap())
    for w in range(8):
        w0 = w * 512
        LD1(qstg.ap().rearrange("p (c n) -> p c n", n=512),
            dt["q0b"].ap()[:, w0:w0 + 512].rearrange("(c p) n -> p c n", p=128))
        for c in range(8):
            TS(sc[:, 0:512], qstg[:, c * 512:(c + 1) * 512], ksp[:, c:c + 1],
               ALU.mult)
            MM(lambda t, c=c: t.matmul(pb[0][0:1, :], onescol[:], sc[:, 0:512],
                                       start=(c == 0), stop=(c == 7)))
        RECIP(rows[0:1, 0:512], pb[0][0:1, :])
        bcast_row(pb[1], rows[0:1, 0:512], 512)
        for ec in range(8):
            def mmc(t, ec=ec):
                i = None
                for dc in range(8):
                    i = t.matmul(pb[2][:, :],
                                 kvsb[:, dc * 1024 + ec * 128:dc * 1024 + (ec + 1) * 128],
                                 qstg[:, dc * 512:(dc + 1) * 512],
                                 start=(dc == 0), stop=(dc == 7))
                return i
            ctx.comp("tensor", mmc)
            CP(sc[:, 0:512], pb[2][:, :])
            TT(sc[:, 512:1024], sc[:, 0:512], pb[1][:, :], ALU.mult)
            CP(bst[:, 0:512], sc[:, 512:1024])
            LD1(dt["c0b"].ap()[ec * 128:(ec + 1) * 128, w0:w0 + 512], bst[:, 0:512])
    reg.close()

    reg = contextlib.ExitStack()
    wreg2 = ctx.sbuf("wreg2b", [128, 8 * 1024], BF16, reg)
    emit_proj("c0b", D, "l0_WoT", D, N, wreg2, wcols=None, bias="l0_bo",
              res="x", res_out="res1")
    reg.close()

    emit_ln("res1", N, "ln1b", out_f32="ln1f")

    reg = contextlib.ExitStack()
    gwsb = ctx.sbuf("gwsb", [128, 64], F32, reg)
    xg = ctx.sbuf("xg", [128, 8 * 128], F32, reg)
    g8 = ctx.sbuf("g8", [128, 8], F32, reg)
    gcol = ctx.sbuf("gcol", [128, 4], F32, reg)
    cwt = ctx.sbuf("cwt", [8, 128], F32, reg)
    LD1(gwsb.ap().rearrange("p (c m) -> p c m", m=8),
        dt["l0_gwT"].ap().rearrange("(c p) m -> p c m", p=128))
    for tt in range(32):
        LD1(xg.ap().rearrange("p (c n) -> p c n", n=128),
            dt["ln1f"].ap()[:, tt * 128:(tt + 1) * 128]
            .rearrange("(c p) n -> p c n", p=128))

        def mlg(t, _=None):
            i = None
            for c in range(8):
                i = t.matmul(pb[0][:, 0:8], xg[:, c * 128:(c + 1) * 128],
                             gwsb[:, c * 8:(c + 1) * 8],
                             start=(c == 0), stop=(c == 7))
            return i
        ctx.comp("tensor", mlg)
        RMAX(gcol[:, 0:1], pb[0][:, 0:8])
        TS(gcol[:, 1:2], gcol[:, 0:1], -1.0, ALU.mult)
        ACT(g8[:], pb[0][:, 0:8], AF.Exp, bias=gcol[:, 1:2])
        RSUM(gcol[:, 0:1], g8[:])
        RECIP(gcol[:, 1:2], gcol[:, 0:1])
        TS(g8[:], g8[:], gcol[:, 1:2], ALU.mult)
        RMAX(gcol[:, 0:1], g8[:])
        TS(sc[:, 0:8], g8[:], gcol[:, 0:1], ALU.is_ge)
        TS(sc[:, 0:8], sc[:, 0:8], -1e30, ALU.mult)
        TT(sc[:, 8:16], g8[:], sc[:, 0:8], ALU.add)
        RMAX(gcol[:, 1:2], sc[:, 8:16])
        TT(gcol[:, 2:3], gcol[:, 0:1], gcol[:, 1:2], ALU.add)
        TS(gcol[:, 2:3], gcol[:, 2:3], 1e-20, ALU.add)
        RECIP(gcol[:, 3:4], gcol[:, 2:3])
        TS(gcol[:, 3:4], gcol[:, 3:4], 2.5, ALU.mult)
        TS(sc[:, 0:8], g8[:], gcol[:, 1:2], ALU.is_ge)
        TT(sc[:, 8:16], sc[:, 0:8], g8[:], ALU.mult)
        TS(sc[:, 8:16], sc[:, 8:16], gcol[:, 3:4], ALU.mult)
        MM(lambda t, _=None: t.transpose(pb[1][0:8, 0:128], sc[:, 8:16], identf[:]))
        CP(cwt[:], pb[1][0:8, 0:128])
        LD1(dt["cw0"].ap()[0:8, tt * 128:(tt + 1) * 128], cwt[:])
    reg.close()

    LDL([(dt["res2"].ap()[c * 128:(c + 1) * 128, :],
          dt["res1"].ap()[c * 128:(c + 1) * 128, :]) for c in range(8)])

    def emit_moe(src, Nc, NE, F, wg_name, wu_name, wd_name, cw_name, out_name,
                 tcw, tag, relu_bias=None, be2_name=None):
        nfc = F // 128
        regm = contextlib.ExitStack()
        w1 = ctx.sbuf(tag + "moe_w1", [128, 8 * F], BF16, regm)
        w2 = ctx.sbuf(tag + "moe_w2", [128, 8 * F] if wu_name else [1, 1],
                      BF16 if wu_name else F32, regm)
        wdn = ctx.sbuf(tag + "moe_wd", [128, nfc * 1024], BF16, regm)
        hidb = ctx.sbuf(tag + "moe_hid", [128, nfc * tcw], BF16, regm)
        cwb = ctx.sbuf(tag + "moe_cwb", [128, 512], F32, regm)
        cwr = ctx.sbuf(tag + "moe_cwr", [1, 512], F32, regm)
        stg = ctx.sbuf(tag + "moe_stg", [128, 8 * tcw], BF16, regm)
        src_a = dt[src].ap()
        wgf = dt[wg_name].ap().rearrange("e d f -> (e d) f")
        wuf = dt[wu_name].ap().rearrange("e d f -> (e d) f") if wu_name else None
        wdf = dt[wd_name].ap().rearrange("e f d -> (e f) d")
        ntc = Nc // tcw
        for e in range(NE):
            LDL([(w1.ap()[:, c * F:(c + 1) * F],
                  wgf[e * D + c * 128:e * D + (c + 1) * 128, :])
                 for c in range(8)])
            if wu_name:
                LDL([(w2.ap()[:, c * F:(c + 1) * F],
                      wuf[e * D + c * 128:e * D + (c + 1) * 128, :])
                     for c in range(8)])
            LDL([(wdn.ap()[:, c * 1024:(c + 1) * 1024],
                  wdf[e * F + c * 128:e * F + (c + 1) * 128, :])
                 for c in range(nfc)])
            for tc in range(ntc):
                t0 = tc * tcw
                LD1(stg.ap().rearrange("p (c n) -> p c n", n=tcw),
                    src_a[:, t0:t0 + tcw].rearrange("(c p) n -> p c n", p=128))
                LD1(cwr[0:1, 0:tcw], dt[cw_name].ap()[e:e + 1, t0:t0 + tcw])
                bcast_row(pb[6], cwr[0:1, 0:tcw], tcw)
                CP(cwb[:, 0:tcw], pb[6][0:128, 0:tcw])
                for fc in range(nfc):
                    def mg(t, fc=fc, tcw=tcw):
                        i = None
                        for dc in range(8):
                            i = t.matmul(pb[4][:, 0:tcw],
                                         w1[:, dc * F + fc * 128:dc * F + (fc + 1) * 128],
                                         stg[:, dc * tcw:(dc + 1) * tcw],
                                         start=(dc == 0), stop=(dc == 7))
                        return i
                    ctx.comp("tensor", mg)
                    if wu_name:
                        def mu(t, fc=fc, tcw=tcw):
                            i = None
                            for dc in range(8):
                                i = t.matmul(pb[5][:, 0:tcw],
                                             w2[:, dc * F + fc * 128:dc * F + (fc + 1) * 128],
                                             stg[:, dc * tcw:(dc + 1) * tcw],
                                             start=(dc == 0), stop=(dc == 7))
                            return i
                        ctx.comp("tensor", mu)
                        ACT(sc[:, 0:tcw], pb[4][:, 0:tcw], AF.Sigmoid)
                        TT(sc[:, 512:512 + tcw], pb[4][:, 0:tcw], sc[:, 0:tcw],
                           ALU.mult)
                        TT(sc[:, 0:tcw], sc[:, 512:512 + tcw], pb[5][:, 0:tcw],
                           ALU.mult)
                    else:
                        ACT(sc[:, 0:tcw], pb[4][:, 0:tcw], AF.Relu,
                            bias=biascols[relu_bias][:, e * 32 + fc:e * 32 + fc + 1])
                    TT(sc[:, 512:512 + tcw], sc[:, 0:tcw], cwb[:, 0:tcw], ALU.mult)
                    CP(hidb[:, fc * tcw:(fc + 1) * tcw], sc[:, 512:512 + tcw])
                for dg in range(2):
                    def md(t, dg=dg, nfc=nfc, tcw=tcw):
                        i = None
                        for dc in range(4):
                            dci = dg * 4 + dc
                            for fc in range(nfc):
                                i = t.matmul(pb[dc][:, 0:tcw],
                                             wdn[:, fc * 1024 + dci * 128:fc * 1024 + (dci + 1) * 128],
                                             hidb[:, fc * tcw:(fc + 1) * tcw],
                                             start=(fc == 0), stop=(fc == nfc - 1))
                        return i
                    ctx.comp("tensor", md)
                    for dc in range(4):
                        dci = dg * 4 + dc
                        CP(sc[:, 0:tcw], pb[dc][:, 0:tcw])
                        GACC(dt[out_name].ap()[dci * 128:(dci + 1) * 128,
                                               t0:t0 + tcw], sc[:, 0:tcw])
        if be2_name is not None:
            be2sb = xs.ap()[0:8, 0:1024]
            cw8 = xs.ap()[0:8, 1024:1536]
            LD1(be2sb, dt[be2_name].ap())
            for tc in range(Nc // 512):
                t0 = tc * 512
                LD1(cw8, dt[cw_name].ap()[0:8, t0:t0 + 512])
                for dci in range(8):
                    MM(lambda t, dci=dci: t.matmul(
                        pb[dci % 2][:, :], be2sb[:, dci * 128:(dci + 1) * 128],
                        cw8, start=True, stop=True))
                    CP(sc[:, 0:512], pb[dci % 2][:, :])
                    GACC(dt[out_name].ap()[dci * 128:(dci + 1) * 128,
                                           t0:t0 + 512], sc[:, 0:512])
        regm.close()

    emit_moe("ln1b", N, 10, 1408, "l0_wg", "l0_wu", "l0_wd", "cw0", "res2",
             512, "m0")

    emit_ln("res2", N, "ln2b")

    reg = contextlib.ExitStack()
    wreg1 = ctx.sbuf("wreg1a", [128, 8 * 256], BF16, reg)
    astg = ctx.sbuf("astg", [128, 8 * 512], BF16, reg)
    for (wn, on, do_pu) in (("l1a_WqT", "q1n", True), ("l1a_WkT", "k1n", True),
                            ("l1a_WvT", "v1T", False)):
        load_w(wreg1, wn, D, 256)
        for w in range(8):
            w0 = w * 512
            LD1(astg.ap().rearrange("p (c n) -> p c n", n=512),
                dt["ln2b"].ap()[:, w0:w0 + 512].rearrange("(c p) n -> p c n", p=128))
            for ch in range(2):
                def mmq(t, ch=ch):
                    i = None
                    for kc in range(8):
                        i = t.matmul(pb[0][:, :],
                                     wreg1[:, kc * 256 + ch * 128:kc * 256 + (ch + 1) * 128],
                                     astg[:, kc * 512:(kc + 1) * 512],
                                     start=(kc == 0), stop=(kc == 7))
                    return i
                ctx.comp("tensor", mmq)
                if do_pu:
                    TS(sc[:, 0:512], pb[0][:, :], 0.0, ALU.min)
                    ACT(sc[:, 512:1024], sc[:, 0:512], AF.Exp)
                    ACT(sc[:, 0:512], pb[0][:, :], AF.Relu)
                    TT(sc[:, 1024:1536], sc[:, 0:512], sc[:, 512:1024], ALU.add)
                    ACT(sc[:, 0:512], sc[:, 1024:1536], AF.Square)
                    for hh in range(2):
                        h0_ = hh * 64
                        MM(lambda t, h0_=h0_: t.matmul(
                            pb[1][0:1, 0:512], onescol[h0_:h0_ + 64, 0:1],
                            sc[h0_:h0_ + 64, 0:512], start=True, stop=True))
                        ACT(rows[0:1, 0:512], pb[1][0:1, 0:512], AF.Sqrt)
                        TS(rows[0:1, 0:512], rows[0:1, 0:512], 1e-6, ALU.add)
                        RECIP(rows[0:1, 512:1024], rows[0:1, 0:512])
                        bcast_row(pb[2], rows[0:1, 512:1024], 512, np_=64)
                        TT(sc[h0_:h0_ + 64, 1024:1536], sc[h0_:h0_ + 64, 1024:1536],
                           pb[2][0:64, 0:512], ALU.mult)
                    LD1(dt[on].ap()[ch * 128:(ch + 1) * 128, w0:w0 + 512],
                        sc[:, 1024:1536])
                else:
                    CP(sc[:, 0:512], pb[0][:, :])
                    LD1(dt[on].ap()[ch * 128:(ch + 1) * 128, w0:w0 + 512],
                        sc[:, 0:512])
    reg.close()

    reg = contextlib.ExitStack()
    kch = ctx.sbuf("a1_k", [64, 128], F32, reg)
    vch = ctx.sbuf("a1_v", [64, 128], F32, reg)
    qch = ctx.sbuf("a1_q", [64, 128], F32, reg)
    ktok = ctx.sbuf("a1_kt", [128, 64], F32, reg)
    vtok = ctx.sbuf("a1_vt", [128, 64], F32, reg)
    stt = ctx.sbuf("a1_st", [64, 64], F32, reg)
    kstc = ctx.sbuf("a1_kst", [64, 1], F32, reg)
    stm = ctx.sbuf("a1_stm", [128, 128], F32, reg)
    ybt = ctx.sbuf("a1_y", [64, 128], BF16, reg)
    dkvf = ctx.sbuf("a1_dkvf", [32, 4096], F32, reg)
    dkm = ctx.sbuf("a1_dkm", [32, 64], F32, reg)
    for hh in range(4):
        for c in range(32):
            r = hh * 32 + c
            LDL([(kch[:], dt["k1n"].ap()[hh * 64:hh * 64 + 64, c * 128:(c + 1) * 128]),
                 (vch[:], dt["v1T"].ap()[hh * 64:hh * 64 + 64, c * 128:(c + 1) * 128])])
            MM(lambda t, _=None: t.transpose(pb[0][:, 0:64], kch[:],
                                             identf[0:64, 0:64]))
            CP(ktok[:], pb[0][:, 0:64])
            MM(lambda t, _=None: t.transpose(pb[1][:, 0:64], vch[:],
                                             identf[0:64, 0:64]))
            CP(vtok[:], pb[1][:, 0:64])
            MM(lambda t, _=None: t.matmul(pb[2][0:64, 0:64], ktok[:], vtok[:],
                                          start=True, stop=True))
            CP(stt[:], pb[2][0:64, 0:64])
            LD1(dt["dkv1"].ap()[0:64, r * 64:(r + 1) * 64], stt[:])
            MM(lambda t, _=None: t.matmul(pb[3][0:1, 0:64], onescol[:], ktok[:],
                                          start=True, stop=True))
            CP(rows[0:1, 0:64], pb[3][0:1, 0:64])
            LD1(dt["dk1T"].ap()[r:r + 1, 0:64], rows[0:1, 0:64])
    for hh in range(4):
        LD1(dkvf.ap().rearrange("c (p j) -> c p j", j=64),
            dt["dkv1"].ap()[:, hh * 2048:(hh + 1) * 2048]
            .rearrange("p (c j) -> c p j", j=64))
        for w in range(8):
            MM(lambda t, w=w: t.matmul(pb[w % 2][0:32, :], m32sb[:],
                                       dkvf[:, w * 512:(w + 1) * 512],
                                       start=True, stop=True))
            CP(sc[0:32, 0:512], pb[w % 2][0:32, :])
            LD1(dt["st1"].ap()[w * 8:(w + 1) * 8, hh * 2048:(hh + 1) * 2048]
                .rearrange("p (c j) -> c p j", j=64),
                sc[0:32, 0:512].rearrange("c (p j) -> c p j", j=64))
        LD1(dkm[:], dt["dk1T"].ap()[hh * 32:(hh + 1) * 32, 0:64])
        MM(lambda t, _=None: t.matmul(pb[2][0:32, 0:64], m32sb[:], dkm[:],
                                      start=True, stop=True))
        CP(sc[0:32, 0:64], pb[2][0:32, 0:64])
        MM(lambda t, _=None: t.transpose(pb[3][0:64, 0:32], sc[0:32, 0:64],
                                         identf[0:32, 0:32]))
        CP(sc[0:64, 512:544], pb[3][0:64, 0:32])
        LD1(dt["kstT"].ap()[0:64, hh * 32:(hh + 1) * 32], sc[0:64, 512:544])
    for hh in range(4):
        for c in range(32):
            r = hh * 32 + c
            LDL([(qch[:], dt["q1n"].ap()[hh * 64:hh * 64 + 64, c * 128:(c + 1) * 128]),
                 (kch[:], dt["k1n"].ap()[hh * 64:hh * 64 + 64, c * 128:(c + 1) * 128]),
                 (vch[:], dt["v1T"].ap()[hh * 64:hh * 64 + 64, c * 128:(c + 1) * 128]),
                 (stt[:], dt["st1"].ap()[0:64, r * 64:(r + 1) * 64])])
            ctx.dma("sync", lambda s, r=r:
                    [s.dma_start(out=kstc[:], in_=dt["kstT"].ap()[0:64, r:r + 1],
                                 allow_slow_non_contiguous=True)], 1)
            MM(lambda t, _=None: t.transpose(pb[0][:, 0:64], vch[:],
                                             identf[0:64, 0:64]))
            CP(vtok[:], pb[0][:, 0:64])
            MM(lambda t, _=None: t.matmul(pb[1][:, 0:128], kch[:], qch[:],
                                          start=True, stop=True))
            TT(stm[:], pb[1][:, 0:128], triu[:], ALU.mult)

            def mden(t, _=None):
                t.matmul(pb[2][0:1, 0:128], onescol[:], stm[:],
                         start=True, stop=False)
                return t.matmul(pb[2][0:1, 0:128], kstc[:], qch[:],
                                start=False, stop=True)
            ctx.comp("tensor", mden)

            def mnum(t, _=None):
                t.matmul(pb[3][0:64, 0:128], vtok[:], stm[:],
                         start=True, stop=False)
                return t.matmul(pb[3][0:64, 0:128], stt[:], qch[:],
                                start=False, stop=True)
            ctx.comp("tensor", mnum)
            TS(rows[0:1, 0:128], pb[2][0:1, 0:128], 1e-6, ALU.add)
            RECIP(rows[0:1, 128:256], rows[0:1, 0:128])
            MM(lambda t, _=None: t.matmul(pb[4][0:64, 0:128], onesrow[0:1, 0:64],
                                          rows[0:1, 128:256], start=True, stop=True))
            CP(sc[0:64, 0:128], pb[3][0:64, 0:128])
            TT(ybt[:], sc[0:64, 0:128], pb[4][0:64, 0:128], ALU.mult)
            LD1(dt["y1b"].ap()[hh * 64:hh * 64 + 64, c * 128:(c + 1) * 128], ybt[:])
    reg.close()

    reg = contextlib.ExitStack()
    wreg1o = ctx.sbuf("wreg1o", [128, 2 * 1024], BF16, reg)
    emit_proj("y1b", 256, "l1a_WoT", D, N, wreg1o, res="res2", res_out="res3")
    reg.close()

    emit_ln("res3", N, "ln3b", out_f32="ln3f")

    reg = contextlib.ExitStack()
    wreg2c = ctx.sbuf("wreg2c", [128, 8 * 1024], BF16, reg)
    emit_proj("ln3b", D, "l1f_Wqkv", 1024, HN, wreg2c, out_bf="qT1f",
              wcols=(0, 1024))
    emit_proj("ln3b", D, "l1f_Wqkv", 1024, N, wreg2c, out_bf="kT1f",
              wcols=(1024, 2048))
    emit_proj("ln3b", D, "l1f_Wqkv", 1024, N, wreg2c, out_bf="vT1f",
              wcols=(2048, 3072))
    emit_proj("peT", D, "l1f_Wr", 1024, PE_W, wreg2c, out_bf="rTs")
    reg.close()

    emit_tok_transpose("vT1f", D, N, "vtok1f")

    reg = contextlib.ExitStack()
    vtkf = ctx.sbuf("f_vtok", [128, 32 * 1024], BF16, reg)
    ksb = ctx.sbuf("f_ksb", [64, N], BF16, reg)
    rsb = ctx.sbuf("f_rsb", [64, PE_W], BF16, reg)
    qub = ctx.sbuf("f_qu", [64, 128], BF16, reg)
    qvb = ctx.sbuf("f_qv", [64, 128], BF16, reg)
    qtl = ctx.sbuf("f_qt", [64, 128], BF16, reg)
    scos = ctx.sbuf("f_scos", [128, 4224], F32, reg)
    attb = ctx.sbuf("f_attb", [128, 4224], BF16, reg)
    atk = ctx.sbuf("f_atk", [128, 512], BF16, reg)
    bdgt = ctx.sbuf("f_bdgt", [128, 512], BF16, reg)
    acol = ctx.sbuf("f_acol", [128, 2], F32, reg)
    LDL([(vtkf.ap()[:, c * 4096:(c + 1) * 4096],
          dt["vtok1f"].ap()[:, c * 4096:(c + 1) * 4096]) for c in range(8)])
    for hh in range(16):
        LD1(ksb[:], dt["kT1f"].ap()[hh * 64:hh * 64 + 64, :])
        LD1(rsb[:], dt["rTs"].ap()[hh * 64:hh * 64 + 64, :])
        for qb in range(16):
            q0 = 128 * qb
            KA = q0 + 128
            nwA = (KA + 511) // 512
            KT = KA + 2048
            LD1(qtl[:], dt["qT1f"].ap()[hh * 64:hh * 64 + 64, q0:q0 + 128])
            TS(qub[:], qtl[:], ubT[:, hh:hh + 1], ALU.add)
            TS(qvb[:], qtl[:], vbT[:, hh:hh + 1], ALU.add)
            for (bd_name, u0, Wd) in (("BDa", 4096 - q0, (nwA - 1) * 512 + 640),
                                      ("BDb", 2048 - q0, 2176)):
                for w0 in range(0, Wd, 512):
                    wa = min(512, Wd - w0)
                    MM(lambda t, u0=u0, w0=w0, wa=wa: t.matmul(
                        pb[0][:, 0:wa], qvb[:],
                        rsb[0:64, u0 + w0:u0 + w0 + wa], start=True, stop=True))
                    CP(bst[:, 0:wa], pb[0][:, 0:wa])
                    LD1(dt[bd_name].ap()[:, w0:w0 + wa], bst[:, 0:wa])
            for kw in range(nwA + 4):
                isA = kw < nwA
                if isA:
                    kw0 = kw * 512
                    w = min(512, KA - kw0)
                    ksl = ksb[0:64, kw0:kw0 + w]
                    bd_t = dt["BDa"]
                    dcol = kw0
                else:
                    j = kw - nwA
                    kw0 = j * 512
                    w = 512
                    ksl = ksb[0:64, 2048 + kw0:2048 + kw0 + w]
                    bd_t = dt["BDb"]
                    dcol = kw0
                MM(lambda t, ksl=ksl, w=w: t.matmul(
                    pb[1][:, 0:w], qub[:], ksl, start=True, stop=True))
                ctx.dma("sync", lambda s, bd_t=bd_t, dcol=dcol, w=w:
                        [s.dma_start(out=bdgt[:, 0:w],
                                     in_=AP(bd_t, dcol + 127, [[BDW - 1, 128], [1, w]]))], 1)
                CP(sc[:, 0:w], bdgt[:, 0:w])
                TT(sc[:, 0:w], sc[:, 0:w], pb[1][:, 0:w], ALU.add)
                col0 = kw0 if isA else KA + kw0
                if isA and kw == nwA - 1:
                    a = w // 128 - 1
                    TS(sc[:, 0:w], sc[:, 0:w], 0.125, ALU.mult)
                    TT(scos[:, col0:col0 + w], sc[:, 0:w],
                       dmask[:, a * 512:a * 512 + w], ALU.add)
                elif isA:
                    TS(scos[:, col0:col0 + w], sc[:, 0:w], 0.125, ALU.mult)
                else:
                    TS(sc[:, 0:w], sc[:, 0:w], 0.125, ALU.mult)
                    bcast_row(pb[2], kmrow[0:1, kw0:kw0 + w], w)
                    TT(scos[:, col0:col0 + w], sc[:, 0:w], pb[2][:, 0:w], ALU.add)
            RMAX(acol[:, 0:1], scos[:, 0:KT])
            TS(acol[:, 1:2], acol[:, 0:1], -1.0, ALU.mult)
            ACT(xs[:, 0:KT], scos[:, 0:KT], AF.Exp, bias=acol[:, 1:2])
            RSUM(acol[:, 0:1], xs[:, 0:KT])
            RECIP(acol[:, 1:2], acol[:, 0:1])
            TS(attb[:, 0:KT], xs[:, 0:KT], acol[:, 1:2], ALU.mult)
            nkc = KT // 128
            kc = 0
            gi = 0
            while kc < nkc:
                gn = min(4, nkc - kc)

                def tpg(t, kc=kc, gn=gn):
                    i = None
                    for j in range(gn):
                        i = t.transpose(pbb[3][:, j * 128:(j + 1) * 128],
                                        attb[:, (kc + j) * 128:(kc + j + 1) * 128],
                                        identb[:])
                    return i
                ctx.comp("tensor", tpg)
                CP(atk[:, 0:gn * 128], pbb[3][:, 0:gn * 128])

                def mmc(t, kc=kc, gn=gn, qb=qb, hh=hh, nkc=nkc):
                    i = None
                    for j in range(gn):
                        kcj = kc + j
                        phys = kcj if kcj < qb + 1 else 16 + (kcj - (qb + 1))
                        i = t.matmul(pb[4][:, 0:64],
                                     atk[:, j * 128:(j + 1) * 128],
                                     vtkf[:, phys * 1024 + hh * 64:phys * 1024 + hh * 64 + 64],
                                     start=(kcj == 0), stop=(kcj == nkc - 1))
                    return i
                ctx.comp("tensor", mmc)
                kc += gn
                gi += 1
            CP(bst[:, 0:64], pb[4][:, 0:64])
            LD1(dt["ctxtok"].ap()[:, qb * 1024 + hh * 64:qb * 1024 + hh * 64 + 64],
                bst[:, 0:64])
    reg.close()

    for qb in range(16):
        for f4 in range(2):
            LD1(tstg[:, 0:512],
                dt["ctxtok"].ap()[:, qb * 1024 + f4 * 512:qb * 1024 + (f4 + 1) * 512])

            def tpc(t, _=None):
                i = None
                for j in range(4):
                    i = t.transpose(pbb[5][:, j * 128:(j + 1) * 128],
                                    tstg[:, j * 128:(j + 1) * 128], identb[:])
                return i
            ctx.comp("tensor", tpc)
            CP(bst[:, 0:512], pbb[5][:, 0:512])
            LD1(dt["ctxT"].ap()[f4 * 512:(f4 + 1) * 512, qb * 128:(qb + 1) * 128]
                .rearrange("(f p) n -> p f n", p=128),
                bst[:, 0:512].rearrange("p (f n) -> p f n", n=128))

    reg = contextlib.ExitStack()
    wregf = ctx.sbuf("wregf", [128, 8 * 4096], BF16, reg)
    emit_proj("ctxT", D, "l1f_Wo", D, HN, wregf, res="ln3f", res_out="h0")
    reg.close()

    emit_ln("h0", HN, "h1b", out_f32="h1f", gb_idx=0)

    reg = contextlib.ExitStack()
    wregf1 = ctx.sbuf("wregf1", [128, 8 * 4096], BF16, reg)
    emit_proj("h1b", D, "l1f_Wff1", 4096, HN, wregf1, out_bf="ffh",
              bias="l1f_bff1", act="relu")
    reg.close()
    reg = contextlib.ExitStack()
    wregf2 = ctx.sbuf("wregf2", [128, 32 * 1024], BF16, reg)
    emit_proj("ffh", 4096, "l1f_Wff2", D, HN, wregf2, bias="l1f_bff2",
              res="h1f", res_out="h2pre")
    reg.close()

    emit_ln("h2pre", HN, "h2b", out_f32="h2f", gb_idx=1)

    reg = contextlib.ExitStack()
    gwsb1 = ctx.sbuf("gwsb1", [128, 64], F32, reg)
    xg1 = ctx.sbuf("xg1", [128, 8 * 128], F32, reg)
    g81 = ctx.sbuf("g81", [128, 8], F32, reg)
    gcol1 = ctx.sbuf("gcol1", [128, 4], F32, reg)
    cwt1 = ctx.sbuf("cwt1", [8, 128], F32, reg)
    LD1(gwsb1.ap().rearrange("p (c m) -> p c m", m=8),
        dt["l1f_Wg"].ap().rearrange("(c p) m -> p c m", p=128))
    for tt in range(16):
        LD1(xg1.ap().rearrange("p (c n) -> p c n", n=128),
            dt["h2f"].ap()[:, tt * 128:(tt + 1) * 128]
            .rearrange("(c p) n -> p c n", p=128))

        def mlg1(t, _=None):
            i = None
            for c in range(8):
                i = t.matmul(pb[0][:, 0:8], xg1[:, c * 128:(c + 1) * 128],
                             gwsb1[:, c * 8:(c + 1) * 8],
                             start=(c == 0), stop=(c == 7))
            return i
        ctx.comp("tensor", mlg1)
        RMAX(gcol1[:, 0:1], pb[0][:, 0:8])
        TS(gcol1[:, 1:2], gcol1[:, 0:1], -1.0, ALU.mult)
        ACT(g81[:], pb[0][:, 0:8], AF.Exp, bias=gcol1[:, 1:2])
        RSUM(gcol1[:, 0:1], g81[:])
        RECIP(gcol1[:, 1:2], gcol1[:, 0:1])
        TS(g81[:], g81[:], gcol1[:, 1:2], ALU.mult)
        RMAX(gcol1[:, 0:1], g81[:])
        TS(sc[:, 0:8], g81[:], gcol1[:, 0:1], ALU.is_ge)
        TT(sc[:, 8:16], sc[:, 0:8], g81[:], ALU.mult)
        MM(lambda t, _=None: t.transpose(pb[1][0:8, 0:128], sc[:, 8:16], identf[:]))
        CP(cwt1[:], pb[1][0:8, 0:128])
        LD1(dt["cw1"].ap()[0:8, tt * 128:(tt + 1) * 128], cwt1[:])
    reg.close()

    LDL([(dt["h3pre"].ap()[c * 128:(c + 1) * 128, :],
          dt["h2f"].ap()[c * 128:(c + 1) * 128, :]) for c in range(8)])
    emit_moe("h2b", HN, 8, 4096, "l1f_We1", None, "l1f_We2", "cw1", "h3pre",
             256, "m1", relu_bias="l1f_be1", be2_name="l1f_be2")

    emit_ln("h3pre", HN, "h3b", gb_idx=2)

    reg = contextlib.ExitStack()
    wrego = ctx.sbuf("wrego", [128, 8 * 1024], BF16, reg)
    emit_proj("h3b", D, "l1f_Wout", D, HN, wrego, res="res3", res_out="y")
    reg.close()

    for nm in DUMPS:
        shape, dty = dsh[nm]
        du = nc.dram_tensor("dump_" + nm, list(shape), dty, kind="ExternalOutput")
        R = shape[0]
        step = 128 if R >= 128 else R
        LDL([(du.ap()[r0:r0 + step, :], dt[nm].ap()[r0:r0 + step, :])
             for r0 in range(0, R, step)])
    ctx.fence("sync")
    ctx.emit(block)
    return nc


_NC_CACHE = []


def kernel(**inp):
    maps = prep_in_maps(inp)
    if not _NC_CACHE:
        _NC_CACHE.append(build_nc())
    nc = _NC_CACHE[0]
    res = run_bass_kernel_spmd(nc, maps, core_ids=list(range(8)))
    out = np.zeros((B, S, D), np.float32)
    for c in range(8):
        b, h = divmod(c, 2)
        y = np.asarray(res.results[c]["y"], np.float32)
        out[b, h * HN:(h + 1) * HN, :] = y.T
    return out


# version 04
# speedup vs baseline: 1.0868x; 1.0868x over previous
import contextlib
import numpy as np
import ml_dtypes
import concourse.bass as bass
import concourse.mybir as mybir
from concourse.bass import AP
from concourse.bass_utils import run_bass_kernel_spmd

F32 = mybir.dt.float32
BF16 = mybir.dt.bfloat16
AF = mybir.ActivationFunctionType
ALU = mybir.AluOpType

B, S, D = 4, 4096, 1024
N = 4096
HN = 2048
PE_W = 6656
BDW = 2304

bf16 = ml_dtypes.bfloat16


def _bf(a):
    return np.ascontiguousarray(np.asarray(a, dtype=np.float32).astype(bf16))


def _f32(a):
    return np.ascontiguousarray(np.asarray(a), dtype=np.float32)


def _cols(v):
    return np.ascontiguousarray(np.asarray(v, np.float32).reshape(-1, 128).T)


def prep_in_maps(inp):
    sh = {}
    sh["l0_WqT"] = _bf(np.asarray(inp["L0_attn_Wq"]).T)
    sh["l0_WkT"] = _bf(np.asarray(inp["L0_attn_Wk"]).T)
    sh["l0_WvT"] = _bf(np.asarray(inp["L0_attn_Wv"]).T)
    sh["l0_WoT"] = _bf(np.asarray(inp["L0_attn_Wo"]).T)
    sh["l0_bq"] = _cols(inp["L0_attn_bq"])
    sh["l0_bk"] = _cols(inp["L0_attn_bk"])
    sh["l0_bv"] = _cols(inp["L0_attn_bv"])
    sh["l0_bo"] = _cols(inp["L0_attn_bo"])
    F = 1408
    wg = np.asarray(inp["L0_ffn_w_gate"]); wu = np.asarray(inp["L0_ffn_w_up"])
    wd = np.asarray(inp["L0_ffn_w_down"])
    swg = np.asarray(inp["L0_ffn_sw_gate"]); swu = np.asarray(inp["L0_ffn_sw_up"])
    swd = np.asarray(inp["L0_ffn_sw_down"])
    wg_all = np.zeros((10, D, F), np.float32)
    wu_all = np.zeros((10, D, F), np.float32)
    wd_all = np.zeros((10, F, D), np.float32)
    wg_all[:8] = wg; wu_all[:8] = wu; wd_all[:8] = wd
    wg_all[8] = swg[:, :F]; wu_all[8] = swu[:, :F]; wd_all[8] = swd[:F]
    wg_all[9] = swg[:, F:]; wu_all[9] = swu[:, F:]; wd_all[9] = swd[F:]
    sh["l0_wg"] = _bf(wg_all); sh["l0_wu"] = _bf(wu_all); sh["l0_wd"] = _bf(wd_all)
    sh["l0_gwT"] = _f32(np.asarray(inp["L0_ffn_gate_w"]).T)
    cw0 = np.zeros((10, N), np.float32); cw0[8:] = 1.0
    sh["cw0_init"] = cw0
    sh["l1a_WqT"] = _bf(np.asarray(inp["L1_attn_Wq"]).T)
    sh["l1a_WkT"] = _bf(np.asarray(inp["L1_attn_Wk"]).T)
    sh["l1a_WvT"] = _bf(np.asarray(inp["L1_attn_Wv"]).T)
    sh["l1a_WoT"] = _bf(np.asarray(inp["L1_attn_Wo"]).T)
    sh["l1f_Wqkv"] = _bf(inp["L1_ffn_Wqkv"])
    sh["l1f_Wo"] = _bf(inp["L1_ffn_Wo"])
    sh["l1f_Wr"] = _bf(inp["L1_ffn_Wr"])
    sh["l1f_ubT"] = _f32(np.asarray(inp["L1_ffn_u_bias"]).T)
    sh["l1f_vbT"] = _f32(np.asarray(inp["L1_ffn_v_bias"]).T)
    gb = np.zeros((128, 48), np.float32)
    for i, nm in enumerate(["ln1_g", "ln1_b", "ln2_g", "ln2_b", "ln3_g", "ln3_b"]):
        gb[:, i * 8:(i + 1) * 8] = _cols(inp["L1_ffn_" + nm])
    sh["l1f_gb"] = gb
    sh["l1f_Wff1"] = _bf(inp["L1_ffn_Wff1"])
    sh["l1f_Wff2"] = _bf(inp["L1_ffn_Wff2"])
    sh["l1f_bff1"] = _cols(inp["L1_ffn_bff1"])
    sh["l1f_bff2"] = _cols(inp["L1_ffn_bff2"])
    sh["l1f_Wg"] = _f32(inp["L1_ffn_Wg"])
    sh["l1f_We1"] = _bf(inp["L1_ffn_We1"])
    sh["l1f_We2"] = _bf(inp["L1_ffn_We2"])
    be1 = np.asarray(inp["L1_ffn_be1"])
    sh["l1f_be1"] = np.ascontiguousarray(
        be1.reshape(8, 32, 128).transpose(2, 0, 1).reshape(128, 256))
    sh["l1f_be2"] = _f32(inp["L1_ffn_be2"])
    sh["l1f_Wout"] = _bf(inp["L1_ffn_Wout"])
    invf = 1.0 / (10000.0 ** (np.arange(0, D, 2, dtype=np.float64) / D))
    dist = 4223 - np.arange(PE_W, dtype=np.float64)
    valid = (dist >= 0) & (dist <= S - 1)
    ang = dist[None, :] * invf[:, None]
    peT = np.concatenate([np.sin(ang), np.cos(ang)], axis=0)
    peT[:, ~valid] = 0.0
    sh["peT"] = np.ascontiguousarray(peT.astype(bf16))
    sh["ident_b"] = np.eye(128, dtype=np.float32).astype(bf16)
    sh["ident_f"] = np.eye(128, dtype=np.float32)
    sh["onescol"] = np.ones((128, 1), np.float32)
    sh["onescolb"] = np.ones((128, 1), np.float32).astype(bf16)
    sh["onesrow"] = np.ones((1, 512), np.float32)
    sh["triu"] = np.triu(np.ones((128, 128), np.float32))
    dm = np.zeros((128, 4 * 512), np.float32)
    for a in range(4):
        w = 128 * (a + 1)
        p = np.arange(128)[:, None]; f = np.arange(512)[None, :]
        dm[:, a * 512:(a + 1) * 512] = np.where(f > p + (w - 128), -1e30, 0.0)
    sh["diagmask"] = dm
    x = np.asarray(inp["L0_attn_x"])
    maps = []
    for c in range(8):
        b, h = c // 2, c % 2
        m = dict(sh)
        xT = np.asarray(x[b], np.float32).T
        if h == 0:
            m["x"] = _f32(xT)
        else:
            m["x"] = _f32(np.concatenate([xT[:, HN:], xT[:, :HN]], axis=1))
        lg = (np.arange(32) + (16 if h else 0)) % 32
        m["M32"] = np.ascontiguousarray((lg[:, None] < lg[None, :]).astype(np.float32))
        mr = np.zeros((1, HN), np.float32)
        if h == 0:
            mr[:] = -1e30
        m["kmaskrow"] = mr
        maps.append(m)
    return maps


ENGS = ("tensor", "vector", "scalar", "sync", "gpsimd")
XSEMS = ("LS0", "LS1", "GW0", "GW1", "ST0")
SEMS = ENGS + XSEMS


class Ctx:

    def __init__(self, nc, stack):
        self.nc, self.stack = nc, stack
        self.sems = {e: stack.enter_context(nc.semaphore("S_" + e)) for e in SEMS}
        self.items = []
        self.counts = {e: 0 for e in SEMS}
        self.eng_last = {e: {o: 0 for o in SEMS} for e in ENGS}
        self.writers = {}
        self.readers = {}

    def sbuf(self, name, shape, dtype, region=None):
        st = region if region is not None else self.stack
        return st.enter_context(self.nc.sbuf_tensor(name, shape, dtype))

    def _add(self, eng, fn, n_inc, rd=None, wr=None, sem=None):
        sem = sem if sem is not None else eng
        deps = []
        if rd is None and wr is None:
            for o in SEMS:
                if self.counts[o]:
                    deps.append((o, self.counts[o]))
            self.writers["__L__"] = (sem, self.counts[sem] + n_inc)
        else:
            rd = rd or ()
            wr = wr or ()
            if "__L__" in self.writers:
                deps.append(self.writers["__L__"])
            for t in rd:
                if t in self.writers:
                    deps.append(self.writers[t])
            for t in wr:
                if t in self.writers:
                    deps.append(self.writers[t])
                for s, c in self.readers.get(t, {}).items():
                    deps.append((s, c))
            after = (sem, self.counts[sem] + n_inc)
            for t in wr:
                self.writers[t] = after
                self.readers[t] = {}
            for t in rd:
                m = self.readers.setdefault(t, {})
                if after[1] > m.get(sem, 0):
                    m[sem] = after[1]
        waits = []
        for (s, c) in deps:
            if s == eng and eng == "tensor":
                continue
            if c > self.counts[s]:
                c = self.counts[s]
            if c > self.eng_last[eng][s]:
                waits.append((s, c))
                self.eng_last[eng][s] = c
        self.items.append((eng, fn, n_inc, waits, sem))
        self.counts[sem] += n_inc

    def comp(self, eng, fn, rd=None, wr=None):
        self._add(eng, fn, 1, rd, wr)

    def dma(self, eng, fn, n, rd=None, wr=None, sem=None):
        self._add(eng, fn, 16 * n, rd, wr, sem)

    def fence(self, eng):
        self._add(eng, None, 0)

    def emit(self, block):
        sems = self.sems
        for e in ENGS:
            items = [(fn, n, w, sm) for (eng, fn, n, w, sm) in self.items
                     if eng == e]

            def section(h, e=e, items=items):
                for fn, n_inc, waits, sm in items:
                    for (o, v) in waits:
                        h.wait_ge(sems[o], v)
                    if n_inc == 0:
                        continue
                    if n_inc == 1:
                        fn(h).then_inc(sems[sm], 1)
                    else:
                        instrs = fn(h)
                        assert len(instrs) * 16 == n_inc, (e, len(instrs), n_inc)
                        for i in instrs:
                            i.then_inc(sems[sm], 16)
            getattr(block, e)(section)


DUMPS = []
PHASE_LIMIT = 99


def build_nc():
    nc = bass.Bass("TRN2", target_bir_lowering=False, detect_race_conditions=False)
    stack = contextlib.ExitStack()
    dt = {}
    dsh = {}

    def din(name, shape, dtype):
        dt[name] = nc.dram_tensor(name, list(shape), dtype, kind="ExternalInput")
        dsh[name] = (shape, dtype)

    def dint(name, shape, dtype):
        dt[name] = nc.dram_tensor(name, list(shape), dtype, kind="Internal")
        dsh[name] = (shape, dtype)

    din("x", [D, N], F32)
    y_d = nc.dram_tensor("y", [D, HN], F32, kind="ExternalOutput")
    dt["y"] = y_d
    for nm in ("l0_WqT", "l0_WkT", "l0_WvT", "l0_WoT"):
        din(nm, [D, D], BF16)
    for nm in ("l0_bq", "l0_bk", "l0_bv", "l0_bo"):
        din(nm, [128, 8], F32)
    din("l0_wg", [10, D, 1408], BF16)
    din("l0_wu", [10, D, 1408], BF16)
    din("l0_wd", [10, 1408, D], BF16)
    din("l0_gwT", [D, 8], F32)
    din("cw0_init", [10, N], F32)
    din("l1a_WqT", [D, 256], BF16); din("l1a_WkT", [D, 256], BF16)
    din("l1a_WvT", [D, 256], BF16); din("l1a_WoT", [256, D], BF16)
    din("l1f_Wqkv", [D, 3072], BF16); din("l1f_Wo", [D, D], BF16)
    din("l1f_Wr", [D, D], BF16)
    din("l1f_ubT", [64, 16], F32); din("l1f_vbT", [64, 16], F32)
    din("l1f_gb", [128, 48], F32)
    din("l1f_Wff1", [D, 4096], BF16); din("l1f_Wff2", [4096, D], BF16)
    din("l1f_bff1", [128, 32], F32); din("l1f_bff2", [128, 8], F32)
    din("l1f_Wg", [D, 8], F32)
    din("l1f_We1", [8, D, 4096], BF16); din("l1f_We2", [8, 4096, D], BF16)
    din("l1f_be1", [128, 256], F32); din("l1f_be2", [8, D], F32)
    din("l1f_Wout", [D, D], BF16)
    din("peT", [D, PE_W], BF16)
    din("ident_b", [128, 128], BF16); din("ident_f", [128, 128], F32)
    din("onescol", [128, 1], F32); din("onescolb", [128, 1], BF16)
    din("onesrow", [1, 512], F32)
    din("triu", [128, 128], F32)
    din("diagmask", [128, 2048], F32)
    din("M32", [32, 32], F32)
    din("kmaskrow", [1, HN], F32)

    dint("ln0b", [D, N], BF16)
    dint("q0b", [D, N], BF16); dint("k0b", [D, N], BF16); dint("v0b", [D, N], BF16)
    dint("k0tok", [128, 32 * 1024], BF16); dint("v0tok", [128, 32 * 1024], BF16)
    dint("kv0b", [D, D], BF16); dint("ksum0", [128, 8], F32)
    dint("c0b", [D, N], BF16)
    dint("res1", [D, N], F32)
    dint("ln1b", [D, N], BF16); dint("ln1f", [D, N], F32)
    dint("cw0", [10, N], F32)
    dint("res2", [D, N], F32)
    dint("ln2b", [D, N], BF16)
    dint("q1n", [256, N], F32); dint("k1n", [256, N], F32); dint("v1T", [256, N], F32)
    dint("dkv1", [64, 4 * 2048], F32)
    dint("dk1T", [128, 64], F32)
    dint("st1", [64, 4 * 2048], F32)
    dint("kstT", [64, 128], F32)
    dint("y1b", [256, N], BF16)
    dint("res3", [D, N], F32)
    dint("ln3b", [D, N], BF16); dint("ln3f", [D, N], F32)
    dint("qT1f", [D, HN], BF16)
    dint("kT1f", [D, N], BF16); dint("vT1f", [D, N], BF16)
    dint("rTs", [D, PE_W], BF16)
    dint("vtok1f", [128, 32 * 1024], BF16)
    dint("BDa", [128, BDW], BF16)
    dint("BDb", [128, BDW], BF16)
    dint("ctxtok", [128, 16 * 1024], BF16)
    dint("ctxT", [D, HN], BF16)
    dint("h0", [D, HN], F32)
    dint("h1b", [D, HN], BF16); dint("h1f", [D, HN], F32)
    dint("ffh", [4096, HN], BF16)
    dint("h2pre", [D, HN], F32)
    dint("h2b", [D, HN], BF16); dint("h2f", [D, HN], F32)
    dint("cw1", [8, HN], F32)
    dint("h3pre", [D, HN], F32)
    dint("h3b", [D, HN], BF16)

    block = stack.enter_context(nc.Block())
    ctx = Ctx(nc, stack)
    pb = [stack.enter_context(nc.psum_tensor("pb%d" % i, [128, 512], F32))
          for i in range(8)]
    pbb = [p.ap().bitcast(BF16) for p in pb]

    onescol = ctx.sbuf("c_onescol", [128, 1], F32)
    onescolb = ctx.sbuf("c_onescolb", [128, 1], BF16)
    onesrow = ctx.sbuf("c_onesrow", [1, 512], F32)
    identb = ctx.sbuf("c_identb", [128, 128], BF16)
    identf = ctx.sbuf("c_identf", [128, 128], F32)
    triu = ctx.sbuf("c_triu", [128, 128], F32)
    ubT = ctx.sbuf("c_ubT", [64, 16], F32)
    vbT = ctx.sbuf("c_vbT", [64, 16], F32)
    gbcols = ctx.sbuf("c_gb", [128, 48], F32)
    m32sb = ctx.sbuf("c_m32", [32, 32], F32)

    xs = ctx.sbuf("g_xs", [128, 8 * 512], F32)
    sc = ctx.sbuf("g_sc", [128, 4 * 512], F32)
    rows = ctx.sbuf("g_rows", [1, 3 * 512], F32)
    bst = ctx.sbuf("g_bst", [128, 512], BF16)
    tstg = ctx.sbuf("g_tstg", [128, 512], BF16)

    def TT(out, a, b, op, rd=None, wr=None):
        ctx.comp("vector", lambda v, out=out, a=a, b=b, op=op:
                 v.tensor_tensor(out=out, in0=a, in1=b, op=op), rd=rd, wr=wr)

    def TS(out, a, s1, op, rd=None, wr=None):
        ctx.comp("vector", lambda v, out=out, a=a, s1=s1, op=op:
                 v.tensor_scalar(out=out, in0=a, scalar1=s1, scalar2=None, op0=op),
                 rd=rd, wr=wr)

    def CP(out, a, rd=None, wr=None):
        ctx.comp("vector", lambda v, out=out, a=a: v.tensor_copy(out=out, in_=a),
                 rd=rd, wr=wr)

    def RMAX(out, a, rd=None, wr=None):
        ctx.comp("vector", lambda v, out=out, a=a:
                 v.reduce_max(out=out, in_=a, axis=mybir.AxisListType.X),
                 rd=rd, wr=wr)

    def RSUM(out, a, rd=None, wr=None):
        ctx.comp("vector", lambda v, out=out, a=a:
                 v.reduce_sum(out=out, in_=a, axis=mybir.AxisListType.X),
                 rd=rd, wr=wr)

    def RECIP(out, a, rd=None, wr=None):
        ctx.comp("vector", lambda v, out=out, a=a: v.reciprocal(out=out, in_=a),
                 rd=rd, wr=wr)

    def ACT(out, a, func, bias=None, scale=1.0, rd=None, wr=None):
        if bias is None:
            ctx.comp("scalar", lambda s, out=out, a=a, func=func, scale=scale:
                     s.activation(out=out, in_=a, func=func, scale=scale),
                     rd=rd, wr=wr)
        else:
            ctx.comp("scalar", lambda s, out=out, a=a, func=func, bias=bias,
                     scale=scale:
                     s.activation(out=out, in_=a, func=func, bias=bias, scale=scale),
                     rd=rd, wr=wr)

    def SMUL(out, a, c, rd=None, wr=None):
        ctx.comp("scalar", lambda s, out=out, a=a, c=c: s.mul(out, a, c),
                 rd=rd, wr=wr)

    def SCP(out, a, rd=None, wr=None):
        ctx.comp("scalar", lambda s, out=out, a=a: s.copy(out=out, in_=a),
                 rd=rd, wr=wr)

    def MM(fn, rd=None, wr=None):
        ctx.comp("tensor", fn, rd=rd, wr=wr)

    def LD1(out, in_, rd=None, wr=None, q="sync", sem=None):
        ctx.dma(q, lambda s, out=out, in_=in_:
                [s.dma_start(out=out, in_=in_)], 1, rd=rd, wr=wr, sem=sem)

    def LDL(pairs, rd=None, wr=None, q="sync", sem=None):
        ctx.dma(q, lambda s, pairs=pairs:
                [s.dma_start(out=o, in_=i) for (o, i) in pairs], len(pairs),
                rd=rd, wr=wr, sem=sem)

    def GDL(pairs, rd=None, wr=None, sem=None):
        ctx.dma("gpsimd", lambda g, pairs=pairs:
                [g.dma_start(out=o, in_=i) for (o, i) in pairs], len(pairs),
                rd=rd, wr=wr, sem=sem)

    def GACC(out, in_, rd=None, wr=None):
        ctx.dma("gpsimd", lambda g, out=out, in_=in_:
                [g.dma_start(out=out, in_=in_, accum_op=ALU.add)], 1,
                rd=rd, wr=wr)

    def bcast_row(bank, row_ap, width, np_=128, rd=None, wr=None):
        MM(lambda t, bank=bank, row_ap=row_ap, width=width, np_=np_:
           t.matmul(bank[0:np_, 0:width], onesrow[0:1, 0:np_], row_ap,
                    start=True, stop=True), rd=rd, wr=wr)

    LDL([(onescol[:], dt["onescol"].ap()),
         (onescolb[:], dt["onescolb"].ap()),
         (onesrow[:], dt["onesrow"].ap()),
         (identb[:], dt["ident_b"].ap()),
         (identf[:], dt["ident_f"].ap()),
         (triu[:], dt["triu"].ap()),
         (ubT[:], dt["l1f_ubT"].ap()),
         (vbT[:], dt["l1f_vbT"].ap()),
         (gbcols[:], dt["l1f_gb"].ap()),
         (m32sb[:], dt["M32"].ap()),
         (dt["cw0"].ap(), dt["cw0_init"].ap())])


    def emit_ln(src, Nc, out_bf, out_f32=None, gb_idx=None):
        src_a, out_a = dt[src].ap(), dt[out_bf].ap()
        of_a = dt[out_f32].ap() if out_f32 else None
        for w in range(Nc // 512):
            w0 = w * 512
            LD1(xs.ap().rearrange("p (c n) -> p c n", n=512),
                src_a[:, w0:w0 + 512].rearrange("(c p) n -> p c n", p=128))
            MM(lambda t, _=None: [
                t.matmul(pb[0][0:1, :], onescol[:], xs[:, c * 512:(c + 1) * 512],
                         start=(c == 0), stop=(c == 7)) for c in range(8)][-1])
            for half in range(2):
                cs = [0, 1, 2, 3] if half == 0 else [4, 5, 6, 7]

                def sqw(s, cs=cs):
                    i = None
                    for j, c in enumerate(cs):
                        i = s.activation(out=sc[:, j * 512:(j + 1) * 512],
                                         in_=xs[:, c * 512:(c + 1) * 512],
                                         func=AF.Square)
                    return i
                ctx.comp("scalar", sqw)

                def mmq(t, cs=cs, half=half):
                    i = None
                    for j in range(4):
                        i = t.matmul(pb[1][0:1, :], onescol[:],
                                     sc[:, j * 512:(j + 1) * 512],
                                     start=(half == 0 and j == 0),
                                     stop=(half == 1 and j == 3))
                    return i
                ctx.comp("tensor", mmq)
            SMUL(rows[0:1, 0:512], pb[0][0:1, :], 1.0 / D)
            SMUL(rows[0:1, 512:1024], pb[1][0:1, :], 1.0 / D)
            TT(rows[0:1, 1024:1536], rows[0:1, 0:512], rows[0:1, 0:512], ALU.mult)
            TT(rows[0:1, 512:1024], rows[0:1, 512:1024], rows[0:1, 1024:1536],
               ALU.subtract)
            TS(rows[0:1, 512:1024], rows[0:1, 512:1024], 1e-5, ALU.add)
            ACT(rows[0:1, 1024:1536], rows[0:1, 512:1024], AF.Ln)
            ACT(rows[0:1, 512:1024], rows[0:1, 1024:1536], AF.Exp, scale=-0.5)
            TT(rows[0:1, 1024:1536], rows[0:1, 0:512], rows[0:1, 512:1024], ALU.mult)
            TS(rows[0:1, 1024:1536], rows[0:1, 1024:1536], -1.0, ALU.mult)
            bcast_row(pb[2], rows[0:1, 512:1024], 512)
            bcast_row(pb[3], rows[0:1, 1024:1536], 512)
            for c in range(8):
                TT(sc[:, 0:512], xs[:, c * 512:(c + 1) * 512], pb[2][:, :], ALU.mult)
                TT(sc[:, 512:1024], sc[:, 0:512], pb[3][:, :], ALU.add)
                if gb_idx is not None:
                    g0 = gb_idx * 16
                    TS(sc[:, 512:1024], sc[:, 512:1024],
                       gbcols[:, g0 + c:g0 + c + 1], ALU.mult)
                    TS(sc[:, 512:1024], sc[:, 512:1024],
                       gbcols[:, g0 + 8 + c:g0 + 8 + c + 1], ALU.add)
                CP(bst[:, 0:512], sc[:, 512:1024])
                LD1(out_a[c * 128:(c + 1) * 128, w0:w0 + 512], bst[:, 0:512])
                if of_a is not None:
                    LD1(of_a[c * 128:(c + 1) * 128, w0:w0 + 512], sc[:, 512:1024])

    def load_w(wreg, w_name, K, M, cols=None):
        a = dt[w_name].ap()
        if cols is not None:
            a = a[:, cols[0]:cols[1]]
            M = cols[1] - cols[0]
        pairs = []
        for c in range(K // 128):
            pairs.append((wreg.ap()[:, c * M:(c + 1) * M],
                          a[c * 128:(c + 1) * 128, :]))
        LDL(pairs)

    def emit_proj(src, K, w_name, M, Nc, wreg, out_bf=None, wcols=None,
                  bias=None, act=None, res=None, res_out=None, srcoff=0,
                  bias_sb=None):
        load_w(wreg, w_name, K, M, cols=wcols)
        src_a = dt[src].ap()
        nkc = K // 128
        regp = contextlib.ExitStack()
        stg = ctx.sbuf("stg_%s" % (out_bf or res_out), [128, nkc * 512], BF16,
                       regp)
        for w in range(Nc // 512):
            w0 = srcoff + w * 512
            LD1(stg.ap().rearrange("p (c n) -> p c n", n=512),
                src_a[:, w0:w0 + 512].rearrange("(c p) n -> p c n", p=128))
            for mc in range(M // 128):
                bank = pb[mc % 2]

                def mmp(t, mc=mc, bank=bank, nkc=nkc):
                    i = None
                    for kc in range(nkc):
                        i = t.matmul(bank[:, :],
                                     wreg[:, kc * M + mc * 128:kc * M + (mc + 1) * 128],
                                     stg[:, kc * 512:(kc + 1) * 512],
                                     start=(kc == 0), stop=(kc == nkc - 1))
                    return i
                ctx.comp("tensor", mmp)
                bcol = None
                if bias is not None:
                    bsb = bias_sb if bias_sb is not None else biascols[bias]
                    bcol = bsb[:, mc:mc + 1]
                if act == "elu1":
                    if bcol is not None:
                        TS(sc[:, 0:512], bank[:, :], bcol, ALU.add)
                    else:
                        CP(sc[:, 0:512], bank[:, :])
                    TS(sc[:, 512:1024], sc[:, 0:512], 0.0, ALU.min)
                    ACT(sc[:, 1024:1536], sc[:, 512:1024], AF.Exp)
                    ACT(sc[:, 512:1024], sc[:, 0:512], AF.Relu)
                    TT(sc[:, 0:512], sc[:, 512:1024], sc[:, 1024:1536], ALU.add)
                elif act == "relu":
                    ACT(sc[:, 0:512], bank[:, :], AF.Relu, bias=bcol)
                else:
                    if bcol is not None:
                        TS(sc[:, 0:512], bank[:, :], bcol, ALU.add)
                    else:
                        CP(sc[:, 0:512], bank[:, :])
                if res is not None:
                    LD1(sc[:, 512:1024],
                        dt[res].ap()[mc * 128:(mc + 1) * 128, w * 512:w * 512 + 512])
                    TT(sc[:, 0:512], sc[:, 0:512], sc[:, 512:1024], ALU.add)
                    LD1(dt[res_out].ap()[mc * 128:(mc + 1) * 128, w * 512:w * 512 + 512],
                        sc[:, 0:512])
                    if out_bf is not None:
                        CP(bst[:, 0:512], sc[:, 0:512])
                        LD1(dt[out_bf].ap()[mc * 128:(mc + 1) * 128,
                                            w * 512:w * 512 + 512], bst[:, 0:512])
                else:
                    CP(bst[:, 0:512], sc[:, 0:512])
                    LD1(dt[out_bf].ap()[mc * 128:(mc + 1) * 128,
                                        w * 512:w * 512 + 512], bst[:, 0:512])
        regp.close()

    def emit_tok_transpose(src, R, Nc, dst):
        src_a, dst_a = dt[src].ap(), dt[dst].ap()
        for t4 in range(Nc // 512):
            for rc in range(R // 128):
                LD1(tstg[:, 0:512],
                    src_a[rc * 128:(rc + 1) * 128, t4 * 512:(t4 + 1) * 512])

                def tp(t, _=None):
                    i = None
                    for j in range(4):
                        i = t.transpose(pbb[4][:, j * 128:(j + 1) * 128],
                                        tstg[:, j * 128:(j + 1) * 128], identb[:])
                    return i
                ctx.comp("tensor", tp)
                CP(bst[:, 0:512], pbb[4][:, 0:512])
                LD1(dst_a.rearrange("p (t r) -> p t r", r=R)
                    [:, t4 * 4:t4 * 4 + 4, rc * 128:(rc + 1) * 128],
                    bst[:, 0:512].rearrange("p (t r) -> p t r", r=128))

    biascols = {}
    for nm in ("l0_bq", "l0_bk", "l0_bv", "l0_bo", "l1f_bff1", "l1f_bff2",
               "l1f_be1"):
        bw = {"l0_bq": 8, "l0_bk": 8, "l0_bv": 8, "l0_bo": 8,
              "l1f_bff1": 32, "l1f_bff2": 8, "l1f_be1": 256}[nm]
        biascols[nm] = ctx.sbuf("bc_" + nm, [128, bw], F32)
        LD1(biascols[nm][:], dt[nm].ap())

    emit_ln("x", N, "ln0b")

    reg = contextlib.ExitStack()
    wreg2 = ctx.sbuf("wreg2", [128, 8 * 1024], BF16, reg)
    emit_proj("ln0b", D, "l0_WqT", D, N, wreg2, out_bf="q0b", bias="l0_bq",
              act="elu1")
    emit_proj("ln0b", D, "l0_WkT", D, N, wreg2, out_bf="k0b", bias="l0_bk",
              act="elu1")
    emit_proj("ln0b", D, "l0_WvT", D, N, wreg2, out_bf="v0b", bias="l0_bv")
    reg.close()

    emit_tok_transpose("k0b", D, N, "k0tok")
    emit_tok_transpose("v0b", D, N, "v0tok")

    reg = contextlib.ExitStack()
    kvkb = ctx.sbuf("kvkb", [128, 512], BF16, reg)
    kvvb = ctx.sbuf("kvvb", [128, 512], BF16, reg)
    for eh in range(2):
        for dg in range(2):
            for tc in range(32):
                LDL([(kvkb[:], dt["k0tok"].ap()[:, tc * 1024 + dg * 512:
                                                tc * 1024 + dg * 512 + 512]),
                     (kvvb[:], dt["v0tok"].ap()[:, tc * 1024 + eh * 512:
                                                tc * 1024 + eh * 512 + 512])])

                def mkv(t, eh=eh, dg=dg, tc=tc):
                    i = None
                    for dc in range(4):
                        i = t.matmul(pb[dc][:, :], kvkb[:, dc * 128:(dc + 1) * 128],
                                     kvvb[:], start=(tc == 0), stop=(tc == 31))
                    if eh == 0:
                        for dc in range(4):
                            i = t.matmul(pb[4][:, dg * 4 + dc:dg * 4 + dc + 1],
                                         kvkb[:, dc * 128:(dc + 1) * 128],
                                         onescolb[:], start=(tc == 0),
                                         stop=(tc == 31))
                    return i
                ctx.comp("tensor", mkv)
            for dc in range(4):
                dci = dg * 4 + dc
                CP(bst[:, 0:512], pb[dc][:, :])
                LD1(dt["kv0b"].ap()[dci * 128:(dci + 1) * 128,
                                    eh * 512:eh * 512 + 512], bst[:, 0:512])
        if eh == 0:
            CP(sc[:, 0:8], pb[4][:, 0:8])
            TS(sc[:, 0:8], sc[:, 0:8], 1e-6, ALU.add)
            LD1(dt["ksum0"].ap(), sc[:, 0:8])
    reg.close()

    reg = contextlib.ExitStack()
    kvsb = ctx.sbuf("kvsb", [128, 8 * 1024], BF16, reg)
    ksp = ctx.sbuf("ksp", [128, 8], F32, reg)
    qstg = ctx.sbuf("qstg", [128, 8 * 512], BF16, reg)
    LDL([(kvsb.ap()[:, c * 1024:(c + 1) * 1024],
          dt["kv0b"].ap()[c * 128:(c + 1) * 128, :]) for c in range(8)])
    LD1(ksp[:], dt["ksum0"].ap())
    for w in range(8):
        w0 = w * 512
        LD1(qstg.ap().rearrange("p (c n) -> p c n", n=512),
            dt["q0b"].ap()[:, w0:w0 + 512].rearrange("(c p) n -> p c n", p=128))
        for c in range(8):
            TS(sc[:, 0:512], qstg[:, c * 512:(c + 1) * 512], ksp[:, c:c + 1],
               ALU.mult)
            MM(lambda t, c=c: t.matmul(pb[0][0:1, :], onescol[:], sc[:, 0:512],
                                       start=(c == 0), stop=(c == 7)))
        RECIP(rows[0:1, 0:512], pb[0][0:1, :])
        bcast_row(pb[1], rows[0:1, 0:512], 512)
        for ec in range(8):
            def mmc(t, ec=ec):
                i = None
                for dc in range(8):
                    i = t.matmul(pb[2][:, :],
                                 kvsb[:, dc * 1024 + ec * 128:dc * 1024 + (ec + 1) * 128],
                                 qstg[:, dc * 512:(dc + 1) * 512],
                                 start=(dc == 0), stop=(dc == 7))
                return i
            ctx.comp("tensor", mmc)
            CP(sc[:, 0:512], pb[2][:, :])
            TT(sc[:, 512:1024], sc[:, 0:512], pb[1][:, :], ALU.mult)
            CP(bst[:, 0:512], sc[:, 512:1024])
            LD1(dt["c0b"].ap()[ec * 128:(ec + 1) * 128, w0:w0 + 512], bst[:, 0:512])
    reg.close()

    reg = contextlib.ExitStack()
    wreg2 = ctx.sbuf("wreg2b", [128, 8 * 1024], BF16, reg)
    emit_proj("c0b", D, "l0_WoT", D, N, wreg2, wcols=None, bias="l0_bo",
              res="x", res_out="res1")
    reg.close()

    emit_ln("res1", N, "ln1b", out_f32="ln1f")

    reg = contextlib.ExitStack()
    gwsb = ctx.sbuf("gwsb", [128, 64], F32, reg)
    xg = ctx.sbuf("xg", [128, 8 * 128], F32, reg)
    g8 = ctx.sbuf("g8", [128, 8], F32, reg)
    gcol = ctx.sbuf("gcol", [128, 4], F32, reg)
    cwt = ctx.sbuf("cwt", [8, 128], F32, reg)
    LD1(gwsb.ap().rearrange("p (c m) -> p c m", m=8),
        dt["l0_gwT"].ap().rearrange("(c p) m -> p c m", p=128))
    for tt in range(32):
        LD1(xg.ap().rearrange("p (c n) -> p c n", n=128),
            dt["ln1f"].ap()[:, tt * 128:(tt + 1) * 128]
            .rearrange("(c p) n -> p c n", p=128))

        def mlg(t, _=None):
            i = None
            for c in range(8):
                i = t.matmul(pb[0][:, 0:8], xg[:, c * 128:(c + 1) * 128],
                             gwsb[:, c * 8:(c + 1) * 8],
                             start=(c == 0), stop=(c == 7))
            return i
        ctx.comp("tensor", mlg)
        RMAX(gcol[:, 0:1], pb[0][:, 0:8])
        TS(gcol[:, 1:2], gcol[:, 0:1], -1.0, ALU.mult)
        ACT(g8[:], pb[0][:, 0:8], AF.Exp, bias=gcol[:, 1:2])
        RSUM(gcol[:, 0:1], g8[:])
        RECIP(gcol[:, 1:2], gcol[:, 0:1])
        TS(g8[:], g8[:], gcol[:, 1:2], ALU.mult)
        RMAX(gcol[:, 0:1], g8[:])
        TS(sc[:, 0:8], g8[:], gcol[:, 0:1], ALU.is_ge)
        TS(sc[:, 0:8], sc[:, 0:8], -1e30, ALU.mult)
        TT(sc[:, 8:16], g8[:], sc[:, 0:8], ALU.add)
        RMAX(gcol[:, 1:2], sc[:, 8:16])
        TT(gcol[:, 2:3], gcol[:, 0:1], gcol[:, 1:2], ALU.add)
        TS(gcol[:, 2:3], gcol[:, 2:3], 1e-20, ALU.add)
        RECIP(gcol[:, 3:4], gcol[:, 2:3])
        TS(gcol[:, 3:4], gcol[:, 3:4], 2.5, ALU.mult)
        TS(sc[:, 0:8], g8[:], gcol[:, 1:2], ALU.is_ge)
        TT(sc[:, 8:16], sc[:, 0:8], g8[:], ALU.mult)
        TS(sc[:, 8:16], sc[:, 8:16], gcol[:, 3:4], ALU.mult)
        MM(lambda t, _=None: t.transpose(pb[1][0:8, 0:128], sc[:, 8:16], identf[:]))
        CP(cwt[:], pb[1][0:8, 0:128])
        LD1(dt["cw0"].ap()[0:8, tt * 128:(tt + 1) * 128], cwt[:])
    reg.close()

    LDL([(dt["res2"].ap()[c * 128:(c + 1) * 128, :],
          dt["res1"].ap()[c * 128:(c + 1) * 128, :]) for c in range(8)])

    def emit_moe(src, Nc, NE, F, wg_name, wu_name, wd_name, cw_name, out_name,
                 tcw, tag, relu_bias=None, be2_name=None):
        nfc = F // 128
        regm = contextlib.ExitStack()
        w1 = ctx.sbuf(tag + "moe_w1", [128, 8 * F], BF16, regm)
        w2 = ctx.sbuf(tag + "moe_w2", [128, 8 * F] if wu_name else [1, 1],
                      BF16 if wu_name else F32, regm)
        wdn = ctx.sbuf(tag + "moe_wd", [128, nfc * 1024], BF16, regm)
        hidb = ctx.sbuf(tag + "moe_hid", [128, nfc * tcw], BF16, regm)
        cwb = ctx.sbuf(tag + "moe_cwb", [128, 2 * tcw], F32, regm)
        cwr = ctx.sbuf(tag + "moe_cwr", [1, 2 * tcw], F32, regm)
        stg = ctx.sbuf(tag + "moe_stg", [128, 2 * 8 * tcw], BF16, regm)
        eob = ctx.sbuf(tag + "moe_eob", [128, 4 * tcw], F32, regm)
        src_a = dt[src].ap()
        wgf = dt[wg_name].ap().rearrange("e d f -> (e d) f")
        wuf = dt[wu_name].ap().rearrange("e d f -> (e d) f") if wu_name else None
        wdf = dt[wd_name].ap().rearrange("e f d -> (e f) d")
        ntc = Nc // tcw
        W = tag + "w"
        X1, X2 = tag + "x1", tag + "x2"
        HD = tag + "hd"
        for e in range(NE):
            wp = [(w1.ap()[:, c * F:(c + 1) * F],
                   wgf[e * D + c * 128:e * D + (c + 1) * 128, :])
                  for c in range(8)]
            if wu_name:
                wp += [(w2.ap()[:, c * F:(c + 1) * F],
                        wuf[e * D + c * 128:e * D + (c + 1) * 128, :])
                       for c in range(8)]
            wp += [(wdn.ap()[:, c * 1024:(c + 1) * 1024],
                    wdf[e * F + c * 128:e * F + (c + 1) * 128, :])
                   for c in range(nfc)]
            GDL(wp, rd=(), wr=(W,), sem="GW0")
            for tc in range(ntc):
                t0 = tc * tcw
                s = tc % 2
                SG = tag + "sg%d" % s
                CB = tag + "cb%d" % s
                stg_s = stg[:, s * 8 * tcw:(s + 1) * 8 * tcw]
                cwr_s = cwr[0:1, s * tcw:(s + 1) * tcw]
                cwb_s = cwb[:, s * tcw:(s + 1) * tcw]
                ctx.dma("sync", lambda q, t0=t0, stg_s=stg_s, cwr_s=cwr_s, e=e,
                        tcw=tcw:
                        [q.dma_start(out=stg_s.rearrange("p (c n) -> p c n", n=tcw),
                                     in_=src_a[:, t0:t0 + tcw]
                                     .rearrange("(c p) n -> p c n", p=128)),
                         q.dma_start(out=cwr_s,
                                     in_=dt[cw_name].ap()[e:e + 1, t0:t0 + tcw])],
                        2, rd=(), wr=(SG,), sem="LS%d" % s)
                bcast_row(pb[6], cwr_s, tcw, rd=(SG,), wr=("p6",))
                CP(cwb_s, pb[6][0:128, 0:tcw], rd=("p6",), wr=(CB,))
                for fc in range(nfc):
                    def mg(t, fc=fc, tcw=tcw, stg_s=stg_s):
                        i = None
                        for dc in range(8):
                            i = t.matmul(pb[4][:, 0:tcw],
                                         w1[:, dc * F + fc * 128:dc * F + (fc + 1) * 128],
                                         stg_s[:, dc * tcw:(dc + 1) * tcw],
                                         start=(dc == 0), stop=(dc == 7))
                        return i
                    ctx.comp("tensor", mg, rd=(W, SG), wr=("p4",))
                    if wu_name:
                        def mu(t, fc=fc, tcw=tcw, stg_s=stg_s):
                            i = None
                            for dc in range(8):
                                i = t.matmul(pb[5][:, 0:tcw],
                                             w2[:, dc * F + fc * 128:dc * F + (fc + 1) * 128],
                                             stg_s[:, dc * tcw:(dc + 1) * tcw],
                                             start=(dc == 0), stop=(dc == 7))
                            return i
                        ctx.comp("tensor", mu, rd=(W, SG), wr=("p5",))
                        ACT(sc[:, 0:tcw], pb[4][:, 0:tcw], AF.Sigmoid,
                            rd=("p4",), wr=(X1,))
                        TT(sc[:, 512:512 + tcw], pb[4][:, 0:tcw], sc[:, 0:tcw],
                           ALU.mult, rd=("p4", X1), wr=(X2,))
                        TT(sc[:, 0:tcw], sc[:, 512:512 + tcw], pb[5][:, 0:tcw],
                           ALU.mult, rd=(X2, "p5"), wr=(X1,))
                    else:
                        ACT(sc[:, 0:tcw], pb[4][:, 0:tcw], AF.Relu,
                            bias=biascols[relu_bias][:, e * 32 + fc:e * 32 + fc + 1],
                            rd=("p4",), wr=(X1,))
                    TT(sc[:, 512:512 + tcw], sc[:, 0:tcw], cwb_s, ALU.mult,
                       rd=(X1, CB), wr=(X2,))
                    CP(hidb[:, fc * tcw:(fc + 1) * tcw], sc[:, 512:512 + tcw],
                       rd=(X2,), wr=(HD,))
                for dg in range(2):
                    def md(t, dg=dg, nfc=nfc, tcw=tcw):
                        i = None
                        for dc in range(4):
                            dci = dg * 4 + dc
                            for fc in range(nfc):
                                i = t.matmul(pb[dc][:, 0:tcw],
                                             wdn[:, fc * 1024 + dci * 128:fc * 1024 + (dci + 1) * 128],
                                             hidb[:, fc * tcw:(fc + 1) * tcw],
                                             start=(fc == 0), stop=(fc == nfc - 1))
                        return i
                    ctx.comp("tensor", md, rd=(HD, W),
                             wr=("p0", "p1", "p2", "p3"))
                    for dc in range(4):
                        dci = dg * 4 + dc
                        EO = tag + "eo%d" % dc
                        CP(eob[:, dc * tcw:(dc + 1) * tcw], pb[dc][:, 0:tcw],
                           rd=("p%d" % dc,), wr=(EO,))
                        GACC(dt[out_name].ap()[dci * 128:(dci + 1) * 128,
                                               t0:t0 + tcw],
                             eob[:, dc * tcw:(dc + 1) * tcw],
                             rd=(EO,), wr=(tag + "a%d_%d" % (tc, dci), EO))
        if be2_name is not None:
            be2sb = xs.ap()[0:8, 0:1024]
            cw8 = xs.ap()[0:8, 1024:1536]
            LD1(be2sb, dt[be2_name].ap())
            for tc in range(Nc // 512):
                t0 = tc * 512
                LD1(cw8, dt[cw_name].ap()[0:8, t0:t0 + 512])
                for dci in range(8):
                    MM(lambda t, dci=dci: t.matmul(
                        pb[dci % 2][:, :], be2sb[:, dci * 128:(dci + 1) * 128],
                        cw8, start=True, stop=True))
                    CP(sc[:, 0:512], pb[dci % 2][:, :])
                    GACC(dt[out_name].ap()[dci * 128:(dci + 1) * 128,
                                           t0:t0 + 512], sc[:, 0:512])
        regm.close()

    emit_moe("ln1b", N, 10, 1408, "l0_wg", "l0_wu", "l0_wd", "cw0", "res2",
             512, "m0")

    if PHASE_LIMIT <= 1:
        ctx.fence("sync")
        ctx.emit(block)
        return nc

    emit_ln("res2", N, "ln2b")

    reg = contextlib.ExitStack()
    wreg1 = ctx.sbuf("wreg1a", [128, 8 * 256], BF16, reg)
    astg = ctx.sbuf("astg", [128, 8 * 512], BF16, reg)
    for (wn, on, do_pu) in (("l1a_WqT", "q1n", True), ("l1a_WkT", "k1n", True),
                            ("l1a_WvT", "v1T", False)):
        load_w(wreg1, wn, D, 256)
        for w in range(8):
            w0 = w * 512
            LD1(astg.ap().rearrange("p (c n) -> p c n", n=512),
                dt["ln2b"].ap()[:, w0:w0 + 512].rearrange("(c p) n -> p c n", p=128))
            for ch in range(2):
                def mmq(t, ch=ch):
                    i = None
                    for kc in range(8):
                        i = t.matmul(pb[0][:, :],
                                     wreg1[:, kc * 256 + ch * 128:kc * 256 + (ch + 1) * 128],
                                     astg[:, kc * 512:(kc + 1) * 512],
                                     start=(kc == 0), stop=(kc == 7))
                    return i
                ctx.comp("tensor", mmq)
                if do_pu:
                    TS(sc[:, 0:512], pb[0][:, :], 0.0, ALU.min)
                    ACT(sc[:, 512:1024], sc[:, 0:512], AF.Exp)
                    ACT(sc[:, 0:512], pb[0][:, :], AF.Relu)
                    TT(sc[:, 1024:1536], sc[:, 0:512], sc[:, 512:1024], ALU.add)
                    ACT(sc[:, 0:512], sc[:, 1024:1536], AF.Square)
                    for hh in range(2):
                        h0_ = hh * 64
                        MM(lambda t, h0_=h0_: t.matmul(
                            pb[1][0:1, 0:512], onescol[h0_:h0_ + 64, 0:1],
                            sc[h0_:h0_ + 64, 0:512], start=True, stop=True))
                        ACT(rows[0:1, 0:512], pb[1][0:1, 0:512], AF.Sqrt)
                        TS(rows[0:1, 0:512], rows[0:1, 0:512], 1e-6, ALU.add)
                        RECIP(rows[0:1, 512:1024], rows[0:1, 0:512])
                        bcast_row(pb[2], rows[0:1, 512:1024], 512, np_=64)
                        TT(sc[h0_:h0_ + 64, 1024:1536], sc[h0_:h0_ + 64, 1024:1536],
                           pb[2][0:64, 0:512], ALU.mult)
                    LD1(dt[on].ap()[ch * 128:(ch + 1) * 128, w0:w0 + 512],
                        sc[:, 1024:1536])
                else:
                    CP(sc[:, 0:512], pb[0][:, :])
                    LD1(dt[on].ap()[ch * 128:(ch + 1) * 128, w0:w0 + 512],
                        sc[:, 0:512])
    reg.close()

    reg = contextlib.ExitStack()
    kch = ctx.sbuf("a1_k", [64, 128], F32, reg)
    vch = ctx.sbuf("a1_v", [64, 128], F32, reg)
    qch = ctx.sbuf("a1_q", [64, 128], F32, reg)
    ktok = ctx.sbuf("a1_kt", [128, 64], F32, reg)
    vtok = ctx.sbuf("a1_vt", [128, 64], F32, reg)
    stt = ctx.sbuf("a1_st", [64, 64], F32, reg)
    kstc = ctx.sbuf("a1_kst", [64, 1], F32, reg)
    stm = ctx.sbuf("a1_stm", [128, 128], F32, reg)
    ybt = ctx.sbuf("a1_y", [64, 128], BF16, reg)
    dkvf = ctx.sbuf("a1_dkvf", [32, 4096], F32, reg)
    dkm = ctx.sbuf("a1_dkm", [32, 64], F32, reg)
    for hh in range(4):
        for c in range(32):
            r = hh * 32 + c
            LDL([(kch[:], dt["k1n"].ap()[hh * 64:hh * 64 + 64, c * 128:(c + 1) * 128]),
                 (vch[:], dt["v1T"].ap()[hh * 64:hh * 64 + 64, c * 128:(c + 1) * 128])])
            MM(lambda t, _=None: t.transpose(pb[0][:, 0:64], kch[:],
                                             identf[0:64, 0:64]))
            CP(ktok[:], pb[0][:, 0:64])
            MM(lambda t, _=None: t.transpose(pb[1][:, 0:64], vch[:],
                                             identf[0:64, 0:64]))
            CP(vtok[:], pb[1][:, 0:64])
            MM(lambda t, _=None: t.matmul(pb[2][0:64, 0:64], ktok[:], vtok[:],
                                          start=True, stop=True))
            CP(stt[:], pb[2][0:64, 0:64])
            LD1(dt["dkv1"].ap()[0:64, r * 64:(r + 1) * 64], stt[:])
            MM(lambda t, _=None: t.matmul(pb[3][0:1, 0:64], onescol[:], ktok[:],
                                          start=True, stop=True))
            CP(rows[0:1, 0:64], pb[3][0:1, 0:64])
            LD1(dt["dk1T"].ap()[r:r + 1, 0:64], rows[0:1, 0:64])
    for hh in range(4):
        LD1(dkvf.ap().rearrange("c (p j) -> c p j", j=64),
            dt["dkv1"].ap()[:, hh * 2048:(hh + 1) * 2048]
            .rearrange("p (c j) -> c p j", j=64))
        for w in range(8):
            MM(lambda t, w=w: t.matmul(pb[w % 2][0:32, :], m32sb[:],
                                       dkvf[:, w * 512:(w + 1) * 512],
                                       start=True, stop=True))
            CP(sc[0:32, 0:512], pb[w % 2][0:32, :])
            LD1(dt["st1"].ap()[w * 8:(w + 1) * 8, hh * 2048:(hh + 1) * 2048]
                .rearrange("p (c j) -> c p j", j=64),
                sc[0:32, 0:512].rearrange("c (p j) -> c p j", j=64))
        LD1(dkm[:], dt["dk1T"].ap()[hh * 32:(hh + 1) * 32, 0:64])
        MM(lambda t, _=None: t.matmul(pb[2][0:32, 0:64], m32sb[:], dkm[:],
                                      start=True, stop=True))
        CP(sc[0:32, 0:64], pb[2][0:32, 0:64])
        MM(lambda t, _=None: t.transpose(pb[3][0:64, 0:32], sc[0:32, 0:64],
                                         identf[0:32, 0:32]))
        CP(sc[0:64, 512:544], pb[3][0:64, 0:32])
        LD1(dt["kstT"].ap()[0:64, hh * 32:(hh + 1) * 32], sc[0:64, 512:544])
    for hh in range(4):
        for c in range(32):
            r = hh * 32 + c
            LDL([(qch[:], dt["q1n"].ap()[hh * 64:hh * 64 + 64, c * 128:(c + 1) * 128]),
                 (kch[:], dt["k1n"].ap()[hh * 64:hh * 64 + 64, c * 128:(c + 1) * 128]),
                 (vch[:], dt["v1T"].ap()[hh * 64:hh * 64 + 64, c * 128:(c + 1) * 128]),
                 (stt[:], dt["st1"].ap()[0:64, r * 64:(r + 1) * 64])])
            ctx.dma("sync", lambda s, r=r:
                    [s.dma_start(out=kstc[:], in_=dt["kstT"].ap()[0:64, r:r + 1],
                                 allow_slow_non_contiguous=True)], 1)
            MM(lambda t, _=None: t.transpose(pb[0][:, 0:64], vch[:],
                                             identf[0:64, 0:64]))
            CP(vtok[:], pb[0][:, 0:64])
            MM(lambda t, _=None: t.matmul(pb[1][:, 0:128], kch[:], qch[:],
                                          start=True, stop=True))
            TT(stm[:], pb[1][:, 0:128], triu[:], ALU.mult)

            def mden(t, _=None):
                t.matmul(pb[2][0:1, 0:128], onescol[:], stm[:],
                         start=True, stop=False)
                return t.matmul(pb[2][0:1, 0:128], kstc[:], qch[:],
                                start=False, stop=True)
            ctx.comp("tensor", mden)

            def mnum(t, _=None):
                t.matmul(pb[3][0:64, 0:128], vtok[:], stm[:],
                         start=True, stop=False)
                return t.matmul(pb[3][0:64, 0:128], stt[:], qch[:],
                                start=False, stop=True)
            ctx.comp("tensor", mnum)
            TS(rows[0:1, 0:128], pb[2][0:1, 0:128], 1e-6, ALU.add)
            RECIP(rows[0:1, 128:256], rows[0:1, 0:128])
            MM(lambda t, _=None: t.matmul(pb[4][0:64, 0:128], onesrow[0:1, 0:64],
                                          rows[0:1, 128:256], start=True, stop=True))
            CP(sc[0:64, 0:128], pb[3][0:64, 0:128])
            TT(ybt[:], sc[0:64, 0:128], pb[4][0:64, 0:128], ALU.mult)
            LD1(dt["y1b"].ap()[hh * 64:hh * 64 + 64, c * 128:(c + 1) * 128], ybt[:])
    reg.close()

    reg = contextlib.ExitStack()
    wreg1o = ctx.sbuf("wreg1o", [128, 2 * 1024], BF16, reg)
    emit_proj("y1b", 256, "l1a_WoT", D, N, wreg1o, res="res2", res_out="res3")
    reg.close()

    if PHASE_LIMIT <= 2:
        ctx.fence("sync")
        ctx.emit(block)
        return nc

    emit_ln("res3", N, "ln3b", out_f32="ln3f")

    reg = contextlib.ExitStack()
    wreg2c = ctx.sbuf("wreg2c", [128, 8 * 1024], BF16, reg)
    emit_proj("ln3b", D, "l1f_Wqkv", 1024, HN, wreg2c, out_bf="qT1f",
              wcols=(0, 1024))
    emit_proj("ln3b", D, "l1f_Wqkv", 1024, N, wreg2c, out_bf="kT1f",
              wcols=(1024, 2048))
    emit_proj("ln3b", D, "l1f_Wqkv", 1024, N, wreg2c, out_bf="vT1f",
              wcols=(2048, 3072))
    emit_proj("peT", D, "l1f_Wr", 1024, PE_W, wreg2c, out_bf="rTs")
    reg.close()

    emit_tok_transpose("vT1f", D, N, "vtok1f")

    if PHASE_LIMIT <= 3:
        ctx.fence("sync")
        ctx.emit(block)
        return nc


    reg = contextlib.ExitStack()
    vtkf = ctx.sbuf("f_vtok", [128, 32 * 1024], BF16, reg)
    ksb = ctx.sbuf("f_ksb", [64, N], BF16, reg)
    rsb = ctx.sbuf("f_rsb", [64, PE_W], BF16, reg)
    qub = ctx.sbuf("f_qu", [64, 128], BF16, reg)
    qvb = ctx.sbuf("f_qv", [64, 128], BF16, reg)
    qtl = ctx.sbuf("f_qt", [64, 128], BF16, reg)
    scos = ctx.sbuf("f_scos", [128, 4224], F32, reg)
    attb = ctx.sbuf("f_attb", [128, 4224], BF16, reg)
    atk = ctx.sbuf("f_atk", [128, 512], BF16, reg)
    bdgt = ctx.sbuf("f_bdgt", [128, 512], BF16, reg)
    acol = ctx.sbuf("f_acol", [128, 2], F32, reg)
    dmask = ctx.sbuf("c_dmask", [128, 2048], F32, reg)
    kmrow = ctx.sbuf("c_kmrow", [1, HN], F32, reg)
    LDL([(dmask[:], dt["diagmask"].ap()),
         (kmrow[:], dt["kmaskrow"].ap())])
    LDL([(vtkf.ap()[:, c * 4096:(c + 1) * 4096],
          dt["vtok1f"].ap()[:, c * 4096:(c + 1) * 4096]) for c in range(8)])
    for hh in range(16):
        LD1(ksb[:], dt["kT1f"].ap()[hh * 64:hh * 64 + 64, :])
        LD1(rsb[:], dt["rTs"].ap()[hh * 64:hh * 64 + 64, :])
        for qb in range(16):
            q0 = 128 * qb
            KA = q0 + 128
            nwA = (KA + 511) // 512
            KT = KA + 2048
            LD1(qtl[:], dt["qT1f"].ap()[hh * 64:hh * 64 + 64, q0:q0 + 128])
            TS(qub[:], qtl[:], ubT[:, hh:hh + 1], ALU.add)
            TS(qvb[:], qtl[:], vbT[:, hh:hh + 1], ALU.add)
            for (bd_name, u0, Wd) in (("BDa", 4096 - q0, (nwA - 1) * 512 + 640),
                                      ("BDb", 2048 - q0, 2176)):
                for w0 in range(0, Wd, 512):
                    wa = min(512, Wd - w0)
                    MM(lambda t, u0=u0, w0=w0, wa=wa: t.matmul(
                        pb[0][:, 0:wa], qvb[:],
                        rsb[0:64, u0 + w0:u0 + w0 + wa], start=True, stop=True))
                    CP(bst[:, 0:wa], pb[0][:, 0:wa])
                    LD1(dt[bd_name].ap()[:, w0:w0 + wa], bst[:, 0:wa])
            for kw in range(nwA + 4):
                isA = kw < nwA
                if isA:
                    kw0 = kw * 512
                    w = min(512, KA - kw0)
                    ksl = ksb[0:64, kw0:kw0 + w]
                    bd_t = dt["BDa"]
                    dcol = kw0
                else:
                    j = kw - nwA
                    kw0 = j * 512
                    w = 512
                    ksl = ksb[0:64, 2048 + kw0:2048 + kw0 + w]
                    bd_t = dt["BDb"]
                    dcol = kw0
                MM(lambda t, ksl=ksl, w=w: t.matmul(
                    pb[1][:, 0:w], qub[:], ksl, start=True, stop=True))
                ctx.dma("sync", lambda s, bd_t=bd_t, dcol=dcol, w=w:
                        [s.dma_start(out=bdgt[:, 0:w],
                                     in_=AP(bd_t, dcol + 127, [[BDW - 1, 128], [1, w]]))], 1)
                CP(sc[:, 0:w], bdgt[:, 0:w])
                TT(sc[:, 0:w], sc[:, 0:w], pb[1][:, 0:w], ALU.add)
                col0 = kw0 if isA else KA + kw0
                if isA and kw == nwA - 1:
                    a = w // 128 - 1
                    TS(sc[:, 0:w], sc[:, 0:w], 0.125, ALU.mult)
                    TT(scos[:, col0:col0 + w], sc[:, 0:w],
                       dmask[:, a * 512:a * 512 + w], ALU.add)
                elif isA:
                    TS(scos[:, col0:col0 + w], sc[:, 0:w], 0.125, ALU.mult)
                else:
                    TS(sc[:, 0:w], sc[:, 0:w], 0.125, ALU.mult)
                    bcast_row(pb[2], kmrow[0:1, kw0:kw0 + w], w)
                    TT(scos[:, col0:col0 + w], sc[:, 0:w], pb[2][:, 0:w], ALU.add)
            RMAX(acol[:, 0:1], scos[:, 0:KT])
            TS(acol[:, 1:2], acol[:, 0:1], -1.0, ALU.mult)
            ACT(xs[:, 0:KT], scos[:, 0:KT], AF.Exp, bias=acol[:, 1:2])
            RSUM(acol[:, 0:1], xs[:, 0:KT])
            RECIP(acol[:, 1:2], acol[:, 0:1])
            TS(attb[:, 0:KT], xs[:, 0:KT], acol[:, 1:2], ALU.mult)
            nkc = KT // 128
            kc = 0
            gi = 0
            while kc < nkc:
                gn = min(4, nkc - kc)

                def tpg(t, kc=kc, gn=gn):
                    i = None
                    for j in range(gn):
                        i = t.transpose(pbb[3][:, j * 128:(j + 1) * 128],
                                        attb[:, (kc + j) * 128:(kc + j + 1) * 128],
                                        identb[:])
                    return i
                ctx.comp("tensor", tpg)
                CP(atk[:, 0:gn * 128], pbb[3][:, 0:gn * 128])

                def mmc(t, kc=kc, gn=gn, qb=qb, hh=hh, nkc=nkc):
                    i = None
                    for j in range(gn):
                        kcj = kc + j
                        phys = kcj if kcj < qb + 1 else 16 + (kcj - (qb + 1))
                        i = t.matmul(pb[4][:, 0:64],
                                     atk[:, j * 128:(j + 1) * 128],
                                     vtkf[:, phys * 1024 + hh * 64:phys * 1024 + hh * 64 + 64],
                                     start=(kcj == 0), stop=(kcj == nkc - 1))
                    return i
                ctx.comp("tensor", mmc)
                kc += gn
                gi += 1
            CP(bst[:, 0:64], pb[4][:, 0:64])
            LD1(dt["ctxtok"].ap()[:, qb * 1024 + hh * 64:qb * 1024 + hh * 64 + 64],
                bst[:, 0:64])
    reg.close()

    if PHASE_LIMIT <= 4:
        ctx.fence("sync")
        ctx.emit(block)
        return nc

    for qb in range(16):
        for f4 in range(2):
            LD1(tstg[:, 0:512],
                dt["ctxtok"].ap()[:, qb * 1024 + f4 * 512:qb * 1024 + (f4 + 1) * 512])

            def tpc(t, _=None):
                i = None
                for j in range(4):
                    i = t.transpose(pbb[5][:, j * 128:(j + 1) * 128],
                                    tstg[:, j * 128:(j + 1) * 128], identb[:])
                return i
            ctx.comp("tensor", tpc)
            CP(bst[:, 0:512], pbb[5][:, 0:512])
            LD1(dt["ctxT"].ap()[f4 * 512:(f4 + 1) * 512, qb * 128:(qb + 1) * 128]
                .rearrange("(f p) n -> p f n", p=128),
                bst[:, 0:512].rearrange("p (f n) -> p f n", n=128))

    reg = contextlib.ExitStack()
    wregf = ctx.sbuf("wregf", [128, 8 * 4096], BF16, reg)
    emit_proj("ctxT", D, "l1f_Wo", D, HN, wregf, res="ln3f", res_out="h0")
    reg.close()

    emit_ln("h0", HN, "h1b", out_f32="h1f", gb_idx=0)

    reg = contextlib.ExitStack()
    wregf1 = ctx.sbuf("wregf1", [128, 8 * 4096], BF16, reg)
    emit_proj("h1b", D, "l1f_Wff1", 4096, HN, wregf1, out_bf="ffh",
              bias="l1f_bff1", act="relu")
    reg.close()
    reg = contextlib.ExitStack()
    wregf2 = ctx.sbuf("wregf2", [128, 32 * 1024], BF16, reg)
    emit_proj("ffh", 4096, "l1f_Wff2", D, HN, wregf2, bias="l1f_bff2",
              res="h1f", res_out="h2pre")
    reg.close()

    emit_ln("h2pre", HN, "h2b", out_f32="h2f", gb_idx=1)

    reg = contextlib.ExitStack()
    gwsb1 = ctx.sbuf("gwsb1", [128, 64], F32, reg)
    xg1 = ctx.sbuf("xg1", [128, 8 * 128], F32, reg)
    g81 = ctx.sbuf("g81", [128, 8], F32, reg)
    gcol1 = ctx.sbuf("gcol1", [128, 4], F32, reg)
    cwt1 = ctx.sbuf("cwt1", [8, 128], F32, reg)
    LD1(gwsb1.ap().rearrange("p (c m) -> p c m", m=8),
        dt["l1f_Wg"].ap().rearrange("(c p) m -> p c m", p=128))
    for tt in range(16):
        LD1(xg1.ap().rearrange("p (c n) -> p c n", n=128),
            dt["h2f"].ap()[:, tt * 128:(tt + 1) * 128]
            .rearrange("(c p) n -> p c n", p=128))

        def mlg1(t, _=None):
            i = None
            for c in range(8):
                i = t.matmul(pb[0][:, 0:8], xg1[:, c * 128:(c + 1) * 128],
                             gwsb1[:, c * 8:(c + 1) * 8],
                             start=(c == 0), stop=(c == 7))
            return i
        ctx.comp("tensor", mlg1)
        RMAX(gcol1[:, 0:1], pb[0][:, 0:8])
        TS(gcol1[:, 1:2], gcol1[:, 0:1], -1.0, ALU.mult)
        ACT(g81[:], pb[0][:, 0:8], AF.Exp, bias=gcol1[:, 1:2])
        RSUM(gcol1[:, 0:1], g81[:])
        RECIP(gcol1[:, 1:2], gcol1[:, 0:1])
        TS(g81[:], g81[:], gcol1[:, 1:2], ALU.mult)
        RMAX(gcol1[:, 0:1], g81[:])
        TS(sc[:, 0:8], g81[:], gcol1[:, 0:1], ALU.is_ge)
        TT(sc[:, 8:16], sc[:, 0:8], g81[:], ALU.mult)
        MM(lambda t, _=None: t.transpose(pb[1][0:8, 0:128], sc[:, 8:16], identf[:]))
        CP(cwt1[:], pb[1][0:8, 0:128])
        LD1(dt["cw1"].ap()[0:8, tt * 128:(tt + 1) * 128], cwt1[:])
    reg.close()

    LDL([(dt["h3pre"].ap()[c * 128:(c + 1) * 128, :],
          dt["h2f"].ap()[c * 128:(c + 1) * 128, :]) for c in range(8)])
    emit_moe("h2b", HN, 8, 4096, "l1f_We1", None, "l1f_We2", "cw1", "h3pre",
             256, "m1", relu_bias="l1f_be1", be2_name="l1f_be2")

    emit_ln("h3pre", HN, "h3b", gb_idx=2)

    reg = contextlib.ExitStack()
    wrego = ctx.sbuf("wrego", [128, 8 * 1024], BF16, reg)
    emit_proj("h3b", D, "l1f_Wout", D, HN, wrego, res="res3", res_out="y")
    reg.close()

    for nm in DUMPS:
        shape, dty = dsh[nm]
        du = nc.dram_tensor("dump_" + nm, list(shape), dty, kind="ExternalOutput")
        R = shape[0]
        step = 128 if R >= 128 else R
        LDL([(du.ap()[r0:r0 + step, :], dt[nm].ap()[r0:r0 + step, :])
             for r0 in range(0, R, step)])
    ctx.fence("sync")
    ctx.emit(block)
    return nc


_NC_CACHE = []


def kernel(**inp):
    maps = prep_in_maps(inp)
    if not _NC_CACHE:
        _NC_CACHE.append(build_nc())
    nc = _NC_CACHE[0]
    res = run_bass_kernel_spmd(nc, maps, core_ids=list(range(8)))
    out = np.zeros((B, S, D), np.float32)
    for c in range(8):
        b, h = divmod(c, 2)
        y = np.asarray(res.results[c]["y"], np.float32)
        out[b, h * HN:(h + 1) * HN, :] = y.T
    return out


# version 05
# speedup vs baseline: 1.5095x; 1.3890x over previous
import contextlib
import numpy as np
import ml_dtypes
import concourse.bass as bass
import concourse.mybir as mybir
from concourse.bass import AP
from concourse.bass_utils import run_bass_kernel_spmd

F32 = mybir.dt.float32
BF16 = mybir.dt.bfloat16
AF = mybir.ActivationFunctionType
ALU = mybir.AluOpType

B, S, D = 4, 4096, 1024
N = 4096
HN = 2048
PE_W = 6656
BDW = 2304

bf16 = ml_dtypes.bfloat16


def _bf(a):
    return np.ascontiguousarray(np.asarray(a, dtype=np.float32).astype(bf16))


def _f32(a):
    return np.ascontiguousarray(np.asarray(a), dtype=np.float32)


def _cols(v):
    return np.ascontiguousarray(np.asarray(v, np.float32).reshape(-1, 128).T)


def prep_in_maps(inp):
    sh = {}
    sh["l0_WqT"] = _bf(np.asarray(inp["L0_attn_Wq"]).T)
    sh["l0_WkT"] = _bf(np.asarray(inp["L0_attn_Wk"]).T)
    sh["l0_WvT"] = _bf(np.asarray(inp["L0_attn_Wv"]).T)
    sh["l0_WoT"] = _bf(np.asarray(inp["L0_attn_Wo"]).T)
    sh["l0_bq"] = _cols(inp["L0_attn_bq"])
    sh["l0_bk"] = _cols(inp["L0_attn_bk"])
    sh["l0_bv"] = _cols(inp["L0_attn_bv"])
    sh["l0_bo"] = _cols(inp["L0_attn_bo"])
    F = 1408
    wg = np.asarray(inp["L0_ffn_w_gate"]); wu = np.asarray(inp["L0_ffn_w_up"])
    wd = np.asarray(inp["L0_ffn_w_down"])
    swg = np.asarray(inp["L0_ffn_sw_gate"]); swu = np.asarray(inp["L0_ffn_sw_up"])
    swd = np.asarray(inp["L0_ffn_sw_down"])
    wg_all = np.zeros((10, D, F), np.float32)
    wu_all = np.zeros((10, D, F), np.float32)
    wd_all = np.zeros((10, F, D), np.float32)
    wg_all[:8] = wg; wu_all[:8] = wu; wd_all[:8] = wd
    wg_all[8] = swg[:, :F]; wu_all[8] = swu[:, :F]; wd_all[8] = swd[:F]
    wg_all[9] = swg[:, F:]; wu_all[9] = swu[:, F:]; wd_all[9] = swd[F:]
    sh["l0_wg"] = _bf(wg_all); sh["l0_wu"] = _bf(wu_all); sh["l0_wd"] = _bf(wd_all)
    sh["l0_gwT"] = _f32(np.asarray(inp["L0_ffn_gate_w"]).T)
    cw0 = np.zeros((10, N), np.float32); cw0[8:] = 1.0
    sh["cw0_init"] = cw0
    sh["l1a_WqT"] = _bf(np.asarray(inp["L1_attn_Wq"]).T)
    sh["l1a_WkT"] = _bf(np.asarray(inp["L1_attn_Wk"]).T)
    sh["l1a_WvT"] = _bf(np.asarray(inp["L1_attn_Wv"]).T)
    sh["l1a_WoT"] = _bf(np.asarray(inp["L1_attn_Wo"]).T)
    sh["l1f_Wqkv"] = _bf(inp["L1_ffn_Wqkv"])
    sh["l1f_Wo"] = _bf(inp["L1_ffn_Wo"])
    sh["l1f_Wr"] = _bf(inp["L1_ffn_Wr"])
    sh["l1f_ubT"] = _f32(np.asarray(inp["L1_ffn_u_bias"]).T)
    sh["l1f_vbT"] = _f32(np.asarray(inp["L1_ffn_v_bias"]).T)
    gb = np.zeros((128, 48), np.float32)
    for i, nm in enumerate(["ln1_g", "ln1_b", "ln2_g", "ln2_b", "ln3_g", "ln3_b"]):
        gb[:, i * 8:(i + 1) * 8] = _cols(inp["L1_ffn_" + nm])
    sh["l1f_gb"] = gb
    sh["l1f_Wff1"] = _bf(inp["L1_ffn_Wff1"])
    sh["l1f_Wff2"] = _bf(inp["L1_ffn_Wff2"])
    sh["l1f_bff1"] = _cols(inp["L1_ffn_bff1"])
    sh["l1f_bff2"] = _cols(inp["L1_ffn_bff2"])
    sh["l1f_Wg"] = _f32(inp["L1_ffn_Wg"])
    sh["l1f_We1"] = _bf(inp["L1_ffn_We1"])
    sh["l1f_We2"] = _bf(inp["L1_ffn_We2"])
    be1 = np.asarray(inp["L1_ffn_be1"])
    sh["l1f_be1"] = np.ascontiguousarray(
        be1.reshape(8, 32, 128).transpose(2, 0, 1).reshape(128, 256))
    sh["l1f_be2"] = _f32(inp["L1_ffn_be2"])
    sh["l1f_Wout"] = _bf(inp["L1_ffn_Wout"])
    invf = 1.0 / (10000.0 ** (np.arange(0, D, 2, dtype=np.float64) / D))
    dist = 4223 - np.arange(PE_W, dtype=np.float64)
    valid = (dist >= 0) & (dist <= S - 1)
    ang = dist[None, :] * invf[:, None]
    peT = np.concatenate([np.sin(ang), np.cos(ang)], axis=0)
    peT[:, ~valid] = 0.0
    sh["peT"] = np.ascontiguousarray(peT.astype(bf16))
    sh["ident_b"] = np.eye(128, dtype=np.float32).astype(bf16)
    sh["ident_f"] = np.eye(128, dtype=np.float32)
    sh["onescol"] = np.ones((128, 1), np.float32)
    sh["onescolb"] = np.ones((128, 1), np.float32).astype(bf16)
    sh["onesrow"] = np.ones((1, 512), np.float32)
    sh["triu"] = np.triu(np.ones((128, 128), np.float32))
    dm = np.zeros((128, 4 * 512), np.float32)
    for a in range(4):
        w = 128 * (a + 1)
        p = np.arange(128)[:, None]; f = np.arange(512)[None, :]
        dm[:, a * 512:(a + 1) * 512] = np.where(f > p + (w - 128), -1e30, 0.0)
    sh["diagmask"] = dm
    x = np.asarray(inp["L0_attn_x"])
    maps = []
    for c in range(8):
        b, h = c // 2, c % 2
        m = dict(sh)
        xT = np.asarray(x[b], np.float32).T
        if h == 0:
            m["x"] = _f32(xT)
        else:
            m["x"] = _f32(np.concatenate([xT[:, HN:], xT[:, :HN]], axis=1))
        lg = (np.arange(32) + (16 if h else 0)) % 32
        m["M32"] = np.ascontiguousarray((lg[:, None] < lg[None, :]).astype(np.float32))
        mr = np.zeros((1, HN), np.float32)
        if h == 0:
            mr[:] = -1e30
        m["kmaskrow"] = mr
        maps.append(m)
    return maps


ENGS = ("tensor", "vector", "scalar", "sync", "gpsimd")
XSEMS = ("LS0", "LS1", "GW0", "GW1", "ST0")
SEMS = ENGS + XSEMS


class Ctx:

    def __init__(self, nc, stack):
        self.nc, self.stack = nc, stack
        self.sems = {e: stack.enter_context(nc.semaphore("S_" + e)) for e in SEMS}
        self.items = []
        self.counts = {e: 0 for e in SEMS}
        self.eng_last = {e: {o: 0 for o in SEMS} for e in ENGS}
        self.writers = {}
        self.readers = {}

    def sbuf(self, name, shape, dtype, region=None):
        st = region if region is not None else self.stack
        return st.enter_context(self.nc.sbuf_tensor(name, shape, dtype))

    def _add(self, eng, fn, n_inc, rd=None, wr=None, sem=None):
        sem = sem if sem is not None else eng
        deps = []
        if rd is None and wr is None:
            for o in SEMS:
                if self.counts[o]:
                    deps.append((o, self.counts[o]))
            self.writers["__L__"] = (sem, self.counts[sem] + n_inc)
        else:
            rd = rd or ()
            wr = wr or ()
            if "__L__" in self.writers:
                deps.append(self.writers["__L__"])
            for t in rd:
                if t in self.writers:
                    deps.append(self.writers[t])
            for t in wr:
                if t in self.writers:
                    deps.append(self.writers[t])
                for s, c in self.readers.get(t, {}).items():
                    deps.append((s, c))
            after = (sem, self.counts[sem] + n_inc)
            for t in wr:
                self.writers[t] = after
                self.readers[t] = {}
            for t in rd:
                m = self.readers.setdefault(t, {})
                if after[1] > m.get(sem, 0):
                    m[sem] = after[1]
        waits = []
        for (s, c) in deps:
            if s == eng and eng == "tensor":
                continue
            if c > self.counts[s]:
                c = self.counts[s]
            if c > self.eng_last[eng][s]:
                waits.append((s, c))
                self.eng_last[eng][s] = c
        self.items.append((eng, fn, n_inc, waits, sem))
        self.counts[sem] += n_inc

    def comp(self, eng, fn, rd=None, wr=None):
        self._add(eng, fn, 1, rd, wr)

    def dma(self, eng, fn, n, rd=None, wr=None, sem=None):
        self._add(eng, fn, 16 * n, rd, wr, sem)

    def fence(self, eng):
        self._add(eng, None, 0)

    def emit(self, block):
        sems = self.sems
        for e in ENGS:
            items = [(fn, n, w, sm) for (eng, fn, n, w, sm) in self.items
                     if eng == e]

            def section(h, e=e, items=items):
                for fn, n_inc, waits, sm in items:
                    for (o, v) in waits:
                        h.wait_ge(sems[o], v)
                    if n_inc == 0:
                        continue
                    if n_inc == 1:
                        fn(h).then_inc(sems[sm], 1)
                    else:
                        instrs = fn(h)
                        assert len(instrs) * 16 == n_inc, (e, len(instrs), n_inc)
                        for i in instrs:
                            i.then_inc(sems[sm], 16)
            getattr(block, e)(section)


DUMPS = []
PHASE_LIMIT = 99


def build_nc():
    nc = bass.Bass("TRN2", target_bir_lowering=False, detect_race_conditions=False)
    stack = contextlib.ExitStack()
    dt = {}
    dsh = {}

    def din(name, shape, dtype):
        dt[name] = nc.dram_tensor(name, list(shape), dtype, kind="ExternalInput")
        dsh[name] = (shape, dtype)

    def dint(name, shape, dtype):
        dt[name] = nc.dram_tensor(name, list(shape), dtype, kind="Internal")
        dsh[name] = (shape, dtype)

    din("x", [D, N], F32)
    y_d = nc.dram_tensor("y", [D, HN], F32, kind="ExternalOutput")
    dt["y"] = y_d
    for nm in ("l0_WqT", "l0_WkT", "l0_WvT", "l0_WoT"):
        din(nm, [D, D], BF16)
    for nm in ("l0_bq", "l0_bk", "l0_bv", "l0_bo"):
        din(nm, [128, 8], F32)
    din("l0_wg", [10, D, 1408], BF16)
    din("l0_wu", [10, D, 1408], BF16)
    din("l0_wd", [10, 1408, D], BF16)
    din("l0_gwT", [D, 8], F32)
    din("cw0_init", [10, N], F32)
    din("l1a_WqT", [D, 256], BF16); din("l1a_WkT", [D, 256], BF16)
    din("l1a_WvT", [D, 256], BF16); din("l1a_WoT", [256, D], BF16)
    din("l1f_Wqkv", [D, 3072], BF16); din("l1f_Wo", [D, D], BF16)
    din("l1f_Wr", [D, D], BF16)
    din("l1f_ubT", [64, 16], F32); din("l1f_vbT", [64, 16], F32)
    din("l1f_gb", [128, 48], F32)
    din("l1f_Wff1", [D, 4096], BF16); din("l1f_Wff2", [4096, D], BF16)
    din("l1f_bff1", [128, 32], F32); din("l1f_bff2", [128, 8], F32)
    din("l1f_Wg", [D, 8], F32)
    din("l1f_We1", [8, D, 4096], BF16); din("l1f_We2", [8, 4096, D], BF16)
    din("l1f_be1", [128, 256], F32); din("l1f_be2", [8, D], F32)
    din("l1f_Wout", [D, D], BF16)
    din("peT", [D, PE_W], BF16)
    din("ident_b", [128, 128], BF16); din("ident_f", [128, 128], F32)
    din("onescol", [128, 1], F32); din("onescolb", [128, 1], BF16)
    din("onesrow", [1, 512], F32)
    din("triu", [128, 128], F32)
    din("diagmask", [128, 2048], F32)
    din("M32", [32, 32], F32)
    din("kmaskrow", [1, HN], F32)

    dint("ln0b", [D, N], BF16)
    dint("q0b", [D, N], BF16); dint("k0b", [D, N], BF16); dint("v0b", [D, N], BF16)
    dint("k0tok", [128, 32 * 1024], BF16); dint("v0tok", [128, 32 * 1024], BF16)
    dint("kv0b", [D, D], BF16); dint("ksum0", [128, 8], F32)
    dint("c0b", [D, N], BF16)
    dint("res1", [D, N], F32)
    dint("ln1b", [D, N], BF16); dint("ln1f", [D, N], F32)
    dint("cw0", [10, N], F32)
    dint("res2", [D, N], F32)
    dint("ln2b", [D, N], BF16)
    dint("q1n", [256, N], F32); dint("k1n", [256, N], F32); dint("v1T", [256, N], F32)
    dint("dkv1", [64, 4 * 2048], F32)
    dint("dk1T", [128, 64], F32)
    dint("st1", [64, 4 * 2048], F32)
    dint("kstT", [64, 128], F32)
    dint("y1b", [256, N], BF16)
    dint("res3", [D, N], F32)
    dint("ln3b", [D, N], BF16); dint("ln3f", [D, N], F32)
    dint("qT1f", [D, HN], BF16)
    dint("kT1f", [D, N], BF16); dint("vT1f", [D, N], BF16)
    dint("rTs", [D, PE_W], BF16)
    dint("vtok1f", [128, 32 * 1024], BF16)
    dint("BDa", [128, BDW], BF16)
    dint("BDb", [128, BDW], BF16)
    dint("ctxtok", [128, 16 * 1024], BF16)
    dint("ctxT", [D, HN], BF16)
    dint("h0", [D, HN], F32)
    dint("h1b", [D, HN], BF16); dint("h1f", [D, HN], F32)
    dint("ffh", [4096, HN], BF16)
    dint("h2pre", [D, HN], F32)
    dint("h2b", [D, HN], BF16); dint("h2f", [D, HN], F32)
    dint("cw1", [8, HN], F32)
    dint("h3pre", [D, HN], F32)
    dint("h3b", [D, HN], BF16)

    block = stack.enter_context(nc.Block())
    ctx = Ctx(nc, stack)
    pb = [stack.enter_context(nc.psum_tensor("pb%d" % i, [128, 512], F32))
          for i in range(8)]
    pbb = [p.ap().bitcast(BF16) for p in pb]

    onescol = ctx.sbuf("c_onescol", [128, 1], F32)
    onescolb = ctx.sbuf("c_onescolb", [128, 1], BF16)
    onesrow = ctx.sbuf("c_onesrow", [1, 512], F32)
    identb = ctx.sbuf("c_identb", [128, 128], BF16)
    identf = ctx.sbuf("c_identf", [128, 128], F32)
    triu = ctx.sbuf("c_triu", [128, 128], F32)
    ubT = ctx.sbuf("c_ubT", [64, 16], F32)
    vbT = ctx.sbuf("c_vbT", [64, 16], F32)
    gbcols = ctx.sbuf("c_gb", [128, 48], F32)
    m32sb = ctx.sbuf("c_m32", [32, 32], F32)

    xs = ctx.sbuf("g_xs", [128, 8 * 512], F32)
    sc = ctx.sbuf("g_sc", [128, 4 * 512], F32)
    rows = ctx.sbuf("g_rows", [1, 3 * 512], F32)
    bst = ctx.sbuf("g_bst", [128, 512], BF16)
    tstg = ctx.sbuf("g_tstg", [128, 512], BF16)

    def TT(out, a, b, op, rd=None, wr=None):
        ctx.comp("vector", lambda v, out=out, a=a, b=b, op=op:
                 v.tensor_tensor(out=out, in0=a, in1=b, op=op), rd=rd, wr=wr)

    def TS(out, a, s1, op, rd=None, wr=None):
        ctx.comp("vector", lambda v, out=out, a=a, s1=s1, op=op:
                 v.tensor_scalar(out=out, in0=a, scalar1=s1, scalar2=None, op0=op),
                 rd=rd, wr=wr)

    def CP(out, a, rd=None, wr=None):
        ctx.comp("vector", lambda v, out=out, a=a: v.tensor_copy(out=out, in_=a),
                 rd=rd, wr=wr)

    def RMAX(out, a, rd=None, wr=None):
        ctx.comp("vector", lambda v, out=out, a=a:
                 v.reduce_max(out=out, in_=a, axis=mybir.AxisListType.X),
                 rd=rd, wr=wr)

    def RSUM(out, a, rd=None, wr=None):
        ctx.comp("vector", lambda v, out=out, a=a:
                 v.reduce_sum(out=out, in_=a, axis=mybir.AxisListType.X),
                 rd=rd, wr=wr)

    def RECIP(out, a, rd=None, wr=None):
        ctx.comp("vector", lambda v, out=out, a=a: v.reciprocal(out=out, in_=a),
                 rd=rd, wr=wr)

    def ACT(out, a, func, bias=None, scale=1.0, rd=None, wr=None):
        if bias is None:
            ctx.comp("scalar", lambda s, out=out, a=a, func=func, scale=scale:
                     s.activation(out=out, in_=a, func=func, scale=scale),
                     rd=rd, wr=wr)
        else:
            ctx.comp("scalar", lambda s, out=out, a=a, func=func, bias=bias,
                     scale=scale:
                     s.activation(out=out, in_=a, func=func, bias=bias, scale=scale),
                     rd=rd, wr=wr)

    def SMUL(out, a, c, rd=None, wr=None):
        ctx.comp("scalar", lambda s, out=out, a=a, c=c: s.mul(out, a, c),
                 rd=rd, wr=wr)

    def SCP(out, a, rd=None, wr=None):
        ctx.comp("scalar", lambda s, out=out, a=a: s.copy(out=out, in_=a),
                 rd=rd, wr=wr)

    def MM(fn, rd=None, wr=None):
        ctx.comp("tensor", fn, rd=rd, wr=wr)

    def LD1(out, in_, rd=None, wr=None, q="sync", sem=None):
        ctx.dma(q, lambda s, out=out, in_=in_:
                [s.dma_start(out=out, in_=in_)], 1, rd=rd, wr=wr, sem=sem)

    def LDL(pairs, rd=None, wr=None, q="sync", sem=None):
        ctx.dma(q, lambda s, pairs=pairs:
                [s.dma_start(out=o, in_=i) for (o, i) in pairs], len(pairs),
                rd=rd, wr=wr, sem=sem)

    def GDL(pairs, rd=None, wr=None, sem=None):
        ctx.dma("gpsimd", lambda g, pairs=pairs:
                [g.dma_start(out=o, in_=i) for (o, i) in pairs], len(pairs),
                rd=rd, wr=wr, sem=sem)

    def GACC(out, in_, rd=None, wr=None):
        ctx.dma("gpsimd", lambda g, out=out, in_=in_:
                [g.dma_start(out=out, in_=in_, accum_op=ALU.add)], 1,
                rd=rd, wr=wr)

    def bcast_row(bank, row_ap, width, np_=128, rd=None, wr=None):
        MM(lambda t, bank=bank, row_ap=row_ap, width=width, np_=np_:
           t.matmul(bank[0:np_, 0:width], onesrow[0:1, 0:np_], row_ap,
                    start=True, stop=True), rd=rd, wr=wr)

    LDL([(onescol[:], dt["onescol"].ap()),
         (onescolb[:], dt["onescolb"].ap()),
         (onesrow[:], dt["onesrow"].ap()),
         (identb[:], dt["ident_b"].ap()),
         (identf[:], dt["ident_f"].ap()),
         (triu[:], dt["triu"].ap()),
         (ubT[:], dt["l1f_ubT"].ap()),
         (vbT[:], dt["l1f_vbT"].ap()),
         (gbcols[:], dt["l1f_gb"].ap()),
         (m32sb[:], dt["M32"].ap()),
         (dt["cw0"].ap(), dt["cw0_init"].ap())])


    def emit_ln(src, Nc, out_bf, out_f32=None, gb_idx=None):
        src_a, out_a = dt[src].ap(), dt[out_bf].ap()
        of_a = dt[out_f32].ap() if out_f32 else None
        for w in range(Nc // 512):
            w0 = w * 512
            LD1(xs.ap().rearrange("p (c n) -> p c n", n=512),
                src_a[:, w0:w0 + 512].rearrange("(c p) n -> p c n", p=128))
            MM(lambda t, _=None: [
                t.matmul(pb[0][0:1, :], onescol[:], xs[:, c * 512:(c + 1) * 512],
                         start=(c == 0), stop=(c == 7)) for c in range(8)][-1])
            for half in range(2):
                cs = [0, 1, 2, 3] if half == 0 else [4, 5, 6, 7]

                def sqw(s, cs=cs):
                    i = None
                    for j, c in enumerate(cs):
                        i = s.activation(out=sc[:, j * 512:(j + 1) * 512],
                                         in_=xs[:, c * 512:(c + 1) * 512],
                                         func=AF.Square)
                    return i
                ctx.comp("scalar", sqw)

                def mmq(t, cs=cs, half=half):
                    i = None
                    for j in range(4):
                        i = t.matmul(pb[1][0:1, :], onescol[:],
                                     sc[:, j * 512:(j + 1) * 512],
                                     start=(half == 0 and j == 0),
                                     stop=(half == 1 and j == 3))
                    return i
                ctx.comp("tensor", mmq)
            SMUL(rows[0:1, 0:512], pb[0][0:1, :], 1.0 / D)
            SMUL(rows[0:1, 512:1024], pb[1][0:1, :], 1.0 / D)
            TT(rows[0:1, 1024:1536], rows[0:1, 0:512], rows[0:1, 0:512], ALU.mult)
            TT(rows[0:1, 512:1024], rows[0:1, 512:1024], rows[0:1, 1024:1536],
               ALU.subtract)
            TS(rows[0:1, 512:1024], rows[0:1, 512:1024], 1e-5, ALU.add)
            ACT(rows[0:1, 1024:1536], rows[0:1, 512:1024], AF.Ln)
            ACT(rows[0:1, 512:1024], rows[0:1, 1024:1536], AF.Exp, scale=-0.5)
            TT(rows[0:1, 1024:1536], rows[0:1, 0:512], rows[0:1, 512:1024], ALU.mult)
            TS(rows[0:1, 1024:1536], rows[0:1, 1024:1536], -1.0, ALU.mult)
            bcast_row(pb[2], rows[0:1, 512:1024], 512)
            bcast_row(pb[3], rows[0:1, 1024:1536], 512)
            for c in range(8):
                TT(sc[:, 0:512], xs[:, c * 512:(c + 1) * 512], pb[2][:, :], ALU.mult)
                TT(sc[:, 512:1024], sc[:, 0:512], pb[3][:, :], ALU.add)
                if gb_idx is not None:
                    g0 = gb_idx * 16
                    TS(sc[:, 512:1024], sc[:, 512:1024],
                       gbcols[:, g0 + c:g0 + c + 1], ALU.mult)
                    TS(sc[:, 512:1024], sc[:, 512:1024],
                       gbcols[:, g0 + 8 + c:g0 + 8 + c + 1], ALU.add)
                CP(bst[:, 0:512], sc[:, 512:1024])
                LD1(out_a[c * 128:(c + 1) * 128, w0:w0 + 512], bst[:, 0:512])
                if of_a is not None:
                    LD1(of_a[c * 128:(c + 1) * 128, w0:w0 + 512], sc[:, 512:1024])

    def load_w(wreg, w_name, K, M, cols=None):
        a = dt[w_name].ap()
        if cols is not None:
            a = a[:, cols[0]:cols[1]]
            M = cols[1] - cols[0]
        pairs = []
        for c in range(K // 128):
            pairs.append((wreg.ap()[:, c * M:(c + 1) * M],
                          a[c * 128:(c + 1) * 128, :]))
        LDL(pairs)

    def emit_proj(src, K, w_name, M, Nc, wreg, out_bf=None, wcols=None,
                  bias=None, act=None, res=None, res_out=None, srcoff=0,
                  bias_sb=None):
        load_w(wreg, w_name, K, M, cols=wcols)
        src_a = dt[src].ap()
        nkc = K // 128
        regp = contextlib.ExitStack()
        stg = ctx.sbuf("stg_%s" % (out_bf or res_out), [128, nkc * 512], BF16,
                       regp)
        for w in range(Nc // 512):
            w0 = srcoff + w * 512
            LD1(stg.ap().rearrange("p (c n) -> p c n", n=512),
                src_a[:, w0:w0 + 512].rearrange("(c p) n -> p c n", p=128))
            for mc in range(M // 128):
                bank = pb[mc % 2]

                def mmp(t, mc=mc, bank=bank, nkc=nkc):
                    i = None
                    for kc in range(nkc):
                        i = t.matmul(bank[:, :],
                                     wreg[:, kc * M + mc * 128:kc * M + (mc + 1) * 128],
                                     stg[:, kc * 512:(kc + 1) * 512],
                                     start=(kc == 0), stop=(kc == nkc - 1))
                    return i
                ctx.comp("tensor", mmp)
                bcol = None
                if bias is not None:
                    bsb = bias_sb if bias_sb is not None else biascols[bias]
                    bcol = bsb[:, mc:mc + 1]
                if act == "elu1":
                    if bcol is not None:
                        TS(sc[:, 0:512], bank[:, :], bcol, ALU.add)
                    else:
                        CP(sc[:, 0:512], bank[:, :])
                    TS(sc[:, 512:1024], sc[:, 0:512], 0.0, ALU.min)
                    ACT(sc[:, 1024:1536], sc[:, 512:1024], AF.Exp)
                    ACT(sc[:, 512:1024], sc[:, 0:512], AF.Relu)
                    TT(sc[:, 0:512], sc[:, 512:1024], sc[:, 1024:1536], ALU.add)
                elif act == "relu":
                    ACT(sc[:, 0:512], bank[:, :], AF.Relu, bias=bcol)
                else:
                    if bcol is not None:
                        TS(sc[:, 0:512], bank[:, :], bcol, ALU.add)
                    else:
                        CP(sc[:, 0:512], bank[:, :])
                if res is not None:
                    LD1(sc[:, 512:1024],
                        dt[res].ap()[mc * 128:(mc + 1) * 128, w * 512:w * 512 + 512])
                    TT(sc[:, 0:512], sc[:, 0:512], sc[:, 512:1024], ALU.add)
                    LD1(dt[res_out].ap()[mc * 128:(mc + 1) * 128, w * 512:w * 512 + 512],
                        sc[:, 0:512])
                    if out_bf is not None:
                        CP(bst[:, 0:512], sc[:, 0:512])
                        LD1(dt[out_bf].ap()[mc * 128:(mc + 1) * 128,
                                            w * 512:w * 512 + 512], bst[:, 0:512])
                else:
                    CP(bst[:, 0:512], sc[:, 0:512])
                    LD1(dt[out_bf].ap()[mc * 128:(mc + 1) * 128,
                                        w * 512:w * 512 + 512], bst[:, 0:512])
        regp.close()

    def emit_tok_transpose(src, R, Nc, dst):
        src_a, dst_a = dt[src].ap(), dt[dst].ap()
        for t4 in range(Nc // 512):
            for rc in range(R // 128):
                LD1(tstg[:, 0:512],
                    src_a[rc * 128:(rc + 1) * 128, t4 * 512:(t4 + 1) * 512])

                def tp(t, _=None):
                    i = None
                    for j in range(4):
                        i = t.transpose(pbb[4][:, j * 128:(j + 1) * 128],
                                        tstg[:, j * 128:(j + 1) * 128], identb[:])
                    return i
                ctx.comp("tensor", tp)
                CP(bst[:, 0:512], pbb[4][:, 0:512])
                LD1(dst_a.rearrange("p (t r) -> p t r", r=R)
                    [:, t4 * 4:t4 * 4 + 4, rc * 128:(rc + 1) * 128],
                    bst[:, 0:512].rearrange("p (t r) -> p t r", r=128))

    biascols = {}
    for nm in ("l0_bq", "l0_bk", "l0_bv", "l0_bo", "l1f_bff1", "l1f_bff2",
               "l1f_be1"):
        bw = {"l0_bq": 8, "l0_bk": 8, "l0_bv": 8, "l0_bo": 8,
              "l1f_bff1": 32, "l1f_bff2": 8, "l1f_be1": 256}[nm]
        biascols[nm] = ctx.sbuf("bc_" + nm, [128, bw], F32)
        LD1(biascols[nm][:], dt[nm].ap())

    emit_ln("x", N, "ln0b")

    reg = contextlib.ExitStack()
    wreg2 = ctx.sbuf("wreg2", [128, 8 * 1024], BF16, reg)
    emit_proj("ln0b", D, "l0_WqT", D, N, wreg2, out_bf="q0b", bias="l0_bq",
              act="elu1")
    emit_proj("ln0b", D, "l0_WkT", D, N, wreg2, out_bf="k0b", bias="l0_bk",
              act="elu1")
    emit_proj("ln0b", D, "l0_WvT", D, N, wreg2, out_bf="v0b", bias="l0_bv")
    reg.close()

    emit_tok_transpose("k0b", D, N, "k0tok")
    emit_tok_transpose("v0b", D, N, "v0tok")

    reg = contextlib.ExitStack()
    kvkb = ctx.sbuf("kvkb", [128, 512], BF16, reg)
    kvvb = ctx.sbuf("kvvb", [128, 512], BF16, reg)
    for eh in range(2):
        for dg in range(2):
            for tc in range(32):
                LDL([(kvkb[:], dt["k0tok"].ap()[:, tc * 1024 + dg * 512:
                                                tc * 1024 + dg * 512 + 512]),
                     (kvvb[:], dt["v0tok"].ap()[:, tc * 1024 + eh * 512:
                                                tc * 1024 + eh * 512 + 512])])

                def mkv(t, eh=eh, dg=dg, tc=tc):
                    i = None
                    for dc in range(4):
                        i = t.matmul(pb[dc][:, :], kvkb[:, dc * 128:(dc + 1) * 128],
                                     kvvb[:], start=(tc == 0), stop=(tc == 31))
                    if eh == 0:
                        for dc in range(4):
                            i = t.matmul(pb[4][:, dg * 4 + dc:dg * 4 + dc + 1],
                                         kvkb[:, dc * 128:(dc + 1) * 128],
                                         onescolb[:], start=(tc == 0),
                                         stop=(tc == 31))
                    return i
                ctx.comp("tensor", mkv)
            for dc in range(4):
                dci = dg * 4 + dc
                CP(bst[:, 0:512], pb[dc][:, :])
                LD1(dt["kv0b"].ap()[dci * 128:(dci + 1) * 128,
                                    eh * 512:eh * 512 + 512], bst[:, 0:512])
        if eh == 0:
            CP(sc[:, 0:8], pb[4][:, 0:8])
            TS(sc[:, 0:8], sc[:, 0:8], 1e-6, ALU.add)
            LD1(dt["ksum0"].ap(), sc[:, 0:8])
    reg.close()

    reg = contextlib.ExitStack()
    kvsb = ctx.sbuf("kvsb", [128, 8 * 1024], BF16, reg)
    ksp = ctx.sbuf("ksp", [128, 8], F32, reg)
    qstg = ctx.sbuf("qstg", [128, 8 * 512], BF16, reg)
    LDL([(kvsb.ap()[:, c * 1024:(c + 1) * 1024],
          dt["kv0b"].ap()[c * 128:(c + 1) * 128, :]) for c in range(8)])
    LD1(ksp[:], dt["ksum0"].ap())
    for w in range(8):
        w0 = w * 512
        LD1(qstg.ap().rearrange("p (c n) -> p c n", n=512),
            dt["q0b"].ap()[:, w0:w0 + 512].rearrange("(c p) n -> p c n", p=128))
        for c in range(8):
            TS(sc[:, 0:512], qstg[:, c * 512:(c + 1) * 512], ksp[:, c:c + 1],
               ALU.mult)
            MM(lambda t, c=c: t.matmul(pb[0][0:1, :], onescol[:], sc[:, 0:512],
                                       start=(c == 0), stop=(c == 7)))
        RECIP(rows[0:1, 0:512], pb[0][0:1, :])
        bcast_row(pb[1], rows[0:1, 0:512], 512)
        for ec in range(8):
            def mmc(t, ec=ec):
                i = None
                for dc in range(8):
                    i = t.matmul(pb[2][:, :],
                                 kvsb[:, dc * 1024 + ec * 128:dc * 1024 + (ec + 1) * 128],
                                 qstg[:, dc * 512:(dc + 1) * 512],
                                 start=(dc == 0), stop=(dc == 7))
                return i
            ctx.comp("tensor", mmc)
            CP(sc[:, 0:512], pb[2][:, :])
            TT(sc[:, 512:1024], sc[:, 0:512], pb[1][:, :], ALU.mult)
            CP(bst[:, 0:512], sc[:, 512:1024])
            LD1(dt["c0b"].ap()[ec * 128:(ec + 1) * 128, w0:w0 + 512], bst[:, 0:512])
    reg.close()

    reg = contextlib.ExitStack()
    wreg2 = ctx.sbuf("wreg2b", [128, 8 * 1024], BF16, reg)
    emit_proj("c0b", D, "l0_WoT", D, N, wreg2, wcols=None, bias="l0_bo",
              res="x", res_out="res1")
    reg.close()

    emit_ln("res1", N, "ln1b", out_f32="ln1f")

    reg = contextlib.ExitStack()
    gwsb = ctx.sbuf("gwsb", [128, 64], F32, reg)
    xg = ctx.sbuf("xg", [128, 8 * 128], F32, reg)
    g8 = ctx.sbuf("g8", [128, 8], F32, reg)
    gcol = ctx.sbuf("gcol", [128, 4], F32, reg)
    cwt = ctx.sbuf("cwt", [8, 128], F32, reg)
    LD1(gwsb.ap().rearrange("p (c m) -> p c m", m=8),
        dt["l0_gwT"].ap().rearrange("(c p) m -> p c m", p=128))
    for tt in range(32):
        LD1(xg.ap().rearrange("p (c n) -> p c n", n=128),
            dt["ln1f"].ap()[:, tt * 128:(tt + 1) * 128]
            .rearrange("(c p) n -> p c n", p=128))

        def mlg(t, _=None):
            i = None
            for c in range(8):
                i = t.matmul(pb[0][:, 0:8], xg[:, c * 128:(c + 1) * 128],
                             gwsb[:, c * 8:(c + 1) * 8],
                             start=(c == 0), stop=(c == 7))
            return i
        ctx.comp("tensor", mlg)
        RMAX(gcol[:, 0:1], pb[0][:, 0:8])
        TS(gcol[:, 1:2], gcol[:, 0:1], -1.0, ALU.mult)
        ACT(g8[:], pb[0][:, 0:8], AF.Exp, bias=gcol[:, 1:2])
        RSUM(gcol[:, 0:1], g8[:])
        RECIP(gcol[:, 1:2], gcol[:, 0:1])
        TS(g8[:], g8[:], gcol[:, 1:2], ALU.mult)
        RMAX(gcol[:, 0:1], g8[:])
        TS(sc[:, 0:8], g8[:], gcol[:, 0:1], ALU.is_ge)
        TS(sc[:, 0:8], sc[:, 0:8], -1e30, ALU.mult)
        TT(sc[:, 8:16], g8[:], sc[:, 0:8], ALU.add)
        RMAX(gcol[:, 1:2], sc[:, 8:16])
        TT(gcol[:, 2:3], gcol[:, 0:1], gcol[:, 1:2], ALU.add)
        TS(gcol[:, 2:3], gcol[:, 2:3], 1e-20, ALU.add)
        RECIP(gcol[:, 3:4], gcol[:, 2:3])
        TS(gcol[:, 3:4], gcol[:, 3:4], 2.5, ALU.mult)
        TS(sc[:, 0:8], g8[:], gcol[:, 1:2], ALU.is_ge)
        TT(sc[:, 8:16], sc[:, 0:8], g8[:], ALU.mult)
        TS(sc[:, 8:16], sc[:, 8:16], gcol[:, 3:4], ALU.mult)
        MM(lambda t, _=None: t.transpose(pb[1][0:8, 0:128], sc[:, 8:16], identf[:]))
        CP(cwt[:], pb[1][0:8, 0:128])
        LD1(dt["cw0"].ap()[0:8, tt * 128:(tt + 1) * 128], cwt[:])
    reg.close()

    LDL([(dt["res2"].ap()[c * 128:(c + 1) * 128, :],
          dt["res1"].ap()[c * 128:(c + 1) * 128, :]) for c in range(8)])

    def emit_moe(src, Nc, NE, F, wg_name, wu_name, wd_name, cw_name, out_name,
                 tcw, tag, relu_bias=None, be2_name=None):
        nfc = F // 128
        regm = contextlib.ExitStack()
        w1 = ctx.sbuf(tag + "moe_w1", [128, 8 * F], BF16, regm)
        w2 = ctx.sbuf(tag + "moe_w2", [128, 8 * F] if wu_name else [1, 1],
                      BF16 if wu_name else F32, regm)
        wdn = ctx.sbuf(tag + "moe_wd", [128, nfc * 1024], BF16, regm)
        hidb = ctx.sbuf(tag + "moe_hid", [128, nfc * tcw], BF16, regm)
        cwb = ctx.sbuf(tag + "moe_cwb", [128, 2 * tcw], F32, regm)
        cwr = ctx.sbuf(tag + "moe_cwr", [1, 2 * tcw], F32, regm)
        stg = ctx.sbuf(tag + "moe_stg", [128, 2 * 8 * tcw], BF16, regm)
        eob = ctx.sbuf(tag + "moe_eob", [128, 4 * tcw], F32, regm)
        src_a = dt[src].ap()
        wgf = dt[wg_name].ap().rearrange("e d f -> (e d) f")
        wuf = dt[wu_name].ap().rearrange("e d f -> (e d) f") if wu_name else None
        wdf = dt[wd_name].ap().rearrange("e f d -> (e f) d")
        ntc = Nc // tcw
        W = tag + "w"
        X1, X2 = tag + "x1", tag + "x2"
        HD = tag + "hd"
        for e in range(NE):
            wp = [(w1.ap()[:, c * F:(c + 1) * F],
                   wgf[e * D + c * 128:e * D + (c + 1) * 128, :])
                  for c in range(8)]
            if wu_name:
                wp += [(w2.ap()[:, c * F:(c + 1) * F],
                        wuf[e * D + c * 128:e * D + (c + 1) * 128, :])
                       for c in range(8)]
            wp += [(wdn.ap()[:, c * 1024:(c + 1) * 1024],
                    wdf[e * F + c * 128:e * F + (c + 1) * 128, :])
                   for c in range(nfc)]
            GDL(wp, rd=(), wr=(W,), sem="GW0")
            for tc in range(ntc):
                t0 = tc * tcw
                s = tc % 2
                SG = tag + "sg%d" % s
                CB = tag + "cb%d" % s
                stg_s = stg[:, s * 8 * tcw:(s + 1) * 8 * tcw]
                cwr_s = cwr[0:1, s * tcw:(s + 1) * tcw]
                cwb_s = cwb[:, s * tcw:(s + 1) * tcw]
                ctx.dma("sync", lambda q, t0=t0, stg_s=stg_s, cwr_s=cwr_s, e=e,
                        tcw=tcw:
                        [q.dma_start(out=stg_s.rearrange("p (c n) -> p c n", n=tcw),
                                     in_=src_a[:, t0:t0 + tcw]
                                     .rearrange("(c p) n -> p c n", p=128)),
                         q.dma_start(out=cwr_s,
                                     in_=dt[cw_name].ap()[e:e + 1, t0:t0 + tcw])],
                        2, rd=(), wr=(SG,), sem="LS%d" % s)
                bcast_row(pb[6], cwr_s, tcw, rd=(SG,), wr=("p6",))
                CP(cwb_s, pb[6][0:128, 0:tcw], rd=("p6",), wr=(CB,))
                for fc in range(nfc):
                    def mg(t, fc=fc, tcw=tcw, stg_s=stg_s):
                        i = None
                        for dc in range(8):
                            i = t.matmul(pb[4][:, 0:tcw],
                                         w1[:, dc * F + fc * 128:dc * F + (fc + 1) * 128],
                                         stg_s[:, dc * tcw:(dc + 1) * tcw],
                                         start=(dc == 0), stop=(dc == 7))
                        return i
                    ctx.comp("tensor", mg, rd=(W, SG), wr=("p4",))
                    if wu_name:
                        def mu(t, fc=fc, tcw=tcw, stg_s=stg_s):
                            i = None
                            for dc in range(8):
                                i = t.matmul(pb[5][:, 0:tcw],
                                             w2[:, dc * F + fc * 128:dc * F + (fc + 1) * 128],
                                             stg_s[:, dc * tcw:(dc + 1) * tcw],
                                             start=(dc == 0), stop=(dc == 7))
                            return i
                        ctx.comp("tensor", mu, rd=(W, SG), wr=("p5",))
                        ACT(sc[:, 0:tcw], pb[4][:, 0:tcw], AF.Sigmoid,
                            rd=("p4",), wr=(X1,))
                        TT(sc[:, 512:512 + tcw], pb[4][:, 0:tcw], sc[:, 0:tcw],
                           ALU.mult, rd=("p4", X1), wr=(X2,))
                        TT(sc[:, 0:tcw], sc[:, 512:512 + tcw], pb[5][:, 0:tcw],
                           ALU.mult, rd=(X2, "p5"), wr=(X1,))
                    else:
                        ACT(sc[:, 0:tcw], pb[4][:, 0:tcw], AF.Relu,
                            bias=biascols[relu_bias][:, e * 32 + fc:e * 32 + fc + 1],
                            rd=("p4",), wr=(X1,))
                    TT(sc[:, 512:512 + tcw], sc[:, 0:tcw], cwb_s, ALU.mult,
                       rd=(X1, CB), wr=(X2,))
                    CP(hidb[:, fc * tcw:(fc + 1) * tcw], sc[:, 512:512 + tcw],
                       rd=(X2,), wr=(HD,))
                for dg in range(2):
                    def md(t, dg=dg, nfc=nfc, tcw=tcw):
                        i = None
                        for dc in range(4):
                            dci = dg * 4 + dc
                            for fc in range(nfc):
                                i = t.matmul(pb[dc][:, 0:tcw],
                                             wdn[:, fc * 1024 + dci * 128:fc * 1024 + (dci + 1) * 128],
                                             hidb[:, fc * tcw:(fc + 1) * tcw],
                                             start=(fc == 0), stop=(fc == nfc - 1))
                        return i
                    ctx.comp("tensor", md, rd=(HD, W),
                             wr=("p0", "p1", "p2", "p3"))
                    for dc in range(4):
                        dci = dg * 4 + dc
                        EO = tag + "eo%d" % dc
                        CP(eob[:, dc * tcw:(dc + 1) * tcw], pb[dc][:, 0:tcw],
                           rd=("p%d" % dc,), wr=(EO,))
                        GACC(dt[out_name].ap()[dci * 128:(dci + 1) * 128,
                                               t0:t0 + tcw],
                             eob[:, dc * tcw:(dc + 1) * tcw],
                             rd=(EO,), wr=(tag + "a%d_%d" % (tc, dci), EO))
        if be2_name is not None:
            be2sb = xs.ap()[0:8, 0:1024]
            cw8 = xs.ap()[0:8, 1024:1536]
            LD1(be2sb, dt[be2_name].ap())
            for tc in range(Nc // 512):
                t0 = tc * 512
                LD1(cw8, dt[cw_name].ap()[0:8, t0:t0 + 512])
                for dci in range(8):
                    MM(lambda t, dci=dci: t.matmul(
                        pb[dci % 2][:, :], be2sb[:, dci * 128:(dci + 1) * 128],
                        cw8, start=True, stop=True))
                    CP(sc[:, 0:512], pb[dci % 2][:, :])
                    GACC(dt[out_name].ap()[dci * 128:(dci + 1) * 128,
                                           t0:t0 + 512], sc[:, 0:512])
        regm.close()

    emit_moe("ln1b", N, 10, 1408, "l0_wg", "l0_wu", "l0_wd", "cw0", "res2",
             512, "m0")

    if PHASE_LIMIT <= 1:
        ctx.fence("sync")
        ctx.emit(block)
        return nc

    emit_ln("res2", N, "ln2b")

    reg = contextlib.ExitStack()
    wreg1 = ctx.sbuf("wreg1a", [128, 8 * 256], BF16, reg)
    astg = ctx.sbuf("astg", [128, 8 * 512], BF16, reg)
    for (wn, on, do_pu) in (("l1a_WqT", "q1n", True), ("l1a_WkT", "k1n", True),
                            ("l1a_WvT", "v1T", False)):
        load_w(wreg1, wn, D, 256)
        for w in range(8):
            w0 = w * 512
            LD1(astg.ap().rearrange("p (c n) -> p c n", n=512),
                dt["ln2b"].ap()[:, w0:w0 + 512].rearrange("(c p) n -> p c n", p=128))
            for ch in range(2):
                def mmq(t, ch=ch):
                    i = None
                    for kc in range(8):
                        i = t.matmul(pb[0][:, :],
                                     wreg1[:, kc * 256 + ch * 128:kc * 256 + (ch + 1) * 128],
                                     astg[:, kc * 512:(kc + 1) * 512],
                                     start=(kc == 0), stop=(kc == 7))
                    return i
                ctx.comp("tensor", mmq)
                if do_pu:
                    TS(sc[:, 0:512], pb[0][:, :], 0.0, ALU.min)
                    ACT(sc[:, 512:1024], sc[:, 0:512], AF.Exp)
                    ACT(sc[:, 0:512], pb[0][:, :], AF.Relu)
                    TT(sc[:, 1024:1536], sc[:, 0:512], sc[:, 512:1024], ALU.add)
                    ACT(sc[:, 0:512], sc[:, 1024:1536], AF.Square)
                    for hh in range(2):
                        h0_ = hh * 64
                        MM(lambda t, h0_=h0_: t.matmul(
                            pb[1][0:1, 0:512], onescol[h0_:h0_ + 64, 0:1],
                            sc[h0_:h0_ + 64, 0:512], start=True, stop=True))
                        ACT(rows[0:1, 0:512], pb[1][0:1, 0:512], AF.Sqrt)
                        TS(rows[0:1, 0:512], rows[0:1, 0:512], 1e-6, ALU.add)
                        RECIP(rows[0:1, 512:1024], rows[0:1, 0:512])
                        bcast_row(pb[2], rows[0:1, 512:1024], 512, np_=64)
                        TT(sc[h0_:h0_ + 64, 1024:1536], sc[h0_:h0_ + 64, 1024:1536],
                           pb[2][0:64, 0:512], ALU.mult)
                    LD1(dt[on].ap()[ch * 128:(ch + 1) * 128, w0:w0 + 512],
                        sc[:, 1024:1536])
                else:
                    CP(sc[:, 0:512], pb[0][:, :])
                    LD1(dt[on].ap()[ch * 128:(ch + 1) * 128, w0:w0 + 512],
                        sc[:, 0:512])
    reg.close()

    reg = contextlib.ExitStack()
    kch = ctx.sbuf("a1_k", [64, 128], F32, reg)
    vch = ctx.sbuf("a1_v", [64, 128], F32, reg)
    qch = ctx.sbuf("a1_q", [64, 128], F32, reg)
    ktok = ctx.sbuf("a1_kt", [128, 64], F32, reg)
    vtok = ctx.sbuf("a1_vt", [128, 64], F32, reg)
    stt = ctx.sbuf("a1_st", [64, 64], F32, reg)
    kstc = ctx.sbuf("a1_kst", [64, 1], F32, reg)
    stm = ctx.sbuf("a1_stm", [128, 128], F32, reg)
    ybt = ctx.sbuf("a1_y", [64, 128], BF16, reg)
    dkvf = ctx.sbuf("a1_dkvf", [32, 4096], F32, reg)
    dkm = ctx.sbuf("a1_dkm", [32, 64], F32, reg)
    for hh in range(4):
        for c in range(32):
            r = hh * 32 + c
            LDL([(kch[:], dt["k1n"].ap()[hh * 64:hh * 64 + 64, c * 128:(c + 1) * 128]),
                 (vch[:], dt["v1T"].ap()[hh * 64:hh * 64 + 64, c * 128:(c + 1) * 128])])
            MM(lambda t, _=None: t.transpose(pb[0][:, 0:64], kch[:],
                                             identf[0:64, 0:64]))
            CP(ktok[:], pb[0][:, 0:64])
            MM(lambda t, _=None: t.transpose(pb[1][:, 0:64], vch[:],
                                             identf[0:64, 0:64]))
            CP(vtok[:], pb[1][:, 0:64])
            MM(lambda t, _=None: t.matmul(pb[2][0:64, 0:64], ktok[:], vtok[:],
                                          start=True, stop=True))
            CP(stt[:], pb[2][0:64, 0:64])
            LD1(dt["dkv1"].ap()[0:64, r * 64:(r + 1) * 64], stt[:])
            MM(lambda t, _=None: t.matmul(pb[3][0:1, 0:64], onescol[:], ktok[:],
                                          start=True, stop=True))
            CP(rows[0:1, 0:64], pb[3][0:1, 0:64])
            LD1(dt["dk1T"].ap()[r:r + 1, 0:64], rows[0:1, 0:64])
    for hh in range(4):
        LD1(dkvf.ap().rearrange("c (p j) -> c p j", j=64),
            dt["dkv1"].ap()[:, hh * 2048:(hh + 1) * 2048]
            .rearrange("p (c j) -> c p j", j=64))
        for w in range(8):
            MM(lambda t, w=w: t.matmul(pb[w % 2][0:32, :], m32sb[:],
                                       dkvf[:, w * 512:(w + 1) * 512],
                                       start=True, stop=True))
            CP(sc[0:32, 0:512], pb[w % 2][0:32, :])
            LD1(dt["st1"].ap()[w * 8:(w + 1) * 8, hh * 2048:(hh + 1) * 2048]
                .rearrange("p (c j) -> c p j", j=64),
                sc[0:32, 0:512].rearrange("c (p j) -> c p j", j=64))
        LD1(dkm[:], dt["dk1T"].ap()[hh * 32:(hh + 1) * 32, 0:64])
        MM(lambda t, _=None: t.matmul(pb[2][0:32, 0:64], m32sb[:], dkm[:],
                                      start=True, stop=True))
        CP(sc[0:32, 0:64], pb[2][0:32, 0:64])
        MM(lambda t, _=None: t.transpose(pb[3][0:64, 0:32], sc[0:32, 0:64],
                                         identf[0:32, 0:32]))
        CP(sc[0:64, 512:544], pb[3][0:64, 0:32])
        LD1(dt["kstT"].ap()[0:64, hh * 32:(hh + 1) * 32], sc[0:64, 512:544])
    for hh in range(4):
        for c in range(32):
            r = hh * 32 + c
            LDL([(qch[:], dt["q1n"].ap()[hh * 64:hh * 64 + 64, c * 128:(c + 1) * 128]),
                 (kch[:], dt["k1n"].ap()[hh * 64:hh * 64 + 64, c * 128:(c + 1) * 128]),
                 (vch[:], dt["v1T"].ap()[hh * 64:hh * 64 + 64, c * 128:(c + 1) * 128]),
                 (stt[:], dt["st1"].ap()[0:64, r * 64:(r + 1) * 64])])
            ctx.dma("sync", lambda s, r=r:
                    [s.dma_start(out=kstc[:], in_=dt["kstT"].ap()[0:64, r:r + 1],
                                 allow_slow_non_contiguous=True)], 1)
            MM(lambda t, _=None: t.transpose(pb[0][:, 0:64], vch[:],
                                             identf[0:64, 0:64]))
            CP(vtok[:], pb[0][:, 0:64])
            MM(lambda t, _=None: t.matmul(pb[1][:, 0:128], kch[:], qch[:],
                                          start=True, stop=True))
            TT(stm[:], pb[1][:, 0:128], triu[:], ALU.mult)

            def mden(t, _=None):
                t.matmul(pb[2][0:1, 0:128], onescol[:], stm[:],
                         start=True, stop=False)
                return t.matmul(pb[2][0:1, 0:128], kstc[:], qch[:],
                                start=False, stop=True)
            ctx.comp("tensor", mden)

            def mnum(t, _=None):
                t.matmul(pb[3][0:64, 0:128], vtok[:], stm[:],
                         start=True, stop=False)
                return t.matmul(pb[3][0:64, 0:128], stt[:], qch[:],
                                start=False, stop=True)
            ctx.comp("tensor", mnum)
            TS(rows[0:1, 0:128], pb[2][0:1, 0:128], 1e-6, ALU.add)
            RECIP(rows[0:1, 128:256], rows[0:1, 0:128])
            MM(lambda t, _=None: t.matmul(pb[4][0:64, 0:128], onesrow[0:1, 0:64],
                                          rows[0:1, 128:256], start=True, stop=True))
            CP(sc[0:64, 0:128], pb[3][0:64, 0:128])
            TT(ybt[:], sc[0:64, 0:128], pb[4][0:64, 0:128], ALU.mult)
            LD1(dt["y1b"].ap()[hh * 64:hh * 64 + 64, c * 128:(c + 1) * 128], ybt[:])
    reg.close()

    reg = contextlib.ExitStack()
    wreg1o = ctx.sbuf("wreg1o", [128, 2 * 1024], BF16, reg)
    emit_proj("y1b", 256, "l1a_WoT", D, N, wreg1o, res="res2", res_out="res3")
    reg.close()

    if PHASE_LIMIT <= 2:
        ctx.fence("sync")
        ctx.emit(block)
        return nc

    emit_ln("res3", N, "ln3b", out_f32="ln3f")

    reg = contextlib.ExitStack()
    wreg2c = ctx.sbuf("wreg2c", [128, 8 * 1024], BF16, reg)
    emit_proj("ln3b", D, "l1f_Wqkv", 1024, HN, wreg2c, out_bf="qT1f",
              wcols=(0, 1024))
    emit_proj("ln3b", D, "l1f_Wqkv", 1024, N, wreg2c, out_bf="kT1f",
              wcols=(1024, 2048))
    emit_proj("ln3b", D, "l1f_Wqkv", 1024, N, wreg2c, out_bf="vT1f",
              wcols=(2048, 3072))
    emit_proj("peT", D, "l1f_Wr", 1024, PE_W, wreg2c, out_bf="rTs")
    reg.close()

    emit_tok_transpose("vT1f", D, N, "vtok1f")

    if PHASE_LIMIT <= 3:
        ctx.fence("sync")
        ctx.emit(block)
        return nc


    reg = contextlib.ExitStack()
    vtkf = ctx.sbuf("f_vtok", [128, 32 * 1024], BF16, reg)
    ksb2 = ctx.sbuf("f_ksb", [64, 2 * N], BF16, reg)
    rsb2 = ctx.sbuf("f_rsb", [64, 2 * PE_W], BF16, reg)
    qb2 = ctx.sbuf("f_qb2", [64, 6 * 128], BF16, reg)
    scos = ctx.sbuf("f_scos", [128, 4224], F32, reg)
    attb = ctx.sbuf("f_attb", [128, 4224], BF16, reg)
    atk2 = ctx.sbuf("f_atk", [128, 2 * 512], BF16, reg)
    bdg2 = ctx.sbuf("f_bdgt", [128, 2 * 512], BF16, reg)
    bde2 = ctx.sbuf("f_bde", [128, 2 * 512], BF16, reg)
    ctl2 = ctx.sbuf("f_ctl", [128, 2 * 64], BF16, reg)
    acol = ctx.sbuf("f_acol", [128, 4], F32, reg)
    dmask = ctx.sbuf("c_dmask", [128, 2048], F32, reg)
    kmbuf = ctx.sbuf("c_kmbuf", [128, 2048], F32, reg)
    kmrow = ctx.sbuf("c_kmrow", [1, HN], F32, reg)
    LDL([(dmask[:], dt["diagmask"].ap()),
         (kmrow[:], dt["kmaskrow"].ap())])
    GDL([(vtkf.ap()[:, c * 4096:(c + 1) * 4096],
          dt["vtok1f"].ap()[:, c * 4096:(c + 1) * 4096]) for c in range(8)],
        rd=(), wr=("VT",), sem="GW0")
    for j in range(4):
        bcast_row(pb[6], kmrow[0:1, j * 512:(j + 1) * 512], 512)
        CP(kmbuf[:, j * 512:(j + 1) * 512], pb[6][:, 0:512])
    for hh in range(16):
        hs = hh % 2
        KS = "KS%d" % hs
        ksb = ksb2[:, hs * N:(hs + 1) * N]
        rsb = rsb2[:, hs * PE_W:(hs + 1) * PE_W]
        GDL([(ksb, dt["kT1f"].ap()[hh * 64:hh * 64 + 64, :]),
             (rsb, dt["rTs"].ap()[hh * 64:hh * 64 + 64, :])],
            rd=(), wr=(KS,), sem="GW1")
        for qb in range(16):
            q0 = 128 * qb
            KA = q0 + 128
            nwA = (KA + 511) // 512
            KT = KA + 2048
            s = qb % 2
            QT, QU, QV = "QT%d" % s, "QU%d" % s, "QV%d" % s
            qtl = qb2[:, s * 128:(s + 1) * 128]
            qub = qb2[:, (2 + s) * 128:(3 + s) * 128]
            qvb = qb2[:, (4 + s) * 128:(5 + s) * 128]
            LD1(qtl, dt["qT1f"].ap()[hh * 64:hh * 64 + 64, q0:q0 + 128],
                rd=(), wr=(QT,), sem="LS%d" % s)
            TS(qub, qtl, ubT[:, hh:hh + 1], ALU.add, rd=(QT,), wr=(QU,))
            TS(qvb, qtl, vbT[:, hh:hh + 1], ALU.add, rd=(QT,), wr=(QV,))
            wi = 0
            for (bd_name, u0, Wd) in (("BDa", 4096 - q0, (nwA - 1) * 512 + 640),
                                      ("BDb", 2048 - q0, 2176)):
                BT = "B_" + bd_name
                for w0 in range(0, Wd, 512):
                    wa = min(512, Wd - w0)
                    bk = pb[0] if wi % 2 == 0 else pb[5]
                    PK = "pBD%d" % (wi % 2)
                    BE = "be%d" % (wi % 2)
                    bde = bde2[:, (wi % 2) * 512:(wi % 2) * 512 + wa]
                    MM(lambda t, u0=u0, w0=w0, wa=wa, bk=bk, qvb=qvb, rsb=rsb:
                       t.matmul(bk[:, 0:wa], qvb,
                                rsb[0:64, u0 + w0:u0 + w0 + wa],
                                start=True, stop=True),
                       rd=(QV, KS), wr=(PK,))
                    SCP(bde, bk[:, 0:wa], rd=(PK,), wr=(BE,))
                    ctx.dma("gpsimd", lambda g, bd_name=bd_name, w0=w0, wa=wa,
                            bde=bde:
                            [g.dma_start(out=dt[bd_name].ap()[:, w0:w0 + wa],
                                         in_=bde)],
                            1, rd=(BE,), wr=(BT,), sem="ST0")
                    wi += 1
            for kw in range(nwA + 4):
                isA = kw < nwA
                if isA:
                    kw0 = kw * 512
                    w = min(512, KA - kw0)
                    ksl = ksb[0:64, kw0:kw0 + w]
                    bd_t, BT = dt["BDa"], "B_BDa"
                else:
                    j = kw - nwA
                    kw0 = j * 512
                    w = 512
                    ksl = ksb[0:64, 2048 + kw0:2048 + kw0 + w]
                    bd_t, BT = dt["BDb"], "B_BDb"
                ws = kw % 2
                bk = pb[1] if ws == 0 else pb[2]
                PS = "pS%d" % ws
                BG = "bg%d" % ws
                bdg = bdg2[:, ws * 512:ws * 512 + w]
                MM(lambda t, ksl=ksl, w=w, bk=bk, qub=qub:
                   t.matmul(bk[:, 0:w], qub, ksl, start=True, stop=True),
                   rd=(QU, KS), wr=(PS,))
                LD1(bdg, AP(bd_t, kw0 + 127, [[BDW - 1, 128], [1, w]]),
                    rd=(BT,), wr=(BG,), sem="LS%d" % ws)
                col0 = kw0 if isA else KA + kw0
                TT(scos[:, col0:col0 + w], bk[:, 0:w], bdg, ALU.add,
                   rd=(PS, BG), wr=("SCS",))
                SMUL(scos[:, col0:col0 + w], scos[:, col0:col0 + w], 0.125,
                     rd=("SCS",), wr=("SCS",))
                if isA and kw == nwA - 1:
                    a = w // 128 - 1
                    TT(scos[:, col0:col0 + w], scos[:, col0:col0 + w],
                       dmask[:, a * 512:a * 512 + w], ALU.add,
                       rd=("SCS",), wr=("SCS",))
                elif not isA:
                    TT(scos[:, col0:col0 + w], scos[:, col0:col0 + w],
                       kmbuf[:, kw0:kw0 + w], ALU.add,
                       rd=("SCS",), wr=("SCS",))
            RMAX(acol[:, 0:1], scos[:, 0:KT], rd=("SCS",), wr=("MX",))
            TS(acol[:, 1:2], acol[:, 0:1], -1.0, ALU.mult, rd=("MX",), wr=("MX",))
            ACT(xs[:, 0:KT], scos[:, 0:KT], AF.Exp, bias=acol[:, 1:2],
                rd=("SCS", "MX"), wr=("XP",))
            RSUM(acol[:, 2:3], xs[:, 0:KT], rd=("XP",), wr=("DN",))
            RECIP(acol[:, 3:4], acol[:, 2:3], rd=("DN",), wr=("DN",))
            TS(attb[:, 0:KT], xs[:, 0:KT], acol[:, 3:4], ALU.mult,
               rd=("XP", "DN"), wr=("AT",))
            nkc = KT // 128
            kc = 0
            while kc < nkc:
                gn = min(4, nkc - kc)
                gs = (kc // 4) % 2
                AK = "ak%d" % gs
                atk = atk2[:, gs * 512:gs * 512 + gn * 128]

                def tpg(t, kc=kc, gn=gn):
                    i = None
                    for j in range(gn):
                        i = t.transpose(pbb[3][:, j * 128:(j + 1) * 128],
                                        attb[:, (kc + j) * 128:(kc + j + 1) * 128],
                                        identb[:])
                    return i
                ctx.comp("tensor", tpg, rd=("AT",), wr=("pT",))
                SCP(atk, pbb[3][:, 0:gn * 128], rd=("pT",), wr=(AK,))

                def mmc(t, kc=kc, gn=gn, qb=qb, hh=hh, nkc=nkc, atk=atk):
                    i = None
                    for j in range(gn):
                        kcj = kc + j
                        phys = kcj if kcj < qb + 1 else 16 + (kcj - (qb + 1))
                        i = t.matmul(pb[4][:, 0:64],
                                     atk[:, j * 128:(j + 1) * 128],
                                     vtkf[:, phys * 1024 + hh * 64:phys * 1024 + hh * 64 + 64],
                                     start=(kcj == 0), stop=(kcj == nkc - 1))
                    return i
                ctx.comp("tensor", mmc, rd=(AK, "VT"), wr=("p4",))
                kc += gn
            ctl = ctl2[:, s * 64:(s + 1) * 64]
            SCP(ctl, pb[4][:, 0:64], rd=("p4",), wr=("CT%d" % s,))
            LD1(dt["ctxtok"].ap()[:, qb * 1024 + hh * 64:qb * 1024 + hh * 64 + 64],
                ctl, rd=("CT%d" % s,), wr=("d_ctxtok",), sem="LS%d" % s)
    reg.close()

    for qb in range(16):
        for f4 in range(2):
            LD1(tstg[:, 0:512],
                dt["ctxtok"].ap()[:, qb * 1024 + f4 * 512:qb * 1024 + (f4 + 1) * 512])

            def tpc(t, _=None):
                i = None
                for j in range(4):
                    i = t.transpose(pbb[5][:, j * 128:(j + 1) * 128],
                                    tstg[:, j * 128:(j + 1) * 128], identb[:])
                return i
            ctx.comp("tensor", tpc)
            CP(bst[:, 0:512], pbb[5][:, 0:512])
            LD1(dt["ctxT"].ap()[f4 * 512:(f4 + 1) * 512, qb * 128:(qb + 1) * 128]
                .rearrange("(f p) n -> p f n", p=128),
                bst[:, 0:512].rearrange("p (f n) -> p f n", n=128))

    reg = contextlib.ExitStack()
    wregf = ctx.sbuf("wregf", [128, 8 * 4096], BF16, reg)
    emit_proj("ctxT", D, "l1f_Wo", D, HN, wregf, res="ln3f", res_out="h0")
    reg.close()

    emit_ln("h0", HN, "h1b", out_f32="h1f", gb_idx=0)

    reg = contextlib.ExitStack()
    wregf1 = ctx.sbuf("wregf1", [128, 8 * 4096], BF16, reg)
    emit_proj("h1b", D, "l1f_Wff1", 4096, HN, wregf1, out_bf="ffh",
              bias="l1f_bff1", act="relu")
    reg.close()
    reg = contextlib.ExitStack()
    wregf2 = ctx.sbuf("wregf2", [128, 32 * 1024], BF16, reg)
    emit_proj("ffh", 4096, "l1f_Wff2", D, HN, wregf2, bias="l1f_bff2",
              res="h1f", res_out="h2pre")
    reg.close()

    emit_ln("h2pre", HN, "h2b", out_f32="h2f", gb_idx=1)

    reg = contextlib.ExitStack()
    gwsb1 = ctx.sbuf("gwsb1", [128, 64], F32, reg)
    xg1 = ctx.sbuf("xg1", [128, 8 * 128], F32, reg)
    g81 = ctx.sbuf("g81", [128, 8], F32, reg)
    gcol1 = ctx.sbuf("gcol1", [128, 4], F32, reg)
    cwt1 = ctx.sbuf("cwt1", [8, 128], F32, reg)
    LD1(gwsb1.ap().rearrange("p (c m) -> p c m", m=8),
        dt["l1f_Wg"].ap().rearrange("(c p) m -> p c m", p=128))
    for tt in range(16):
        LD1(xg1.ap().rearrange("p (c n) -> p c n", n=128),
            dt["h2f"].ap()[:, tt * 128:(tt + 1) * 128]
            .rearrange("(c p) n -> p c n", p=128))

        def mlg1(t, _=None):
            i = None
            for c in range(8):
                i = t.matmul(pb[0][:, 0:8], xg1[:, c * 128:(c + 1) * 128],
                             gwsb1[:, c * 8:(c + 1) * 8],
                             start=(c == 0), stop=(c == 7))
            return i
        ctx.comp("tensor", mlg1)
        RMAX(gcol1[:, 0:1], pb[0][:, 0:8])
        TS(gcol1[:, 1:2], gcol1[:, 0:1], -1.0, ALU.mult)
        ACT(g81[:], pb[0][:, 0:8], AF.Exp, bias=gcol1[:, 1:2])
        RSUM(gcol1[:, 0:1], g81[:])
        RECIP(gcol1[:, 1:2], gcol1[:, 0:1])
        TS(g81[:], g81[:], gcol1[:, 1:2], ALU.mult)
        RMAX(gcol1[:, 0:1], g81[:])
        TS(sc[:, 0:8], g81[:], gcol1[:, 0:1], ALU.is_ge)
        TT(sc[:, 8:16], sc[:, 0:8], g81[:], ALU.mult)
        MM(lambda t, _=None: t.transpose(pb[1][0:8, 0:128], sc[:, 8:16], identf[:]))
        CP(cwt1[:], pb[1][0:8, 0:128])
        LD1(dt["cw1"].ap()[0:8, tt * 128:(tt + 1) * 128], cwt1[:])
    reg.close()

    LDL([(dt["h3pre"].ap()[c * 128:(c + 1) * 128, :],
          dt["h2f"].ap()[c * 128:(c + 1) * 128, :]) for c in range(8)])
    emit_moe("h2b", HN, 8, 4096, "l1f_We1", None, "l1f_We2", "cw1", "h3pre",
             256, "m1", relu_bias="l1f_be1", be2_name="l1f_be2")

    emit_ln("h3pre", HN, "h3b", gb_idx=2)

    reg = contextlib.ExitStack()
    wrego = ctx.sbuf("wrego", [128, 8 * 1024], BF16, reg)
    emit_proj("h3b", D, "l1f_Wout", D, HN, wrego, res="res3", res_out="y")
    reg.close()

    for nm in DUMPS:
        shape, dty = dsh[nm]
        du = nc.dram_tensor("dump_" + nm, list(shape), dty, kind="ExternalOutput")
        R = shape[0]
        step = 128 if R >= 128 else R
        LDL([(du.ap()[r0:r0 + step, :], dt[nm].ap()[r0:r0 + step, :])
             for r0 in range(0, R, step)])
    ctx.fence("sync")
    ctx.emit(block)
    return nc


_NC_CACHE = []


def kernel(**inp):
    maps = prep_in_maps(inp)
    if not _NC_CACHE:
        _NC_CACHE.append(build_nc())
    nc = _NC_CACHE[0]
    res = run_bass_kernel_spmd(nc, maps, core_ids=list(range(8)))
    out = np.zeros((B, S, D), np.float32)
    for c in range(8):
        b, h = divmod(c, 2)
        y = np.asarray(res.results[c]["y"], np.float32)
        out[b, h * HN:(h + 1) * HN, :] = y.T
    return out


# version 06 (final)
# speedup vs baseline: 1.6216x; 1.0742x over previous
import contextlib
import numpy as np
import ml_dtypes
import concourse.bass as bass
import concourse.mybir as mybir
from concourse.bass import AP
from concourse.bass_utils import run_bass_kernel_spmd

F32 = mybir.dt.float32
BF16 = mybir.dt.bfloat16
AF = mybir.ActivationFunctionType
ALU = mybir.AluOpType

B, S, D = 4, 4096, 1024
N = 4096
HN = 2048
PE_W = 6656
BDW = 2304

bf16 = ml_dtypes.bfloat16


def _bf(a):
    return np.ascontiguousarray(np.asarray(a, dtype=np.float32).astype(bf16))


def _f32(a):
    return np.ascontiguousarray(np.asarray(a), dtype=np.float32)


def _cols(v):
    return np.ascontiguousarray(np.asarray(v, np.float32).reshape(-1, 128).T)


def prep_in_maps(inp):
    sh = {}
    sh["l0_WqT"] = _bf(np.asarray(inp["L0_attn_Wq"]).T)
    sh["l0_WkT"] = _bf(np.asarray(inp["L0_attn_Wk"]).T)
    sh["l0_WvT"] = _bf(np.asarray(inp["L0_attn_Wv"]).T)
    sh["l0_WoT"] = _bf(np.asarray(inp["L0_attn_Wo"]).T)
    sh["l0_bq"] = _cols(inp["L0_attn_bq"])
    sh["l0_bk"] = _cols(inp["L0_attn_bk"])
    sh["l0_bv"] = _cols(inp["L0_attn_bv"])
    sh["l0_bo"] = _cols(inp["L0_attn_bo"])
    F = 1408
    wg = np.asarray(inp["L0_ffn_w_gate"]); wu = np.asarray(inp["L0_ffn_w_up"])
    wd = np.asarray(inp["L0_ffn_w_down"])
    swg = np.asarray(inp["L0_ffn_sw_gate"]); swu = np.asarray(inp["L0_ffn_sw_up"])
    swd = np.asarray(inp["L0_ffn_sw_down"])
    wg_all = np.zeros((10, D, F), np.float32)
    wu_all = np.zeros((10, D, F), np.float32)
    wd_all = np.zeros((10, F, D), np.float32)
    wg_all[:8] = wg; wu_all[:8] = wu; wd_all[:8] = wd
    wg_all[8] = swg[:, :F]; wu_all[8] = swu[:, :F]; wd_all[8] = swd[:F]
    wg_all[9] = swg[:, F:]; wu_all[9] = swu[:, F:]; wd_all[9] = swd[F:]
    sh["l0_wg"] = _bf(wg_all); sh["l0_wu"] = _bf(wu_all); sh["l0_wd"] = _bf(wd_all)
    sh["l0_gwT"] = _f32(np.asarray(inp["L0_ffn_gate_w"]).T)
    cw0 = np.zeros((10, N), np.float32); cw0[8:] = 1.0
    sh["cw0_init"] = cw0
    sh["l1a_WqT"] = _bf(np.asarray(inp["L1_attn_Wq"]).T)
    sh["l1a_WkT"] = _bf(np.asarray(inp["L1_attn_Wk"]).T)
    sh["l1a_WvT"] = _bf(np.asarray(inp["L1_attn_Wv"]).T)
    sh["l1a_WoT"] = _bf(np.asarray(inp["L1_attn_Wo"]).T)
    sh["l1f_Wqkv"] = _bf(inp["L1_ffn_Wqkv"])
    sh["l1f_Wo"] = _bf(inp["L1_ffn_Wo"])
    sh["l1f_Wr"] = _bf(inp["L1_ffn_Wr"])
    sh["l1f_ubT"] = _f32(np.asarray(inp["L1_ffn_u_bias"]).T)
    sh["l1f_vbT"] = _f32(np.asarray(inp["L1_ffn_v_bias"]).T)
    gb = np.zeros((128, 48), np.float32)
    for i, nm in enumerate(["ln1_g", "ln1_b", "ln2_g", "ln2_b", "ln3_g", "ln3_b"]):
        gb[:, i * 8:(i + 1) * 8] = _cols(inp["L1_ffn_" + nm])
    sh["l1f_gb"] = gb
    sh["l1f_Wff1"] = _bf(inp["L1_ffn_Wff1"])
    sh["l1f_Wff2"] = _bf(inp["L1_ffn_Wff2"])
    sh["l1f_bff1"] = _cols(inp["L1_ffn_bff1"])
    sh["l1f_bff2"] = _cols(inp["L1_ffn_bff2"])
    sh["l1f_Wg"] = _f32(inp["L1_ffn_Wg"])
    sh["l1f_We1"] = _bf(inp["L1_ffn_We1"])
    sh["l1f_We2"] = _bf(inp["L1_ffn_We2"])
    be1 = np.asarray(inp["L1_ffn_be1"])
    sh["l1f_be1"] = np.ascontiguousarray(
        be1.reshape(8, 32, 128).transpose(2, 0, 1).reshape(128, 256))
    sh["l1f_be2"] = _f32(inp["L1_ffn_be2"])
    sh["l1f_Wout"] = _bf(inp["L1_ffn_Wout"])
    invf = 1.0 / (10000.0 ** (np.arange(0, D, 2, dtype=np.float64) / D))
    dist = 4223 - np.arange(PE_W, dtype=np.float64)
    valid = (dist >= 0) & (dist <= S - 1)
    ang = dist[None, :] * invf[:, None]
    peT = np.concatenate([np.sin(ang), np.cos(ang)], axis=0)
    peT[:, ~valid] = 0.0
    sh["peT"] = np.ascontiguousarray(peT.astype(bf16))
    sh["ident_b"] = np.eye(128, dtype=np.float32).astype(bf16)
    sh["ident_f"] = np.eye(128, dtype=np.float32)
    sh["onescol"] = np.ones((128, 1), np.float32)
    sh["onescolb"] = np.ones((128, 1), np.float32).astype(bf16)
    sh["onesrow"] = np.ones((1, 512), np.float32)
    sh["triu"] = np.triu(np.ones((128, 128), np.float32))
    dm = np.zeros((128, 4 * 512), np.float32)
    for a in range(4):
        w = 128 * (a + 1)
        p = np.arange(128)[:, None]; f = np.arange(512)[None, :]
        dm[:, a * 512:(a + 1) * 512] = np.where(f > p + (w - 128), -1e30, 0.0)
    sh["diagmask"] = dm
    x = np.asarray(inp["L0_attn_x"])
    maps = []
    for c in range(8):
        b, h = c // 2, c % 2
        m = dict(sh)
        xT = np.asarray(x[b], np.float32).T
        if h == 0:
            m["x"] = _f32(xT)
        else:
            m["x"] = _f32(np.concatenate([xT[:, HN:], xT[:, :HN]], axis=1))
        lg = (np.arange(32) + (16 if h else 0)) % 32
        m["M32"] = np.ascontiguousarray((lg[:, None] < lg[None, :]).astype(np.float32))
        mr = np.zeros((1, HN), np.float32)
        if h == 0:
            mr[:] = -1e30
        m["kmaskrow"] = mr
        maps.append(m)
    return maps


ENGS = ("tensor", "vector", "scalar", "sync", "gpsimd")
XSEMS = ("LS0", "LS1", "GW0", "GW1", "ST0")
SEMS = ENGS + XSEMS


class Ctx:

    def __init__(self, nc, stack):
        self.nc, self.stack = nc, stack
        self.sems = {e: stack.enter_context(nc.semaphore("S_" + e)) for e in SEMS}
        self.items = []
        self.counts = {e: 0 for e in SEMS}
        self.eng_last = {e: {o: 0 for o in SEMS} for e in ENGS}
        self.writers = {}
        self.readers = {}

    def sbuf(self, name, shape, dtype, region=None):
        st = region if region is not None else self.stack
        return st.enter_context(self.nc.sbuf_tensor(name, shape, dtype))

    def _add(self, eng, fn, n_inc, rd=None, wr=None, sem=None):
        sem = sem if sem is not None else eng
        deps = []
        if rd is None and wr is None:
            for o in SEMS:
                if self.counts[o]:
                    deps.append((o, self.counts[o]))
            self.writers["__L__"] = (sem, self.counts[sem] + n_inc)
        else:
            rd = rd or ()
            wr = wr or ()
            if "__L__" in self.writers:
                deps.append(self.writers["__L__"])
            for t in rd:
                if t in self.writers:
                    deps.append(self.writers[t])
            for t in wr:
                if t in self.writers:
                    deps.append(self.writers[t])
                for s, c in self.readers.get(t, {}).items():
                    deps.append((s, c))
            after = (sem, self.counts[sem] + n_inc)
            for t in wr:
                self.writers[t] = after
                self.readers[t] = {}
            for t in rd:
                m = self.readers.setdefault(t, {})
                if after[1] > m.get(sem, 0):
                    m[sem] = after[1]
        waits = []
        for (s, c) in deps:
            if s == eng and eng == "tensor":
                continue
            if c > self.counts[s]:
                c = self.counts[s]
            if c > self.eng_last[eng][s]:
                waits.append((s, c))
                self.eng_last[eng][s] = c
        self.items.append((eng, fn, n_inc, waits, sem))
        self.counts[sem] += n_inc

    def comp(self, eng, fn, rd=None, wr=None):
        self._add(eng, fn, 1, rd, wr)

    def dma(self, eng, fn, n, rd=None, wr=None, sem=None):
        self._add(eng, fn, 16 * n, rd, wr, sem)

    def fence(self, eng):
        self._add(eng, None, 0)

    def emit(self, block):
        sems = self.sems
        for e in ENGS:
            items = [(fn, n, w, sm) for (eng, fn, n, w, sm) in self.items
                     if eng == e]

            def section(h, e=e, items=items):
                for fn, n_inc, waits, sm in items:
                    for (o, v) in waits:
                        h.wait_ge(sems[o], v)
                    if n_inc == 0:
                        continue
                    if n_inc == 1:
                        fn(h).then_inc(sems[sm], 1)
                    else:
                        instrs = fn(h)
                        assert len(instrs) * 16 == n_inc, (e, len(instrs), n_inc)
                        for i in instrs:
                            i.then_inc(sems[sm], 16)
            getattr(block, e)(section)


DUMPS = []
PHASE_LIMIT = 99


def build_nc():
    nc = bass.Bass("TRN2", target_bir_lowering=False, detect_race_conditions=False)
    stack = contextlib.ExitStack()
    dt = {}
    dsh = {}

    def din(name, shape, dtype):
        dt[name] = nc.dram_tensor(name, list(shape), dtype, kind="ExternalInput")
        dsh[name] = (shape, dtype)

    def dint(name, shape, dtype):
        dt[name] = nc.dram_tensor(name, list(shape), dtype, kind="Internal")
        dsh[name] = (shape, dtype)

    din("x", [D, N], F32)
    y_d = nc.dram_tensor("y", [D, HN], F32, kind="ExternalOutput")
    dt["y"] = y_d
    for nm in ("l0_WqT", "l0_WkT", "l0_WvT", "l0_WoT"):
        din(nm, [D, D], BF16)
    for nm in ("l0_bq", "l0_bk", "l0_bv", "l0_bo"):
        din(nm, [128, 8], F32)
    din("l0_wg", [10, D, 1408], BF16)
    din("l0_wu", [10, D, 1408], BF16)
    din("l0_wd", [10, 1408, D], BF16)
    din("l0_gwT", [D, 8], F32)
    din("cw0_init", [10, N], F32)
    din("l1a_WqT", [D, 256], BF16); din("l1a_WkT", [D, 256], BF16)
    din("l1a_WvT", [D, 256], BF16); din("l1a_WoT", [256, D], BF16)
    din("l1f_Wqkv", [D, 3072], BF16); din("l1f_Wo", [D, D], BF16)
    din("l1f_Wr", [D, D], BF16)
    din("l1f_ubT", [64, 16], F32); din("l1f_vbT", [64, 16], F32)
    din("l1f_gb", [128, 48], F32)
    din("l1f_Wff1", [D, 4096], BF16); din("l1f_Wff2", [4096, D], BF16)
    din("l1f_bff1", [128, 32], F32); din("l1f_bff2", [128, 8], F32)
    din("l1f_Wg", [D, 8], F32)
    din("l1f_We1", [8, D, 4096], BF16); din("l1f_We2", [8, 4096, D], BF16)
    din("l1f_be1", [128, 256], F32); din("l1f_be2", [8, D], F32)
    din("l1f_Wout", [D, D], BF16)
    din("peT", [D, PE_W], BF16)
    din("ident_b", [128, 128], BF16); din("ident_f", [128, 128], F32)
    din("onescol", [128, 1], F32); din("onescolb", [128, 1], BF16)
    din("onesrow", [1, 512], F32)
    din("triu", [128, 128], F32)
    din("diagmask", [128, 2048], F32)
    din("M32", [32, 32], F32)
    din("kmaskrow", [1, HN], F32)

    dint("ln0b", [D, N], BF16)
    dint("q0b", [D, N], BF16); dint("k0b", [D, N], BF16); dint("v0b", [D, N], BF16)
    dint("k0tok", [128, 32 * 1024], BF16); dint("v0tok", [128, 32 * 1024], BF16)
    dint("kv0b", [D, D], BF16); dint("ksum0", [128, 8], F32)
    dint("c0b", [D, N], BF16)
    dint("res1", [D, N], F32)
    dint("ln1b", [D, N], BF16); dint("ln1f", [D, N], F32)
    dint("cw0", [10, N], F32)
    dint("res2", [D, N], F32)
    dint("ln2b", [D, N], BF16)
    dint("q1n", [256, N], F32); dint("k1n", [256, N], F32); dint("v1T", [256, N], F32)
    dint("dkv1", [64, 4 * 2048], F32)
    dint("dk1T", [128, 64], F32)
    dint("st1", [64, 4 * 2048], F32)
    dint("kstT", [64, 128], F32)
    dint("y1b", [256, N], BF16)
    dint("res3", [D, N], F32)
    dint("ln3b", [D, N], BF16); dint("ln3f", [D, N], F32)
    dint("qT1f", [D, HN], BF16)
    dint("kT1f", [D, N], BF16); dint("vT1f", [D, N], BF16)
    dint("rTs", [D, PE_W], BF16)
    dint("vtok1f", [128, 32 * 1024], BF16)
    dint("BDa", [128, BDW], BF16)
    dint("BDb", [128, BDW], BF16)
    dint("ctxtok", [128, 16 * 1024], BF16)
    dint("ctxT", [D, HN], BF16)
    dint("h0", [D, HN], F32)
    dint("h1b", [D, HN], BF16); dint("h1f", [D, HN], F32)
    dint("ffh", [4096, HN], BF16)
    dint("h2pre", [D, HN], F32)
    dint("h2b", [D, HN], BF16); dint("h2f", [D, HN], F32)
    dint("cw1", [8, HN], F32)
    dint("h3pre", [D, HN], F32)
    dint("h3b", [D, HN], BF16)

    block = stack.enter_context(nc.Block())
    ctx = Ctx(nc, stack)
    pb = [stack.enter_context(nc.psum_tensor("pb%d" % i, [128, 512], F32))
          for i in range(8)]
    pbb = [p.ap().bitcast(BF16) for p in pb]

    onescol = ctx.sbuf("c_onescol", [128, 1], F32)
    onescolb = ctx.sbuf("c_onescolb", [128, 1], BF16)
    onesrow = ctx.sbuf("c_onesrow", [1, 512], F32)
    identb = ctx.sbuf("c_identb", [128, 128], BF16)
    identf = ctx.sbuf("c_identf", [128, 128], F32)
    triu = ctx.sbuf("c_triu", [128, 128], F32)
    ubT = ctx.sbuf("c_ubT", [64, 16], F32)
    vbT = ctx.sbuf("c_vbT", [64, 16], F32)
    gbcols = ctx.sbuf("c_gb", [128, 48], F32)
    m32sb = ctx.sbuf("c_m32", [32, 32], F32)

    xs = ctx.sbuf("g_xs", [128, 8 * 512], F32)
    sc = ctx.sbuf("g_sc", [128, 4 * 512], F32)
    rows = ctx.sbuf("g_rows", [1, 3 * 512], F32)
    bst = ctx.sbuf("g_bst", [128, 512], BF16)
    tstg = ctx.sbuf("g_tstg", [128, 512], BF16)

    def TT(out, a, b, op, rd=None, wr=None):
        ctx.comp("vector", lambda v, out=out, a=a, b=b, op=op:
                 v.tensor_tensor(out=out, in0=a, in1=b, op=op), rd=rd, wr=wr)

    def TS(out, a, s1, op, rd=None, wr=None):
        ctx.comp("vector", lambda v, out=out, a=a, s1=s1, op=op:
                 v.tensor_scalar(out=out, in0=a, scalar1=s1, scalar2=None, op0=op),
                 rd=rd, wr=wr)

    def CP(out, a, rd=None, wr=None):
        ctx.comp("vector", lambda v, out=out, a=a: v.tensor_copy(out=out, in_=a),
                 rd=rd, wr=wr)

    def RMAX(out, a, rd=None, wr=None):
        ctx.comp("vector", lambda v, out=out, a=a:
                 v.reduce_max(out=out, in_=a, axis=mybir.AxisListType.X),
                 rd=rd, wr=wr)

    def RSUM(out, a, rd=None, wr=None):
        ctx.comp("vector", lambda v, out=out, a=a:
                 v.reduce_sum(out=out, in_=a, axis=mybir.AxisListType.X),
                 rd=rd, wr=wr)

    def RECIP(out, a, rd=None, wr=None):
        ctx.comp("vector", lambda v, out=out, a=a: v.reciprocal(out=out, in_=a),
                 rd=rd, wr=wr)

    def ACT(out, a, func, bias=None, scale=1.0, rd=None, wr=None):
        if bias is None:
            ctx.comp("scalar", lambda s, out=out, a=a, func=func, scale=scale:
                     s.activation(out=out, in_=a, func=func, scale=scale),
                     rd=rd, wr=wr)
        else:
            ctx.comp("scalar", lambda s, out=out, a=a, func=func, bias=bias,
                     scale=scale:
                     s.activation(out=out, in_=a, func=func, bias=bias, scale=scale),
                     rd=rd, wr=wr)

    def SMUL(out, a, c, rd=None, wr=None):
        ctx.comp("scalar", lambda s, out=out, a=a, c=c: s.mul(out, a, c),
                 rd=rd, wr=wr)

    def SCP(out, a, rd=None, wr=None):
        ctx.comp("scalar", lambda s, out=out, a=a: s.copy(out=out, in_=a),
                 rd=rd, wr=wr)

    def MM(fn, rd=None, wr=None):
        ctx.comp("tensor", fn, rd=rd, wr=wr)

    def LD1(out, in_, rd=None, wr=None, q="sync", sem=None):
        ctx.dma(q, lambda s, out=out, in_=in_:
                [s.dma_start(out=out, in_=in_)], 1, rd=rd, wr=wr, sem=sem)

    def LDL(pairs, rd=None, wr=None, q="sync", sem=None):
        ctx.dma(q, lambda s, pairs=pairs:
                [s.dma_start(out=o, in_=i) for (o, i) in pairs], len(pairs),
                rd=rd, wr=wr, sem=sem)

    def GDL(pairs, rd=None, wr=None, sem=None):
        ctx.dma("gpsimd", lambda g, pairs=pairs:
                [g.dma_start(out=o, in_=i) for (o, i) in pairs], len(pairs),
                rd=rd, wr=wr, sem=sem)

    def GACC(out, in_, rd=None, wr=None):
        ctx.dma("gpsimd", lambda g, out=out, in_=in_:
                [g.dma_start(out=out, in_=in_, accum_op=ALU.add)], 1,
                rd=rd, wr=wr)

    def bcast_row(bank, row_ap, width, np_=128, rd=None, wr=None):
        MM(lambda t, bank=bank, row_ap=row_ap, width=width, np_=np_:
           t.matmul(bank[0:np_, 0:width], onesrow[0:1, 0:np_], row_ap,
                    start=True, stop=True), rd=rd, wr=wr)

    LDL([(onescol[:], dt["onescol"].ap()),
         (onescolb[:], dt["onescolb"].ap()),
         (onesrow[:], dt["onesrow"].ap()),
         (identb[:], dt["ident_b"].ap()),
         (identf[:], dt["ident_f"].ap()),
         (triu[:], dt["triu"].ap()),
         (ubT[:], dt["l1f_ubT"].ap()),
         (vbT[:], dt["l1f_vbT"].ap()),
         (gbcols[:], dt["l1f_gb"].ap()),
         (m32sb[:], dt["M32"].ap()),
         (dt["cw0"].ap(), dt["cw0_init"].ap())])


    def emit_ln(src, Nc, out_bf, out_f32=None, gb_idx=None):
        src_a, out_a = dt[src].ap(), dt[out_bf].ap()
        of_a = dt[out_f32].ap() if out_f32 else None
        for w in range(Nc // 512):
            w0 = w * 512
            LD1(xs.ap().rearrange("p (c n) -> p c n", n=512),
                src_a[:, w0:w0 + 512].rearrange("(c p) n -> p c n", p=128))
            MM(lambda t, _=None: [
                t.matmul(pb[0][0:1, :], onescol[:], xs[:, c * 512:(c + 1) * 512],
                         start=(c == 0), stop=(c == 7)) for c in range(8)][-1])
            for half in range(2):
                cs = [0, 1, 2, 3] if half == 0 else [4, 5, 6, 7]

                def sqw(s, cs=cs):
                    i = None
                    for j, c in enumerate(cs):
                        i = s.activation(out=sc[:, j * 512:(j + 1) * 512],
                                         in_=xs[:, c * 512:(c + 1) * 512],
                                         func=AF.Square)
                    return i
                ctx.comp("scalar", sqw)

                def mmq(t, cs=cs, half=half):
                    i = None
                    for j in range(4):
                        i = t.matmul(pb[1][0:1, :], onescol[:],
                                     sc[:, j * 512:(j + 1) * 512],
                                     start=(half == 0 and j == 0),
                                     stop=(half == 1 and j == 3))
                    return i
                ctx.comp("tensor", mmq)
            SMUL(rows[0:1, 0:512], pb[0][0:1, :], 1.0 / D)
            SMUL(rows[0:1, 512:1024], pb[1][0:1, :], 1.0 / D)
            TT(rows[0:1, 1024:1536], rows[0:1, 0:512], rows[0:1, 0:512], ALU.mult)
            TT(rows[0:1, 512:1024], rows[0:1, 512:1024], rows[0:1, 1024:1536],
               ALU.subtract)
            TS(rows[0:1, 512:1024], rows[0:1, 512:1024], 1e-5, ALU.add)
            ACT(rows[0:1, 1024:1536], rows[0:1, 512:1024], AF.Ln)
            ACT(rows[0:1, 512:1024], rows[0:1, 1024:1536], AF.Exp, scale=-0.5)
            TT(rows[0:1, 1024:1536], rows[0:1, 0:512], rows[0:1, 512:1024], ALU.mult)
            TS(rows[0:1, 1024:1536], rows[0:1, 1024:1536], -1.0, ALU.mult)
            bcast_row(pb[2], rows[0:1, 512:1024], 512)
            bcast_row(pb[3], rows[0:1, 1024:1536], 512)
            for c in range(8):
                TT(sc[:, 0:512], xs[:, c * 512:(c + 1) * 512], pb[2][:, :], ALU.mult)
                TT(sc[:, 512:1024], sc[:, 0:512], pb[3][:, :], ALU.add)
                if gb_idx is not None:
                    g0 = gb_idx * 16
                    TS(sc[:, 512:1024], sc[:, 512:1024],
                       gbcols[:, g0 + c:g0 + c + 1], ALU.mult)
                    TS(sc[:, 512:1024], sc[:, 512:1024],
                       gbcols[:, g0 + 8 + c:g0 + 8 + c + 1], ALU.add)
                CP(bst[:, 0:512], sc[:, 512:1024])
                LD1(out_a[c * 128:(c + 1) * 128, w0:w0 + 512], bst[:, 0:512])
                if of_a is not None:
                    LD1(of_a[c * 128:(c + 1) * 128, w0:w0 + 512], sc[:, 512:1024])

    def load_w(wreg, w_name, K, M, cols=None, wr=None):
        a = dt[w_name].ap()
        if cols is not None:
            a = a[:, cols[0]:cols[1]]
            M = cols[1] - cols[0]
        pairs = []
        for c in range(K // 128):
            pairs.append((wreg.ap()[:, c * M:(c + 1) * M],
                          a[c * 128:(c + 1) * 128, :]))
        LDL(pairs, wr=wr)

    def emit_proj(src, K, w_name, M, Nc, wreg, out_bf=None, wcols=None,
                  bias=None, act=None, res=None, res_out=None, srcoff=0,
                  bias_sb=None):
        load_w(wreg, w_name, K, M, cols=wcols, wr=("PW",))
        src_a = dt[src].ap()
        nkc = K // 128
        regp = contextlib.ExitStack()
        stg = ctx.sbuf("stg_%s" % (out_bf or res_out), [128, nkc * 512], BF16,
                       regp)
        for w in range(Nc // 512):
            w0 = srcoff + w * 512
            LD1(stg.ap().rearrange("p (c n) -> p c n", n=512),
                src_a[:, w0:w0 + 512].rearrange("(c p) n -> p c n", p=128),
                wr=("PSTG",))
            for mc in range(M // 128):
                bank = pb[mc % 2]
                bt = "PB%d" % (mc % 2)

                def mmp(t, mc=mc, bank=bank, nkc=nkc):
                    i = None
                    for kc in range(nkc):
                        i = t.matmul(bank[:, :],
                                     wreg[:, kc * M + mc * 128:kc * M + (mc + 1) * 128],
                                     stg[:, kc * 512:(kc + 1) * 512],
                                     start=(kc == 0), stop=(kc == nkc - 1))
                    return i
                ctx.comp("tensor", mmp, rd=("PSTG", "PW"), wr=(bt,))
                bcol = None
                if bias is not None:
                    bsb = bias_sb if bias_sb is not None else biascols[bias]
                    bcol = bsb[:, mc:mc + 1]
                if act == "elu1":
                    if bcol is not None:
                        TS(sc[:, 0:512], bank[:, :], bcol, ALU.add,
                           rd=(bt, "PSC"), wr=("PSC",))
                    else:
                        CP(sc[:, 0:512], bank[:, :], rd=(bt, "PSC"), wr=("PSC",))
                    TS(sc[:, 512:1024], sc[:, 0:512], 0.0, ALU.min,
                       rd=("PSC",), wr=("PSC",))
                    ACT(sc[:, 1024:1536], sc[:, 512:1024], AF.Exp,
                        rd=("PSC",), wr=("PSC",))
                    ACT(sc[:, 512:1024], sc[:, 0:512], AF.Relu,
                        rd=("PSC",), wr=("PSC",))
                    TT(sc[:, 0:512], sc[:, 512:1024], sc[:, 1024:1536], ALU.add,
                       rd=("PSC",), wr=("PSC",))
                elif act == "relu":
                    ACT(sc[:, 0:512], bank[:, :], AF.Relu, bias=bcol,
                        rd=(bt, "PSC"), wr=("PSC",))
                else:
                    if bcol is not None:
                        TS(sc[:, 0:512], bank[:, :], bcol, ALU.add,
                           rd=(bt, "PSC"), wr=("PSC",))
                    else:
                        CP(sc[:, 0:512], bank[:, :], rd=(bt, "PSC"), wr=("PSC",))
                if res is not None:
                    LD1(sc[:, 512:1024],
                        dt[res].ap()[mc * 128:(mc + 1) * 128, w * 512:w * 512 + 512],
                        wr=("PSC",))
                    TT(sc[:, 0:512], sc[:, 0:512], sc[:, 512:1024], ALU.add,
                       rd=("PSC",), wr=("PSC",))
                    LD1(dt[res_out].ap()[mc * 128:(mc + 1) * 128, w * 512:w * 512 + 512],
                        sc[:, 0:512], rd=("PSC",))
                    if out_bf is not None:
                        CP(bst[:, 0:512], sc[:, 0:512], rd=("PSC",), wr=("PBST",))
                        LD1(dt[out_bf].ap()[mc * 128:(mc + 1) * 128,
                                            w * 512:w * 512 + 512], bst[:, 0:512],
                            rd=("PBST",))
                else:
                    CP(bst[:, 0:512], sc[:, 0:512], rd=("PSC",), wr=("PBST",))
                    LD1(dt[out_bf].ap()[mc * 128:(mc + 1) * 128,
                                        w * 512:w * 512 + 512], bst[:, 0:512],
                        rd=("PBST",))
        regp.close()

    def emit_tok_transpose(src, R, Nc, dst):
        src_a, dst_a = dt[src].ap(), dt[dst].ap()
        for t4 in range(Nc // 512):
            for rc in range(R // 128):
                LD1(tstg[:, 0:512],
                    src_a[rc * 128:(rc + 1) * 128, t4 * 512:(t4 + 1) * 512])

                def tp(t, _=None):
                    i = None
                    for j in range(4):
                        i = t.transpose(pbb[4][:, j * 128:(j + 1) * 128],
                                        tstg[:, j * 128:(j + 1) * 128], identb[:])
                    return i
                ctx.comp("tensor", tp)
                CP(bst[:, 0:512], pbb[4][:, 0:512])
                LD1(dst_a.rearrange("p (t r) -> p t r", r=R)
                    [:, t4 * 4:t4 * 4 + 4, rc * 128:(rc + 1) * 128],
                    bst[:, 0:512].rearrange("p (t r) -> p t r", r=128))

    biascols = {}
    for nm in ("l0_bq", "l0_bk", "l0_bv", "l0_bo", "l1f_bff1", "l1f_bff2",
               "l1f_be1"):
        bw = {"l0_bq": 8, "l0_bk": 8, "l0_bv": 8, "l0_bo": 8,
              "l1f_bff1": 32, "l1f_bff2": 8, "l1f_be1": 256}[nm]
        biascols[nm] = ctx.sbuf("bc_" + nm, [128, bw], F32)
        LD1(biascols[nm][:], dt[nm].ap())

    emit_ln("x", N, "ln0b")

    reg = contextlib.ExitStack()
    wreg2 = ctx.sbuf("wreg2", [128, 8 * 1024], BF16, reg)
    emit_proj("ln0b", D, "l0_WqT", D, N, wreg2, out_bf="q0b", bias="l0_bq",
              act="elu1")
    emit_proj("ln0b", D, "l0_WkT", D, N, wreg2, out_bf="k0b", bias="l0_bk",
              act="elu1")
    emit_proj("ln0b", D, "l0_WvT", D, N, wreg2, out_bf="v0b", bias="l0_bv")
    reg.close()

    emit_tok_transpose("k0b", D, N, "k0tok")
    emit_tok_transpose("v0b", D, N, "v0tok")

    reg = contextlib.ExitStack()
    kvkb = ctx.sbuf("kvkb", [128, 512], BF16, reg)
    kvvb = ctx.sbuf("kvvb", [128, 512], BF16, reg)
    for eh in range(2):
        for dg in range(2):
            for tc in range(32):
                LDL([(kvkb[:], dt["k0tok"].ap()[:, tc * 1024 + dg * 512:
                                                tc * 1024 + dg * 512 + 512]),
                     (kvvb[:], dt["v0tok"].ap()[:, tc * 1024 + eh * 512:
                                                tc * 1024 + eh * 512 + 512])])

                def mkv(t, eh=eh, dg=dg, tc=tc):
                    i = None
                    for dc in range(4):
                        i = t.matmul(pb[dc][:, :], kvkb[:, dc * 128:(dc + 1) * 128],
                                     kvvb[:], start=(tc == 0), stop=(tc == 31))
                    if eh == 0:
                        for dc in range(4):
                            i = t.matmul(pb[4][:, dg * 4 + dc:dg * 4 + dc + 1],
                                         kvkb[:, dc * 128:(dc + 1) * 128],
                                         onescolb[:], start=(tc == 0),
                                         stop=(tc == 31))
                    return i
                ctx.comp("tensor", mkv)
            for dc in range(4):
                dci = dg * 4 + dc
                CP(bst[:, 0:512], pb[dc][:, :])
                LD1(dt["kv0b"].ap()[dci * 128:(dci + 1) * 128,
                                    eh * 512:eh * 512 + 512], bst[:, 0:512])
        if eh == 0:
            CP(sc[:, 0:8], pb[4][:, 0:8])
            TS(sc[:, 0:8], sc[:, 0:8], 1e-6, ALU.add)
            LD1(dt["ksum0"].ap(), sc[:, 0:8])
    reg.close()

    reg = contextlib.ExitStack()
    kvsb = ctx.sbuf("kvsb", [128, 8 * 1024], BF16, reg)
    ksp = ctx.sbuf("ksp", [128, 8], F32, reg)
    qstg = ctx.sbuf("qstg", [128, 8 * 512], BF16, reg)
    LDL([(kvsb.ap()[:, c * 1024:(c + 1) * 1024],
          dt["kv0b"].ap()[c * 128:(c + 1) * 128, :]) for c in range(8)])
    LD1(ksp[:], dt["ksum0"].ap())
    for w in range(8):
        w0 = w * 512
        LD1(qstg.ap().rearrange("p (c n) -> p c n", n=512),
            dt["q0b"].ap()[:, w0:w0 + 512].rearrange("(c p) n -> p c n", p=128))
        for c in range(8):
            TS(sc[:, 0:512], qstg[:, c * 512:(c + 1) * 512], ksp[:, c:c + 1],
               ALU.mult)
            MM(lambda t, c=c: t.matmul(pb[0][0:1, :], onescol[:], sc[:, 0:512],
                                       start=(c == 0), stop=(c == 7)))
        RECIP(rows[0:1, 0:512], pb[0][0:1, :])
        bcast_row(pb[1], rows[0:1, 0:512], 512)
        for ec in range(8):
            def mmc(t, ec=ec):
                i = None
                for dc in range(8):
                    i = t.matmul(pb[2][:, :],
                                 kvsb[:, dc * 1024 + ec * 128:dc * 1024 + (ec + 1) * 128],
                                 qstg[:, dc * 512:(dc + 1) * 512],
                                 start=(dc == 0), stop=(dc == 7))
                return i
            ctx.comp("tensor", mmc)
            CP(sc[:, 0:512], pb[2][:, :])
            TT(sc[:, 512:1024], sc[:, 0:512], pb[1][:, :], ALU.mult)
            CP(bst[:, 0:512], sc[:, 512:1024])
            LD1(dt["c0b"].ap()[ec * 128:(ec + 1) * 128, w0:w0 + 512], bst[:, 0:512])
    reg.close()

    reg = contextlib.ExitStack()
    wreg2 = ctx.sbuf("wreg2b", [128, 8 * 1024], BF16, reg)
    emit_proj("c0b", D, "l0_WoT", D, N, wreg2, wcols=None, bias="l0_bo",
              res="x", res_out="res1")
    reg.close()

    emit_ln("res1", N, "ln1b", out_f32="ln1f")

    reg = contextlib.ExitStack()
    gwsb = ctx.sbuf("gwsb", [128, 64], F32, reg)
    xg = ctx.sbuf("xg", [128, 8 * 128], F32, reg)
    g8 = ctx.sbuf("g8", [128, 8], F32, reg)
    gcol = ctx.sbuf("gcol", [128, 4], F32, reg)
    cwt = ctx.sbuf("cwt", [8, 128], F32, reg)
    LD1(gwsb.ap().rearrange("p (c m) -> p c m", m=8),
        dt["l0_gwT"].ap().rearrange("(c p) m -> p c m", p=128))
    for tt in range(32):
        LD1(xg.ap().rearrange("p (c n) -> p c n", n=128),
            dt["ln1f"].ap()[:, tt * 128:(tt + 1) * 128]
            .rearrange("(c p) n -> p c n", p=128))

        def mlg(t, _=None):
            i = None
            for c in range(8):
                i = t.matmul(pb[0][:, 0:8], xg[:, c * 128:(c + 1) * 128],
                             gwsb[:, c * 8:(c + 1) * 8],
                             start=(c == 0), stop=(c == 7))
            return i
        ctx.comp("tensor", mlg)
        RMAX(gcol[:, 0:1], pb[0][:, 0:8])
        TS(gcol[:, 1:2], gcol[:, 0:1], -1.0, ALU.mult)
        ACT(g8[:], pb[0][:, 0:8], AF.Exp, bias=gcol[:, 1:2])
        RSUM(gcol[:, 0:1], g8[:])
        RECIP(gcol[:, 1:2], gcol[:, 0:1])
        TS(g8[:], g8[:], gcol[:, 1:2], ALU.mult)
        RMAX(gcol[:, 0:1], g8[:])
        TS(sc[:, 0:8], g8[:], gcol[:, 0:1], ALU.is_ge)
        TS(sc[:, 0:8], sc[:, 0:8], -1e30, ALU.mult)
        TT(sc[:, 8:16], g8[:], sc[:, 0:8], ALU.add)
        RMAX(gcol[:, 1:2], sc[:, 8:16])
        TT(gcol[:, 2:3], gcol[:, 0:1], gcol[:, 1:2], ALU.add)
        TS(gcol[:, 2:3], gcol[:, 2:3], 1e-20, ALU.add)
        RECIP(gcol[:, 3:4], gcol[:, 2:3])
        TS(gcol[:, 3:4], gcol[:, 3:4], 2.5, ALU.mult)
        TS(sc[:, 0:8], g8[:], gcol[:, 1:2], ALU.is_ge)
        TT(sc[:, 8:16], sc[:, 0:8], g8[:], ALU.mult)
        TS(sc[:, 8:16], sc[:, 8:16], gcol[:, 3:4], ALU.mult)
        MM(lambda t, _=None: t.transpose(pb[1][0:8, 0:128], sc[:, 8:16], identf[:]))
        CP(cwt[:], pb[1][0:8, 0:128])
        LD1(dt["cw0"].ap()[0:8, tt * 128:(tt + 1) * 128], cwt[:])
    reg.close()

    LDL([(dt["res2"].ap()[c * 128:(c + 1) * 128, :],
          dt["res1"].ap()[c * 128:(c + 1) * 128, :]) for c in range(8)])

    def emit_moe(src, Nc, NE, F, wg_name, wu_name, wd_name, cw_name, out_name,
                 tcw, tag, relu_bias=None, be2_name=None):
        nfc = F // 128
        regm = contextlib.ExitStack()
        w1 = ctx.sbuf(tag + "moe_w1", [128, 8 * F], BF16, regm)
        w2 = ctx.sbuf(tag + "moe_w2", [128, 8 * F] if wu_name else [1, 1],
                      BF16 if wu_name else F32, regm)
        wdn = ctx.sbuf(tag + "moe_wd", [128, nfc * 1024], BF16, regm)
        hidb = ctx.sbuf(tag + "moe_hid", [128, nfc * tcw], BF16, regm)
        cwb = ctx.sbuf(tag + "moe_cwb", [128, 2 * tcw], F32, regm)
        cwr = ctx.sbuf(tag + "moe_cwr", [1, 2 * tcw], F32, regm)
        stg = ctx.sbuf(tag + "moe_stg", [128, 2 * 8 * tcw], BF16, regm)
        eob = ctx.sbuf(tag + "moe_eob", [128, 4 * tcw], F32, regm)
        src_a = dt[src].ap()
        wgf = dt[wg_name].ap().rearrange("e d f -> (e d) f")
        wuf = dt[wu_name].ap().rearrange("e d f -> (e d) f") if wu_name else None
        wdf = dt[wd_name].ap().rearrange("e f d -> (e f) d")
        ntc = Nc // tcw
        W = tag + "w"
        X1, X2 = tag + "x1", tag + "x2"
        HD = tag + "hd"
        for e in range(NE):
            wp = [(w1.ap()[:, c * F:(c + 1) * F],
                   wgf[e * D + c * 128:e * D + (c + 1) * 128, :])
                  for c in range(8)]
            if wu_name:
                wp += [(w2.ap()[:, c * F:(c + 1) * F],
                        wuf[e * D + c * 128:e * D + (c + 1) * 128, :])
                       for c in range(8)]
            wp += [(wdn.ap()[:, c * 1024:(c + 1) * 1024],
                    wdf[e * F + c * 128:e * F + (c + 1) * 128, :])
                   for c in range(nfc)]
            GDL(wp, rd=(), wr=(W,), sem="GW0")
            for tc in range(ntc):
                t0 = tc * tcw
                s = tc % 2
                SG = tag + "sg%d" % s
                CB = tag + "cb%d" % s
                stg_s = stg[:, s * 8 * tcw:(s + 1) * 8 * tcw]
                cwr_s = cwr[0:1, s * tcw:(s + 1) * tcw]
                cwb_s = cwb[:, s * tcw:(s + 1) * tcw]
                ctx.dma("sync", lambda q, t0=t0, stg_s=stg_s, cwr_s=cwr_s, e=e,
                        tcw=tcw:
                        [q.dma_start(out=stg_s.rearrange("p (c n) -> p c n", n=tcw),
                                     in_=src_a[:, t0:t0 + tcw]
                                     .rearrange("(c p) n -> p c n", p=128)),
                         q.dma_start(out=cwr_s,
                                     in_=dt[cw_name].ap()[e:e + 1, t0:t0 + tcw])],
                        2, rd=(), wr=(SG,), sem="LS%d" % s)
                bcast_row(pb[6], cwr_s, tcw, rd=(SG,), wr=("p6",))
                CP(cwb_s, pb[6][0:128, 0:tcw], rd=("p6",), wr=(CB,))
                for fc in range(nfc):
                    def mg(t, fc=fc, tcw=tcw, stg_s=stg_s):
                        i = None
                        for dc in range(8):
                            i = t.matmul(pb[4][:, 0:tcw],
                                         w1[:, dc * F + fc * 128:dc * F + (fc + 1) * 128],
                                         stg_s[:, dc * tcw:(dc + 1) * tcw],
                                         start=(dc == 0), stop=(dc == 7))
                        return i
                    ctx.comp("tensor", mg, rd=(W, SG), wr=("p4",))
                    if wu_name:
                        def mu(t, fc=fc, tcw=tcw, stg_s=stg_s):
                            i = None
                            for dc in range(8):
                                i = t.matmul(pb[5][:, 0:tcw],
                                             w2[:, dc * F + fc * 128:dc * F + (fc + 1) * 128],
                                             stg_s[:, dc * tcw:(dc + 1) * tcw],
                                             start=(dc == 0), stop=(dc == 7))
                            return i
                        ctx.comp("tensor", mu, rd=(W, SG), wr=("p5",))
                        ACT(sc[:, 0:tcw], pb[4][:, 0:tcw], AF.Sigmoid,
                            rd=("p4",), wr=(X1,))
                        TT(sc[:, 512:512 + tcw], pb[4][:, 0:tcw], sc[:, 0:tcw],
                           ALU.mult, rd=("p4", X1), wr=(X2,))
                        TT(sc[:, 0:tcw], sc[:, 512:512 + tcw], pb[5][:, 0:tcw],
                           ALU.mult, rd=(X2, "p5"), wr=(X1,))
                    else:
                        ACT(sc[:, 0:tcw], pb[4][:, 0:tcw], AF.Relu,
                            bias=biascols[relu_bias][:, e * 32 + fc:e * 32 + fc + 1],
                            rd=("p4",), wr=(X1,))
                    TT(sc[:, 512:512 + tcw], sc[:, 0:tcw], cwb_s, ALU.mult,
                       rd=(X1, CB), wr=(X2,))
                    CP(hidb[:, fc * tcw:(fc + 1) * tcw], sc[:, 512:512 + tcw],
                       rd=(X2,), wr=(HD,))
                for dg in range(2):
                    def md(t, dg=dg, nfc=nfc, tcw=tcw):
                        i = None
                        for dc in range(4):
                            dci = dg * 4 + dc
                            for fc in range(nfc):
                                i = t.matmul(pb[dc][:, 0:tcw],
                                             wdn[:, fc * 1024 + dci * 128:fc * 1024 + (dci + 1) * 128],
                                             hidb[:, fc * tcw:(fc + 1) * tcw],
                                             start=(fc == 0), stop=(fc == nfc - 1))
                        return i
                    ctx.comp("tensor", md, rd=(HD, W),
                             wr=("p0", "p1", "p2", "p3"))
                    for dc in range(4):
                        dci = dg * 4 + dc
                        EO = tag + "eo%d" % dc
                        CP(eob[:, dc * tcw:(dc + 1) * tcw], pb[dc][:, 0:tcw],
                           rd=("p%d" % dc,), wr=(EO,))
                        GACC(dt[out_name].ap()[dci * 128:(dci + 1) * 128,
                                               t0:t0 + tcw],
                             eob[:, dc * tcw:(dc + 1) * tcw],
                             rd=(EO,), wr=(tag + "a%d_%d" % (tc, dci), EO))
        if be2_name is not None:
            be2sb = xs.ap()[0:8, 0:1024]
            cw8 = xs.ap()[0:8, 1024:1536]
            LD1(be2sb, dt[be2_name].ap())
            for tc in range(Nc // 512):
                t0 = tc * 512
                LD1(cw8, dt[cw_name].ap()[0:8, t0:t0 + 512])
                for dci in range(8):
                    MM(lambda t, dci=dci: t.matmul(
                        pb[dci % 2][:, :], be2sb[:, dci * 128:(dci + 1) * 128],
                        cw8, start=True, stop=True))
                    CP(sc[:, 0:512], pb[dci % 2][:, :])
                    GACC(dt[out_name].ap()[dci * 128:(dci + 1) * 128,
                                           t0:t0 + 512], sc[:, 0:512])
        regm.close()

    emit_moe("ln1b", N, 10, 1408, "l0_wg", "l0_wu", "l0_wd", "cw0", "res2",
             512, "m0")

    if PHASE_LIMIT <= 1:
        ctx.fence("sync")
        ctx.emit(block)
        return nc

    emit_ln("res2", N, "ln2b")

    reg = contextlib.ExitStack()
    wreg1 = ctx.sbuf("wreg1a", [128, 8 * 256], BF16, reg)
    astg = ctx.sbuf("astg", [128, 8 * 512], BF16, reg)
    for (wn, on, do_pu) in (("l1a_WqT", "q1n", True), ("l1a_WkT", "k1n", True),
                            ("l1a_WvT", "v1T", False)):
        load_w(wreg1, wn, D, 256)
        for w in range(8):
            w0 = w * 512
            LD1(astg.ap().rearrange("p (c n) -> p c n", n=512),
                dt["ln2b"].ap()[:, w0:w0 + 512].rearrange("(c p) n -> p c n", p=128))
            for ch in range(2):
                def mmq(t, ch=ch):
                    i = None
                    for kc in range(8):
                        i = t.matmul(pb[0][:, :],
                                     wreg1[:, kc * 256 + ch * 128:kc * 256 + (ch + 1) * 128],
                                     astg[:, kc * 512:(kc + 1) * 512],
                                     start=(kc == 0), stop=(kc == 7))
                    return i
                ctx.comp("tensor", mmq)
                if do_pu:
                    TS(sc[:, 0:512], pb[0][:, :], 0.0, ALU.min)
                    ACT(sc[:, 512:1024], sc[:, 0:512], AF.Exp)
                    ACT(sc[:, 0:512], pb[0][:, :], AF.Relu)
                    TT(sc[:, 1024:1536], sc[:, 0:512], sc[:, 512:1024], ALU.add)
                    ACT(sc[:, 0:512], sc[:, 1024:1536], AF.Square)
                    for hh in range(2):
                        h0_ = hh * 64
                        MM(lambda t, h0_=h0_: t.matmul(
                            pb[1][0:1, 0:512], onescol[h0_:h0_ + 64, 0:1],
                            sc[h0_:h0_ + 64, 0:512], start=True, stop=True))
                        ACT(rows[0:1, 0:512], pb[1][0:1, 0:512], AF.Sqrt)
                        TS(rows[0:1, 0:512], rows[0:1, 0:512], 1e-6, ALU.add)
                        RECIP(rows[0:1, 512:1024], rows[0:1, 0:512])
                        bcast_row(pb[2], rows[0:1, 512:1024], 512, np_=64)
                        TT(sc[h0_:h0_ + 64, 1024:1536], sc[h0_:h0_ + 64, 1024:1536],
                           pb[2][0:64, 0:512], ALU.mult)
                    LD1(dt[on].ap()[ch * 128:(ch + 1) * 128, w0:w0 + 512],
                        sc[:, 1024:1536])
                else:
                    CP(sc[:, 0:512], pb[0][:, :])
                    LD1(dt[on].ap()[ch * 128:(ch + 1) * 128, w0:w0 + 512],
                        sc[:, 0:512])
    reg.close()

    reg = contextlib.ExitStack()
    kch = ctx.sbuf("a1_k", [64, 128], F32, reg)
    vch = ctx.sbuf("a1_v", [64, 128], F32, reg)
    qch = ctx.sbuf("a1_q", [64, 128], F32, reg)
    ktok = ctx.sbuf("a1_kt", [128, 64], F32, reg)
    vtok = ctx.sbuf("a1_vt", [128, 64], F32, reg)
    stt = ctx.sbuf("a1_st", [64, 64], F32, reg)
    kstc = ctx.sbuf("a1_kst", [64, 1], F32, reg)
    stm = ctx.sbuf("a1_stm", [128, 128], F32, reg)
    ybt = ctx.sbuf("a1_y", [64, 128], BF16, reg)
    dkvf = ctx.sbuf("a1_dkvf", [32, 4096], F32, reg)
    dkm = ctx.sbuf("a1_dkm", [32, 64], F32, reg)
    for hh in range(4):
        for c in range(32):
            r = hh * 32 + c
            LDL([(kch[:], dt["k1n"].ap()[hh * 64:hh * 64 + 64, c * 128:(c + 1) * 128]),
                 (vch[:], dt["v1T"].ap()[hh * 64:hh * 64 + 64, c * 128:(c + 1) * 128])])
            MM(lambda t, _=None: t.transpose(pb[0][:, 0:64], kch[:],
                                             identf[0:64, 0:64]))
            CP(ktok[:], pb[0][:, 0:64])
            MM(lambda t, _=None: t.transpose(pb[1][:, 0:64], vch[:],
                                             identf[0:64, 0:64]))
            CP(vtok[:], pb[1][:, 0:64])
            MM(lambda t, _=None: t.matmul(pb[2][0:64, 0:64], ktok[:], vtok[:],
                                          start=True, stop=True))
            CP(stt[:], pb[2][0:64, 0:64])
            LD1(dt["dkv1"].ap()[0:64, r * 64:(r + 1) * 64], stt[:])
            MM(lambda t, _=None: t.matmul(pb[3][0:1, 0:64], onescol[:], ktok[:],
                                          start=True, stop=True))
            CP(rows[0:1, 0:64], pb[3][0:1, 0:64])
            LD1(dt["dk1T"].ap()[r:r + 1, 0:64], rows[0:1, 0:64])
    for hh in range(4):
        LD1(dkvf.ap().rearrange("c (p j) -> c p j", j=64),
            dt["dkv1"].ap()[:, hh * 2048:(hh + 1) * 2048]
            .rearrange("p (c j) -> c p j", j=64))
        for w in range(8):
            MM(lambda t, w=w: t.matmul(pb[w % 2][0:32, :], m32sb[:],
                                       dkvf[:, w * 512:(w + 1) * 512],
                                       start=True, stop=True))
            CP(sc[0:32, 0:512], pb[w % 2][0:32, :])
            LD1(dt["st1"].ap()[w * 8:(w + 1) * 8, hh * 2048:(hh + 1) * 2048]
                .rearrange("p (c j) -> c p j", j=64),
                sc[0:32, 0:512].rearrange("c (p j) -> c p j", j=64))
        LD1(dkm[:], dt["dk1T"].ap()[hh * 32:(hh + 1) * 32, 0:64])
        MM(lambda t, _=None: t.matmul(pb[2][0:32, 0:64], m32sb[:], dkm[:],
                                      start=True, stop=True))
        CP(sc[0:32, 0:64], pb[2][0:32, 0:64])
        MM(lambda t, _=None: t.transpose(pb[3][0:64, 0:32], sc[0:32, 0:64],
                                         identf[0:32, 0:32]))
        CP(sc[0:64, 512:544], pb[3][0:64, 0:32])
        LD1(dt["kstT"].ap()[0:64, hh * 32:(hh + 1) * 32], sc[0:64, 512:544])
    for hh in range(4):
        for c in range(32):
            r = hh * 32 + c
            LDL([(qch[:], dt["q1n"].ap()[hh * 64:hh * 64 + 64, c * 128:(c + 1) * 128]),
                 (kch[:], dt["k1n"].ap()[hh * 64:hh * 64 + 64, c * 128:(c + 1) * 128]),
                 (vch[:], dt["v1T"].ap()[hh * 64:hh * 64 + 64, c * 128:(c + 1) * 128]),
                 (stt[:], dt["st1"].ap()[0:64, r * 64:(r + 1) * 64])])
            ctx.dma("sync", lambda s, r=r:
                    [s.dma_start(out=kstc[:], in_=dt["kstT"].ap()[0:64, r:r + 1],
                                 allow_slow_non_contiguous=True)], 1)
            MM(lambda t, _=None: t.transpose(pb[0][:, 0:64], vch[:],
                                             identf[0:64, 0:64]))
            CP(vtok[:], pb[0][:, 0:64])
            MM(lambda t, _=None: t.matmul(pb[1][:, 0:128], kch[:], qch[:],
                                          start=True, stop=True))
            TT(stm[:], pb[1][:, 0:128], triu[:], ALU.mult)

            def mden(t, _=None):
                t.matmul(pb[2][0:1, 0:128], onescol[:], stm[:],
                         start=True, stop=False)
                return t.matmul(pb[2][0:1, 0:128], kstc[:], qch[:],
                                start=False, stop=True)
            ctx.comp("tensor", mden)

            def mnum(t, _=None):
                t.matmul(pb[3][0:64, 0:128], vtok[:], stm[:],
                         start=True, stop=False)
                return t.matmul(pb[3][0:64, 0:128], stt[:], qch[:],
                                start=False, stop=True)
            ctx.comp("tensor", mnum)
            TS(rows[0:1, 0:128], pb[2][0:1, 0:128], 1e-6, ALU.add)
            RECIP(rows[0:1, 128:256], rows[0:1, 0:128])
            MM(lambda t, _=None: t.matmul(pb[4][0:64, 0:128], onesrow[0:1, 0:64],
                                          rows[0:1, 128:256], start=True, stop=True))
            CP(sc[0:64, 0:128], pb[3][0:64, 0:128])
            TT(ybt[:], sc[0:64, 0:128], pb[4][0:64, 0:128], ALU.mult)
            LD1(dt["y1b"].ap()[hh * 64:hh * 64 + 64, c * 128:(c + 1) * 128], ybt[:])
    reg.close()

    reg = contextlib.ExitStack()
    wreg1o = ctx.sbuf("wreg1o", [128, 2 * 1024], BF16, reg)
    emit_proj("y1b", 256, "l1a_WoT", D, N, wreg1o, res="res2", res_out="res3")
    reg.close()

    if PHASE_LIMIT <= 2:
        ctx.fence("sync")
        ctx.emit(block)
        return nc

    emit_ln("res3", N, "ln3b", out_f32="ln3f")

    reg = contextlib.ExitStack()
    wreg2c = ctx.sbuf("wreg2c", [128, 8 * 1024], BF16, reg)
    emit_proj("ln3b", D, "l1f_Wqkv", 1024, HN, wreg2c, out_bf="qT1f",
              wcols=(0, 1024))
    emit_proj("ln3b", D, "l1f_Wqkv", 1024, N, wreg2c, out_bf="kT1f",
              wcols=(1024, 2048))
    emit_proj("ln3b", D, "l1f_Wqkv", 1024, N, wreg2c, out_bf="vT1f",
              wcols=(2048, 3072))
    emit_proj("peT", D, "l1f_Wr", 1024, PE_W, wreg2c, out_bf="rTs")
    reg.close()

    emit_tok_transpose("vT1f", D, N, "vtok1f")

    if PHASE_LIMIT <= 3:
        ctx.fence("sync")
        ctx.emit(block)
        return nc


    reg = contextlib.ExitStack()
    vtkf = ctx.sbuf("f_vtok", [128, 32 * 1024], BF16, reg)
    ksb2 = ctx.sbuf("f_ksb", [64, 2 * N], BF16, reg)
    rsb2 = ctx.sbuf("f_rsb", [64, 2 * PE_W], BF16, reg)
    qb2 = ctx.sbuf("f_qb2", [64, 6 * 128], BF16, reg)
    scos = ctx.sbuf("f_scos", [128, 4224], F32, reg)
    attb = ctx.sbuf("f_attb", [128, 4224], BF16, reg)
    atk2 = ctx.sbuf("f_atk", [128, 2 * 512], BF16, reg)
    bdg2 = ctx.sbuf("f_bdgt", [128, 2 * 512], BF16, reg)
    bde2 = ctx.sbuf("f_bde", [128, 2 * 512], BF16, reg)
    ctl2 = ctx.sbuf("f_ctl", [128, 2 * 64], BF16, reg)
    acol = ctx.sbuf("f_acol", [128, 4], F32, reg)
    dmask = ctx.sbuf("c_dmask", [128, 2048], F32, reg)
    kmbuf = ctx.sbuf("c_kmbuf", [128, 2048], F32, reg)
    kmrow = ctx.sbuf("c_kmrow", [1, HN], F32, reg)
    LDL([(dmask[:], dt["diagmask"].ap()),
         (kmrow[:], dt["kmaskrow"].ap())])
    GDL([(vtkf.ap()[:, c * 4096:(c + 1) * 4096],
          dt["vtok1f"].ap()[:, c * 4096:(c + 1) * 4096]) for c in range(8)],
        rd=(), wr=("VT",), sem="GW0")
    for j in range(4):
        bcast_row(pb[6], kmrow[0:1, j * 512:(j + 1) * 512], 512)
        CP(kmbuf[:, j * 512:(j + 1) * 512], pb[6][:, 0:512])
    for hh in range(16):
        hs = hh % 2
        KS = "KS%d" % hs
        ksb = ksb2[:, hs * N:(hs + 1) * N]
        rsb = rsb2[:, hs * PE_W:(hs + 1) * PE_W]
        GDL([(ksb, dt["kT1f"].ap()[hh * 64:hh * 64 + 64, :]),
             (rsb, dt["rTs"].ap()[hh * 64:hh * 64 + 64, :])],
            rd=(), wr=(KS,), sem="GW1")
        for qb in range(16):
            q0 = 128 * qb
            KA = q0 + 128
            nwA = (KA + 511) // 512
            KT = KA + 2048
            s = qb % 2
            QT, QU, QV = "QT%d" % s, "QU%d" % s, "QV%d" % s
            qtl = qb2[:, s * 128:(s + 1) * 128]
            qub = qb2[:, (2 + s) * 128:(3 + s) * 128]
            qvb = qb2[:, (4 + s) * 128:(5 + s) * 128]
            LD1(qtl, dt["qT1f"].ap()[hh * 64:hh * 64 + 64, q0:q0 + 128],
                rd=(), wr=(QT,), sem="LS%d" % s)
            TS(qub, qtl, ubT[:, hh:hh + 1], ALU.add, rd=(QT,), wr=(QU,))
            TS(qvb, qtl, vbT[:, hh:hh + 1], ALU.add, rd=(QT,), wr=(QV,))
            wi = 0
            for (bd_name, u0, Wd) in (("BDa", 4096 - q0, (nwA - 1) * 512 + 640),
                                      ("BDb", 2048 - q0, 2176)):
                BT = "B_" + bd_name
                for w0 in range(0, Wd, 512):
                    wa = min(512, Wd - w0)
                    bk = pb[0] if wi % 2 == 0 else pb[5]
                    PK = "pBD%d" % (wi % 2)
                    BE = "be%d" % (wi % 2)
                    bde = bde2[:, (wi % 2) * 512:(wi % 2) * 512 + wa]
                    MM(lambda t, u0=u0, w0=w0, wa=wa, bk=bk, qvb=qvb, rsb=rsb:
                       t.matmul(bk[:, 0:wa], qvb,
                                rsb[0:64, u0 + w0:u0 + w0 + wa],
                                start=True, stop=True),
                       rd=(QV, KS), wr=(PK,))
                    SCP(bde, bk[:, 0:wa], rd=(PK,), wr=(BE,))
                    ctx.dma("gpsimd", lambda g, bd_name=bd_name, w0=w0, wa=wa,
                            bde=bde:
                            [g.dma_start(out=dt[bd_name].ap()[:, w0:w0 + wa],
                                         in_=bde)],
                            1, rd=(BE,), wr=(BT,), sem="ST0")
                    wi += 1
            for kw in range(nwA + 4):
                isA = kw < nwA
                if isA:
                    kw0 = kw * 512
                    w = min(512, KA - kw0)
                    ksl = ksb[0:64, kw0:kw0 + w]
                    bd_t, BT = dt["BDa"], "B_BDa"
                else:
                    j = kw - nwA
                    kw0 = j * 512
                    w = 512
                    ksl = ksb[0:64, 2048 + kw0:2048 + kw0 + w]
                    bd_t, BT = dt["BDb"], "B_BDb"
                ws = kw % 2
                bk = pb[1] if ws == 0 else pb[2]
                PS = "pS%d" % ws
                BG = "bg%d" % ws
                bdg = bdg2[:, ws * 512:ws * 512 + w]
                MM(lambda t, ksl=ksl, w=w, bk=bk, qub=qub:
                   t.matmul(bk[:, 0:w], qub, ksl, start=True, stop=True),
                   rd=(QU, KS), wr=(PS,))
                LD1(bdg, AP(bd_t, kw0 + 127, [[BDW - 1, 128], [1, w]]),
                    rd=(BT,), wr=(BG,), sem="LS%d" % ws)
                col0 = kw0 if isA else KA + kw0
                TT(scos[:, col0:col0 + w], bk[:, 0:w], bdg, ALU.add,
                   rd=(PS, BG), wr=("SCS",))
                SMUL(scos[:, col0:col0 + w], scos[:, col0:col0 + w], 0.125,
                     rd=("SCS",), wr=("SCS",))
                if isA and kw == nwA - 1:
                    a = w // 128 - 1
                    TT(scos[:, col0:col0 + w], scos[:, col0:col0 + w],
                       dmask[:, a * 512:a * 512 + w], ALU.add,
                       rd=("SCS",), wr=("SCS",))
                elif not isA:
                    TT(scos[:, col0:col0 + w], scos[:, col0:col0 + w],
                       kmbuf[:, kw0:kw0 + w], ALU.add,
                       rd=("SCS",), wr=("SCS",))
            RMAX(acol[:, 0:1], scos[:, 0:KT], rd=("SCS",), wr=("MX",))
            TS(acol[:, 1:2], acol[:, 0:1], -1.0, ALU.mult, rd=("MX",), wr=("MX",))
            ACT(xs[:, 0:KT], scos[:, 0:KT], AF.Exp, bias=acol[:, 1:2],
                rd=("SCS", "MX"), wr=("XP",))
            RSUM(acol[:, 2:3], xs[:, 0:KT], rd=("XP",), wr=("DN",))
            RECIP(acol[:, 3:4], acol[:, 2:3], rd=("DN",), wr=("DN",))
            TS(attb[:, 0:KT], xs[:, 0:KT], acol[:, 3:4], ALU.mult,
               rd=("XP", "DN"), wr=("AT",))
            nkc = KT // 128
            kc = 0
            while kc < nkc:
                gn = min(4, nkc - kc)
                gs = (kc // 4) % 2
                AK = "ak%d" % gs
                atk = atk2[:, gs * 512:gs * 512 + gn * 128]

                def tpg(t, kc=kc, gn=gn):
                    i = None
                    for j in range(gn):
                        i = t.transpose(pbb[3][:, j * 128:(j + 1) * 128],
                                        attb[:, (kc + j) * 128:(kc + j + 1) * 128],
                                        identb[:])
                    return i
                ctx.comp("tensor", tpg, rd=("AT",), wr=("pT",))
                SCP(atk, pbb[3][:, 0:gn * 128], rd=("pT",), wr=(AK,))

                def mmc(t, kc=kc, gn=gn, qb=qb, hh=hh, nkc=nkc, atk=atk):
                    i = None
                    for j in range(gn):
                        kcj = kc + j
                        phys = kcj if kcj < qb + 1 else 16 + (kcj - (qb + 1))
                        i = t.matmul(pb[4][:, 0:64],
                                     atk[:, j * 128:(j + 1) * 128],
                                     vtkf[:, phys * 1024 + hh * 64:phys * 1024 + hh * 64 + 64],
                                     start=(kcj == 0), stop=(kcj == nkc - 1))
                    return i
                ctx.comp("tensor", mmc, rd=(AK, "VT"), wr=("p4",))
                kc += gn
            ctl = ctl2[:, s * 64:(s + 1) * 64]
            SCP(ctl, pb[4][:, 0:64], rd=("p4",), wr=("CT%d" % s,))
            LD1(dt["ctxtok"].ap()[:, qb * 1024 + hh * 64:qb * 1024 + hh * 64 + 64],
                ctl, rd=("CT%d" % s,), wr=("d_ctxtok",), sem="LS%d" % s)
    reg.close()

    for qb in range(16):
        for f4 in range(2):
            LD1(tstg[:, 0:512],
                dt["ctxtok"].ap()[:, qb * 1024 + f4 * 512:qb * 1024 + (f4 + 1) * 512])

            def tpc(t, _=None):
                i = None
                for j in range(4):
                    i = t.transpose(pbb[5][:, j * 128:(j + 1) * 128],
                                    tstg[:, j * 128:(j + 1) * 128], identb[:])
                return i
            ctx.comp("tensor", tpc)
            CP(bst[:, 0:512], pbb[5][:, 0:512])
            LD1(dt["ctxT"].ap()[f4 * 512:(f4 + 1) * 512, qb * 128:(qb + 1) * 128]
                .rearrange("(f p) n -> p f n", p=128),
                bst[:, 0:512].rearrange("p (f n) -> p f n", n=128))

    reg = contextlib.ExitStack()
    wregf = ctx.sbuf("wregf", [128, 8 * 4096], BF16, reg)
    emit_proj("ctxT", D, "l1f_Wo", D, HN, wregf, res="ln3f", res_out="h0")
    reg.close()

    emit_ln("h0", HN, "h1b", out_f32="h1f", gb_idx=0)

    reg = contextlib.ExitStack()
    wregf1 = ctx.sbuf("wregf1", [128, 8 * 4096], BF16, reg)
    emit_proj("h1b", D, "l1f_Wff1", 4096, HN, wregf1, out_bf="ffh",
              bias="l1f_bff1", act="relu")
    reg.close()
    reg = contextlib.ExitStack()
    wregf2 = ctx.sbuf("wregf2", [128, 32 * 1024], BF16, reg)
    emit_proj("ffh", 4096, "l1f_Wff2", D, HN, wregf2, bias="l1f_bff2",
              res="h1f", res_out="h2pre")
    reg.close()

    emit_ln("h2pre", HN, "h2b", out_f32="h2f", gb_idx=1)

    reg = contextlib.ExitStack()
    gwsb1 = ctx.sbuf("gwsb1", [128, 64], F32, reg)
    xg1 = ctx.sbuf("xg1", [128, 8 * 128], F32, reg)
    g81 = ctx.sbuf("g81", [128, 8], F32, reg)
    gcol1 = ctx.sbuf("gcol1", [128, 4], F32, reg)
    cwt1 = ctx.sbuf("cwt1", [8, 128], F32, reg)
    LD1(gwsb1.ap().rearrange("p (c m) -> p c m", m=8),
        dt["l1f_Wg"].ap().rearrange("(c p) m -> p c m", p=128))
    for tt in range(16):
        LD1(xg1.ap().rearrange("p (c n) -> p c n", n=128),
            dt["h2f"].ap()[:, tt * 128:(tt + 1) * 128]
            .rearrange("(c p) n -> p c n", p=128))

        def mlg1(t, _=None):
            i = None
            for c in range(8):
                i = t.matmul(pb[0][:, 0:8], xg1[:, c * 128:(c + 1) * 128],
                             gwsb1[:, c * 8:(c + 1) * 8],
                             start=(c == 0), stop=(c == 7))
            return i
        ctx.comp("tensor", mlg1)
        RMAX(gcol1[:, 0:1], pb[0][:, 0:8])
        TS(gcol1[:, 1:2], gcol1[:, 0:1], -1.0, ALU.mult)
        ACT(g81[:], pb[0][:, 0:8], AF.Exp, bias=gcol1[:, 1:2])
        RSUM(gcol1[:, 0:1], g81[:])
        RECIP(gcol1[:, 1:2], gcol1[:, 0:1])
        TS(g81[:], g81[:], gcol1[:, 1:2], ALU.mult)
        RMAX(gcol1[:, 0:1], g81[:])
        TS(sc[:, 0:8], g81[:], gcol1[:, 0:1], ALU.is_ge)
        TT(sc[:, 8:16], sc[:, 0:8], g81[:], ALU.mult)
        MM(lambda t, _=None: t.transpose(pb[1][0:8, 0:128], sc[:, 8:16], identf[:]))
        CP(cwt1[:], pb[1][0:8, 0:128])
        LD1(dt["cw1"].ap()[0:8, tt * 128:(tt + 1) * 128], cwt1[:])
    reg.close()

    LDL([(dt["h3pre"].ap()[c * 128:(c + 1) * 128, :],
          dt["h2f"].ap()[c * 128:(c + 1) * 128, :]) for c in range(8)])
    emit_moe("h2b", HN, 8, 4096, "l1f_We1", None, "l1f_We2", "cw1", "h3pre",
             256, "m1", relu_bias="l1f_be1", be2_name="l1f_be2")

    emit_ln("h3pre", HN, "h3b", gb_idx=2)

    reg = contextlib.ExitStack()
    wrego = ctx.sbuf("wrego", [128, 8 * 1024], BF16, reg)
    emit_proj("h3b", D, "l1f_Wout", D, HN, wrego, res="res3", res_out="y")
    reg.close()

    for nm in DUMPS:
        shape, dty = dsh[nm]
        du = nc.dram_tensor("dump_" + nm, list(shape), dty, kind="ExternalOutput")
        R = shape[0]
        step = 128 if R >= 128 else R
        LDL([(du.ap()[r0:r0 + step, :], dt[nm].ap()[r0:r0 + step, :])
             for r0 in range(0, R, step)])
    ctx.fence("sync")
    ctx.emit(block)
    return nc


_NC_CACHE = []


def kernel(**inp):
    maps = prep_in_maps(inp)
    if not _NC_CACHE:
        _NC_CACHE.append(build_nc())
    nc = _NC_CACHE[0]
    res = run_bass_kernel_spmd(nc, maps, core_ids=list(range(8)))
    out = np.zeros((B, S, D), np.float32)
    for c in range(8):
        b, h = divmod(c, 2)
        y = np.asarray(res.results[c]["y"], np.float32)
        out[b, h * HN:(h + 1) * HN, :] = y.T
    return out
